# version 03
# baseline (speedup 1.0000x reference)
import numpy as np
from contextlib import ExitStack

import concourse.bass as bass
import concourse.mybir as mybir
from concourse.bass_utils import run_bass_kernel_spmd

F32 = mybir.dt.float32
BF16 = mybir.dt.bfloat16
I8 = mybir.dt.int8
ALU = mybir.AluOpType
AF = mybir.ActivationFunctionType
AX = mybir.AxisListType

C = 1024
T = 4096
H = 4096
NCH = C // 128
NHC = H // 128
TW = 512
NT = T // TW
HW = TW // 2
EPS = 1e-5

ENGINES = ("sync", "gpsimd", "tensor", "vector", "scalar")
CI_INVC, CI_EPS, CI_NEG1, CI_127, CI_INV127, CI_LN127, CI_NLN127, CI_ONE = range(8)


class Plan:

    def __init__(self):
        self.items = {e: [] for e in ENGINES}
        self.cnt = {}
        self.seen = {e: {} for e in ENGINES}

    def sem(self, name):
        self.cnt[name] = 0

    def now(self, s):
        return (s, self.cnt[s])

    def emit(self, eng, build, wait=(), inc=None, inc_k=1):
        waits = []
        for w_ in wait:
            s, v = w_ if isinstance(w_, tuple) else (w_, self.cnt[w_])
            if v <= 0 or self.seen[eng].get(s, -1) >= v:
                continue
            self.seen[eng][s] = v
            waits.append((s, v))
        if inc is not None:
            self.cnt[inc] += inc_k
        self.items[eng].append((waits, build, inc, inc_k))
        return (inc, self.cnt[inc]) if inc is not None else None

    def run(self, e, eng_name, sems):
        for waits, build, inc, inc_k in self.items[eng_name]:
            for s, v in waits:
                e.wait_ge(sems[s], v)
            ins = build(e) if build is not None else None
            if inc is not None:
                ins.then_inc(sems[inc], inc_k)


def host_prep(inputs):
    d = inputs
    x = np.asarray(d["L0_attn_x"], dtype=np.float32)
    B = x.shape[0]

    def vec8(v):
        return np.ascontiguousarray(np.asarray(v, np.float32).reshape(NCH, 128).T)

    def wproj(w):
        a = np.asarray(w, np.float32).T.reshape(NCH, 128, NCH, 128)
        return np.ascontiguousarray(a.transpose(2, 1, 0, 3).reshape(C, C))

    def wfk_prep(w):
        a = np.asarray(w, np.float32).T.reshape(NCH, 128, NHC, 128)
        return np.ascontiguousarray(a.transpose(2, 1, 0, 3).reshape(H, C))

    def wfv_prep(w):
        a = np.asarray(w, np.float32).T.reshape(NHC, 128, NCH, 128)
        return np.ascontiguousarray(a.transpose(2, 1, 0, 3).reshape(C, H))

    vecs = np.concatenate([
        vec8(d["L0_attn_ln0_w"]), vec8(d["L0_attn_ln0_b"]),
        vec8(d["L0_attn_ln1_w"]), vec8(d["L0_attn_ln1_b"]),
        vec8(d["L0_attn_ln2_w"]), vec8(d["L0_attn_ln2_b"]),
        vec8(d["L0_attn_time_decay"]), vec8(d["L0_attn_time_first"]),
        vec8(d["L0_attn_tmk"]), vec8(d["L0_attn_tmv"]), vec8(d["L0_attn_tmr"]),
        vec8(d["L0_attn_cmk"]), vec8(d["L0_attn_cmr"]),
    ], axis=1)

    csts = np.zeros((128, 8), np.float32)
    csts[:, CI_INVC] = 1.0 / C
    csts[:, CI_EPS] = EPS
    csts[:, CI_NEG1] = -1.0
    csts[:, CI_127] = 127.0
    csts[:, CI_INV127] = 1.0 / 127.0
    csts[:, CI_LN127] = np.log(127.0)
    csts[:, CI_NLN127] = -np.log(127.0)
    csts[:, CI_ONE] = 1.0

    sel8 = np.zeros((8, 8 * 128), np.float32)
    for j in range(8):
        sel8[j, j * 128:(j + 1) * 128] = 1.0

    shared = {
        "w_k": wproj(d["L0_attn_Wk"]), "w_v": wproj(d["L0_attn_Wv"]),
        "w_r": wproj(d["L0_attn_Wr"]), "w_o": wproj(d["L0_attn_Wo"]),
        "w_fr": wproj(d["L0_attn_Wfr"]),
        "w_fk": wfk_prep(d["L0_attn_Wfk"]), "w_fv": wfv_prep(d["L0_attn_Wfv"]),
        "rhsT": np.ascontiguousarray(np.asarray(d["L0_ffn_rhs"], np.float32).T),
        "vecs": vecs, "csts": csts,
        "ones": np.ones((128, 128), np.float32),
        "ident": np.eye(128, dtype=np.float32),
        "sel8": sel8,
    }
    per_core = []
    for b in range(B):
        m = dict(shared)
        m["xT"] = np.ascontiguousarray(x[b].T)
        per_core.append(m)
    return per_core


def build_program(dbg=None):
    nc = bass.Bass("TRN2", target_bir_lowering=False)
    ctx = ExitStack()

    xT = nc.dram_tensor("xT", [C, T], F32, kind="ExternalInput")
    outT = nc.dram_tensor("outT", [C, T], F32, kind="ExternalOutput")
    dbgF = nc.dram_tensor("dbgF", [C, T], F32, kind="ExternalOutput")
    dbgB = nc.dram_tensor("dbgB", [C, T], BF16, kind="ExternalOutput")
    din = {}
    for n, shp in [("w_k", [C, C]), ("w_v", [C, C]), ("w_r", [C, C]),
                   ("w_o", [C, C]), ("w_fr", [C, C]), ("w_fk", [H, C]),
                   ("w_fv", [C, H]), ("rhsT", [C, C]),
                   ("vecs", [128, 13 * 8]), ("csts", [128, 8]),
                   ("ones", [128, 128]), ("ident", [128, 128]),
                   ("sel8", [8, 8 * 128])]:
        din[n] = nc.dram_tensor(n, shp, F32, kind="ExternalInput")
    scr = {}
    for n, shp in [("s_k", [C, C]), ("s_v", [C, C]), ("s_r", [C, C]),
                   ("s_o", [C, C]), ("s_fr", [C, C]), ("s_fk", [H, C]),
                   ("s_fv", [C, H]), ("s_q", [C, C])]:
        scr[n] = nc.dram_tensor(n, shp, BF16, kind="Internal")

    def sb(name, shape, dtype=F32):
        return ctx.enter_context(nc.sbuf_tensor(name, shape, dtype))

    x_in = sb("x_in", [128, NCH * TW])
    xres = sb("xres", [128, NCH * TW])
    h1b = [sb(f"h1_{j}", [128, TW + 1], BF16) for j in range(NCH)]
    h2b = [sb(f"h2_{j}", [128, TW + 1], BF16) for j in range(NCH)]
    xkb = sb("xkb", [128, NCH * TW], BF16)
    xvb = sb("xvb", [128, NCH * TW], BF16)
    xrb = sb("xrb", [128, NCH * TW], BF16)
    sig2 = sb("sig2", [128, NCH * TW], BF16)
    ekb = [sb(f"ek_{j}", [128, TW + 1]) for j in range(NCH)]
    zb = [sb(f"z_{j}", [128, TW + 1]) for j in range(NCH)]
    PA = sb("PA", [128, NCH * TW])
    PB = sb("PB", [128, NCH * TW])
    kf = sb("kf", [128, NHC * HW], BF16)
    pool = [sb(f"pool{i}", [128, TW]) for i in range(6)]
    dl = sb("dl", [128, TW], BF16)
    vcs = sb("vcs", [128, 13 * 8])
    cst = sb("cst", [128, 8])
    dvec = sb("dvec", [128, 8])
    euv = sb("euv", [128, 8])
    srhs = sb("srhs", [128, 8])
    rrv = sb("rrv", [128, 8])
    hist = sb("hist", [128, 4 * NCH])
    am4 = sb("am4", [128, 8])
    amaxR = sb("amaxR", [128, 8])
    prT = sb("prT", [8, 128])
    onesb = sb("onesb", [128, 128], BF16)
    idf = sb("idf", [128, 128])
    idb = sb("idb", [128, 128], BF16)
    self8 = sb("self8", [8, 8 * 128])
    qi8 = sb("qi8", [128, 1024], I8)
    pw = [sb(f"pw{i}", [128, NCH * 128], BF16) for i in range(2)]
    fkw = [sb(f"fkw{i}", [128, NCH * 128], BF16) for i in range(2)]
    fvw = [sb(f"fvw{i}", [128, NHC * 128], BF16) for i in range(2)]
    def alias(name, shape, dtype, backing):
        h = nc.virtual_sbuf_tensor(name, shape, dtype)
        nc._tensor(h.name, list(h.shape), h.dtype, type="SB")
        mv = nc.lookup_mloc(h)
        mv.allocated = True
        mv.addr = nc.lookup_mloc(backing).addr
        return h

    pcf = [alias("pcf0", [128, 2048], F32, kf), alias("pcf1", [128, 2048], F32, xkb)]
    pcb = [alias("pcb0", [128, 2048], BF16, sig2), alias("pcb1", [128, 2048], BF16, xrb)]
    pi8 = alias("pi8", [128, 1024], I8, qi8)

    pb = [ctx.enter_context(nc.psum_tensor(f"pb{i}", [128, 512], F32)) for i in range(6)]
    pT = ctx.enter_context(nc.psum_tensor("pT", [128, 1024], F32))

    P = Plan()
    semnames = ["sx", "sw", "gw", "so", "t", "v", "a",
                "pw0", "pw1", "fk0", "fk1", "fv0", "fv1", "cl0", "cl1", "cg0", "cg1"]
    for s in semnames:
        P.sem(s)
    sems = {s: ctx.enter_context(nc.semaphore(s)) for s in semnames}

    def xs(buf, j, w=TW):
        return buf[:, j * w:(j + 1) * w]

    VC = {n: i for i, n in enumerate(
        ["ln0_w", "ln0_b", "ln1_w", "ln1_b", "ln2_w", "ln2_b",
         "td", "tf", "tmk", "tmv", "tmr", "cmk", "cmr"])}

    def vap(name, j):
        return vcs[:, VC[name] * 8 + j:VC[name] * 8 + j + 1]

    def cap(i):
        return cst[:, i:i + 1]

    P.emit("sync", lambda e: e.dma_start(vcs[:, :], din["vecs"][:, :]), inc="sx", inc_k=16)
    P.emit("sync", lambda e: e.dma_start(cst[:, :], din["csts"][:, :]), inc="sx", inc_k=16)
    P.emit("sync", lambda e: e.dma_start(idf[:, :], din["ident"][:, :]), inc="sx", inc_k=16)
    P.emit("sync", lambda e: e.dma_start(pcf[0][:, 0:128], din["ones"][:, :]), inc="sx", inc_k=16)
    P.emit("sync", lambda e: e.dma_start(self8[:, :], din["sel8"][:, :]), inc="sx", inc_k=16)
    P.emit("vector", lambda e: e.tensor_copy(onesb[:, :], pcf[0][:, 0:128]), wait=["sx"], inc="v")
    P.emit("vector", lambda e: e.tensor_copy(idb[:, :], idf[:, :]), inc="v")
    P.emit("scalar", lambda e: e.activation(am4[:, :], vcs[:, VC["td"] * 8:VC["td"] * 8 + 8], AF.Exp, bias=0.0, scale=1.0), wait=["sx"], inc="a")
    P.emit("scalar", lambda e: e.activation(dvec[:, :], am4[:, :], AF.Exp, bias=0.0, scale=-1.0), inc="a")
    P.emit("scalar", lambda e: e.activation(euv[:, :], vcs[:, VC["tf"] * 8:VC["tf"] * 8 + 8], AF.Exp, bias=0.0, scale=1.0), inc="a")
    for j in range(NCH):
        P.emit("vector", lambda e, j=j: e.memset(zb[j][:, 0:1], 0.0), inc="v")
        P.emit("vector", lambda e, j=j: e.memset(ekb[j][:, 0:1], 0.0), inc="v")
        P.emit("vector", lambda e, j=j: e.memset(h1b[j][:, 0:1], 0.0), inc="v")
        P.emit("vector", lambda e, j=j: e.memset(h2b[j][:, 0:1], 0.0), inc="v")
    P.emit("vector", lambda e: e.memset(hist[:, :], 0.0), inc="v")

    cast_jobs = []
    for n_in, n_scr, rows, cols in [
        ("w_k", "s_k", C, C), ("w_v", "s_v", C, C), ("w_r", "s_r", C, C),
        ("w_o", "s_o", C, C), ("w_fr", "s_fr", C, C),
        ("w_fk", "s_fk", H, C), ("w_fv", "s_fv", C, H),
    ]:
        for r0 in range(0, rows, 128):
            for c0 in range(0, cols, 2048):
                cast_jobs.append((n_in, n_scr, r0, c0, min(2048, cols - c0)))
    store_marks = []
    for par, (n_in, n_scr, r0, c0, cw) in enumerate(cast_jobs):
        b = par % 2
        waits = [P.now("v"), P.now("a")] if par < 2 else [store_marks[par - 2]]
        ld_mk = P.emit("sync", lambda e, n_in=n_in, r0=r0, c0=c0, cw=cw, b=b:
                       e.dma_start(pcf[b][:, 0:cw], din[n_in][r0:r0 + 128, c0:c0 + cw]),
                       wait=waits, inc=f"cl{b}", inc_k=16)
        P.emit("scalar", lambda e, b=b, cw=cw: e.copy(pcb[b][:, 0:cw], pcf[b][:, 0:cw]),
               wait=[ld_mk], inc="a")
        mk = P.emit("gpsimd", lambda e, n_scr=n_scr, r0=r0, c0=c0, cw=cw, b=b:
                    e.dma_start(scr[n_scr][r0:r0 + 128, c0:c0 + cw], pcb[b][:, 0:cw]),
                    wait=["a"], inc=f"cg{b}", inc_k=16)
        store_marks.append(mk)

    red_marks = {}
    for j in range(NCH):
        b = j % 2
        lw = [P.now("a")] if j < 2 else [red_marks[j - 2]]
        ld_mk = P.emit("sync", lambda e, j=j, b=b: e.dma_start(pcf[b][:, 0:1024], din["rhsT"][j * 128:(j + 1) * 128, :]),
               wait=lw, inc=f"cl{b}", inc_k=16)
        red_marks[j] = P.emit("vector", lambda e, j=j, b=b: e.tensor_reduce(amaxR[:, j:j + 1], pcf[b][:, 0:1024], AX.X, ALU.max, apply_absolute_value=True),
               wait=[ld_mk], inc="v")
    P.emit("scalar", lambda e: e.activation(pcf[0][:, 0:8], amaxR[:, :], AF.Ln, bias=0.0, scale=1.0),
           wait=["v", "sx"], inc="a")
    P.emit("scalar", lambda e: e.activation(rrv[:, :], pcf[0][:, 0:8], AF.Exp, bias=cap(CI_LN127), scale=-1.0), inc="a")
    P.emit("scalar", lambda e: e.activation(srhs[:, :], pcf[0][:, 0:8], AF.Exp, bias=cap(CI_NLN127), scale=1.0), inc="a")
    P.emit("vector", lambda e: None, wait=["a"])
    for j in range(NCH):
        BW = ["v", "a", "t", P.now("gw"), P.now("sw"), P.now("sx")]
        ld2 = P.emit("sync", lambda e, j=j: e.dma_start(pcf[0][:, 0:1024], din["rhsT"][j * 128:(j + 1) * 128, :]),
               wait=list(BW), inc="cl0", inc_k=16)
        P.emit("vector", lambda e, j=j: e.tensor_scalar(pi8[:, 0:1024], pcf[0][:, 0:1024], rrv[:, j:j + 1], None, ALU.mult),
               wait=[ld2, "a", "t", P.now("gw")], inc="v")
        P.emit("vector", lambda e: e.tensor_copy(pcf[1][:, 0:1024], pi8[:, 0:1024]),
               wait=[P.now("gw"), P.now("t")], inc="v")
        for m in range(NCH):
            P.emit("tensor", lambda e, m=m: e.transpose(pT[:, m * 128:(m + 1) * 128], pcf[1][:, m * 128:(m + 1) * 128], idf[:, :]),
                   wait=(["v", P.now("a"), P.now("gw")] if m == 0 else []), inc=("t" if m == NCH - 1 else None))
        P.emit("scalar", lambda e: e.copy(pcb[1][:, 0:1024], pT[:, 0:1024]),
               wait=["t", "v", P.now("gw")], inc="a")
        for m in range(NCH):
            P.emit("gpsimd", lambda e, j=j, m=m: e.dma_start(scr["s_q"][j * 128:(j + 1) * 128, m * 128:(m + 1) * 128], pcb[1][:, m * 128:(m + 1) * 128]),
                   wait=["a", "t", "v"], inc="gw", inc_k=16)
    setup_cg = (P.cnt["cg0"], P.cnt["cg1"])
    if dbg == "qrhs":
        for mm_ in range(NCH):
            P.emit("sync", lambda e, mm_=mm_: e.dma_start(pw[0][:, 0:1024], scr["s_q"][mm_ * 128:(mm_ + 1) * 128, :]),
                   wait=[P.now("gw"), ("cg0", P.cnt["cg0"]), ("cg1", P.cnt["cg1"]), P.now("so")], inc="sw", inc_k=16)
            P.emit("sync", lambda e, mm_=mm_: e.dma_start(dbgB[mm_ * 128:(mm_ + 1) * 128, 0:1024], pw[0][:, 0:1024]),
                   wait=["sw"], inc="so", inc_k=16)
        P.emit("sync", lambda e: e.dma_start(dbgF[0:128, 0:8], srhs[:, :]), wait=[P.now("a")], inc="so", inc_k=16)
        P.emit("sync", lambda e: e.dma_start(dbgF[0:128, 8:16], rrv[:, :]), inc="so", inc_k=16)
        P.emit("sync", lambda e: e.dma_start(dbgF[0:128, 16:24], amaxR[:, :]), wait=[P.now("v")], inc="so", inc_k=16)
    setup_gw = P.cnt["gw"]

    SETUP_DONE = [("gw", None), ("cg0", None), ("cg1", None)]

    def setup_done_waits():
        return [("gw", setup_gw), ("cg0", setup_cg[0]), ("cg1", setup_cg[1])]

    def stream_w(scr_name, buf_pair, m, ncols, wait_extra, engine="sync", kind="pw"):
        b = m % 2
        semn = f"{kind}{b}"

        def go(e, scr_name=scr_name, m=m, ncols=ncols, b=b):
            return e.dma_start(buf_pair[b][:, 0:ncols], scr[scr_name][m * 128:(m + 1) * 128, 0:ncols])
        mk = P.emit(engine, go, wait=setup_done_waits() + list(wait_extra), inc=semn, inc_k=16)
        return buf_pair[b], mk

    def proj(scr_name, rhs_buf, banks, consumer):
        start_v = P.now("v")
        start_a = P.now("a")
        cons_marks = {}
        tmarks = {}
        for m in range(NCH):
            wx = [P.now("t")] if m < 2 else [tmarks[m - 2]]
            buf, rdy = stream_w(scr_name, pw, m, NCH * 128, wx)
            waits = [rdy, start_v, start_a]
            if m >= 2:
                waits.append(cons_marks[m - 2])
            bank = banks[m % 2]
            for j in range(NCH):
                tm = P.emit("tensor", lambda e, j=j, buf=buf, bank=bank:
                            e.matmul(bank[:, 0:TW], buf[:, j * 128:(j + 1) * 128], xs(rhs_buf, j),
                                     start=(j == 0), stop=(j == NCH - 1)),
                            wait=(waits if j == 0 else []), inc=("t" if j == NCH - 1 else None))
            tmarks[m] = tm
            cons_marks[m] = consumer(m, bank)

    def ln_stats(src):
        for j in range(NCH):
            P.emit("scalar", lambda e, j=j: e.copy(xs(xkb, j), xs(src, j)),
                   wait=[P.now("v"), P.now("t")], inc="a")
        for j in range(NCH):
            P.emit("vector", lambda e, j=j: e.tensor_tensor(xs(xvb, j), xs(xkb, j), xs(xkb, j), ALU.mult),
                   wait=["a", P.now("t")], inc="v")
        for j in range(NCH):
            P.emit("tensor", lambda e, j=j: e.matmul(pb[4][:, 0:TW], onesb[:, :], xs(xkb, j), start=(j == 0), stop=(j == NCH - 1)),
                   wait=(["a", P.now("v")] if j == 0 else []), inc=("t" if j == NCH - 1 else None))
        for j in range(NCH):
            P.emit("tensor", lambda e, j=j: e.matmul(pb[5][:, 0:TW], onesb[:, :], xs(xvb, j), start=(j == 0), stop=(j == NCH - 1)),
                   wait=(["v"] if j == 0 else []), inc=("t" if j == NCH - 1 else None))
        P.emit("vector", lambda e: e.tensor_scalar(pool[1][:, :], pb[4][:, 0:TW], cap(CI_INVC), None, ALU.mult),
               wait=["t"], inc="v")
        P.emit("vector", lambda e: e.tensor_scalar(pool[2][:, :], pb[5][:, 0:TW], cap(CI_INVC), cap(CI_EPS), ALU.mult, ALU.add),
               inc="v")
        P.emit("vector", lambda e: e.scalar_tensor_tensor(pool[3][:, :], pool[1][:, :], cap(CI_NEG1), pool[1][:, :], ALU.mult, ALU.mult),
               inc="v")
        P.emit("vector", lambda e: e.tensor_tensor(pool[2][:, :], pool[2][:, :], pool[3][:, :], ALU.add), inc="v")
        P.emit("scalar", lambda e: e.activation(pool[3][:, :], pool[2][:, :], AF.Ln, bias=0.0, scale=1.0),
               wait=["v"], inc="a")
        P.emit("scalar", lambda e: e.activation(pool[0][:, :], pool[3][:, :], AF.Exp, bias=0.0, scale=-0.5), inc="a")

    def ln_apply(src, dst_fn, wname=None, bname=None):
        for j in range(NCH):
            P.emit("vector", lambda e, j=j: e.tensor_tensor(pool[4][:, :], xs(src, j), pool[1][:, :], ALU.subtract),
                   wait=(["a"] if j == 0 else []), inc="v")
            if wname is None:
                P.emit("vector", lambda e, j=j: e.tensor_tensor(dst_fn(j), pool[4][:, :], pool[0][:, :], ALU.mult), inc="v")
            else:
                P.emit("vector", lambda e, j=j: e.scalar_tensor_tensor(pool[5][:, :], pool[4][:, :], vap(wname, j), pool[0][:, :], ALU.mult, ALU.mult),
                       inc="v")
                P.emit("vector", lambda e, j=j: e.tensor_scalar(dst_fn(j), pool[5][:, :], vap(bname, j), None, ALU.add), inc="v")

    def tap(name, t0, ap_fn, bf=False):
        if dbg != name:
            return
        dst = dbgB if bf else dbgF
        for j in range(NCH):
            P.emit("sync", lambda e, j=j, t0=t0: e.dma_start(dst[j * 128:(j + 1) * 128, t0:t0 + TW], ap_fn(j)),
                   wait=[P.now("v"), P.now("a"), P.now("t")], inc="so", inc_k=16)
        P.emit("vector", lambda e: None, wait=["so"])
        P.emit("scalar", lambda e: None, wait=["so"])
        P.emit("tensor", lambda e: None, wait=["so"])

    P.emit("vector", lambda e: None, wait=setup_done_waits() + [P.now("t"), P.now("a")])
    P.emit("scalar", lambda e: None, wait=setup_done_waits() + [P.now("t"), P.now("v")])
    P.emit("tensor", lambda e: None, wait=setup_done_waits())
    for it in range(NT):
        t0 = it * TW
        for j in range(NCH):
            P.emit("sync", lambda e, j=j, t0=t0: e.dma_start(xs(x_in, j), xT[j * 128:(j + 1) * 128, t0:t0 + TW]),
                   wait=[P.now("v")], inc="sx", inc_k=16)
        if it > 0:
            P.emit("scalar", lambda e: e.copy(h1b[0][:, 0:1], h1b[0][:, TW:TW + 1]),
                   wait=[P.now("v")], inc="a")
            for j in range(NCH):
                if j > 0:
                    P.emit("scalar", lambda e, j=j: e.copy(h1b[j][:, 0:1], h1b[j][:, TW:TW + 1]), inc="a")
                P.emit("scalar", lambda e, j=j: e.copy(h2b[j][:, 0:1], h2b[j][:, TW:TW + 1]), inc="a")
                P.emit("scalar", lambda e, j=j: e.copy(ekb[j][:, 0:1], ekb[j][:, TW:TW + 1]), inc="a")
            for j in range(NCH):
                P.emit("vector", lambda e, j=j: e.tensor_copy(zb[j][:, 0:1], zb[j][:, TW:TW + 1]),
                       wait=[P.now("so")], inc="v")
        P.emit("vector", lambda e: None, wait=["sx"])
        P.emit("scalar", lambda e: None, wait=["sx"])

        ln_stats(x_in)
        ln_apply(x_in, lambda j: xs(xres, j), "ln0_w", "ln0_b")
        tap("x0", t0, lambda j: xs(xres, j))
        ln_stats(xres)
        ln_apply(xres, lambda j: h1b[j][:, 1:TW + 1], "ln1_w", "ln1_b")

        tap("h1", t0, lambda j: h1b[j][:, 1:TW + 1], bf=True)
        for j in range(NCH):
            P.emit("vector", lambda e, j=j: e.tensor_tensor(dl[:, :], h1b[j][:, 1:TW + 1], h1b[j][:, 0:TW], ALU.subtract),
                   wait=(["a", P.now("t")] if j == 0 else ["a"]), inc="v")
            P.emit("vector", lambda e, j=j: e.scalar_tensor_tensor(xs(xkb, j), dl[:, :], vap("tmk", j), h1b[j][:, 0:TW], ALU.mult, ALU.add), inc="v")
            P.emit("vector", lambda e, j=j: e.scalar_tensor_tensor(xs(xvb, j), dl[:, :], vap("tmv", j), h1b[j][:, 0:TW], ALU.mult, ALU.add), inc="v")
            P.emit("vector", lambda e, j=j: e.scalar_tensor_tensor(xs(xrb, j), dl[:, :], vap("tmr", j), h1b[j][:, 0:TW], ALU.mult, ALU.add), inc="v")

        tap("xk", t0, lambda j: xs(xkb, j), bf=True)
        tap("xv", t0, lambda j: xs(xvb, j), bf=True)
        tap("xr", t0, lambda j: xs(xrb, j), bf=True)
        def k_cons(m, bank):
            return P.emit("scalar", lambda e, m=m, bank=bank: e.activation(ekb[m][:, 1:TW + 1], bank[:, 0:TW], AF.Exp, bias=0.0, scale=1.0),
                          wait=["t"], inc="a")
        proj("s_k", xkb, (pb[0], pb[1]), k_cons)

        def v_cons(m, bank):
            return P.emit("vector", lambda e, m=m, bank=bank: e.tensor_tensor(zb[m][:, 1:TW + 1], ekb[m][:, 1:TW + 1], bank[:, 0:TW], ALU.mult),
                          wait=["t", "a"], inc="v")
        proj("s_v", xvb, (pb[2], pb[3]), v_cons)

        tap("ek", t0, lambda j: ekb[j][:, 1:TW + 1])
        tap("z", t0, lambda j: zb[j][:, 1:TW + 1])
        scan_marks = []
        for j in range(NCH):
            P.emit("vector", lambda e, j=j: e.tensor_tensor_scan(xs(PA, j), bass.AP(dvec, j, [[8, 128], [0, TW]]), zb[j][:, 0:TW], hist[:, 4 * j:4 * j + 1], ALU.mult, ALU.add),
                   wait=(["a"] if j == 0 else []), inc="v")
            P.emit("vector", lambda e, j=j: e.tensor_tensor_scan(xs(PB, j), bass.AP(dvec, j, [[8, 128], [0, TW]]), ekb[j][:, 0:TW], hist[:, 4 * j + 1:4 * j + 2], ALU.mult, ALU.add),
                   inc="v")
            P.emit("vector", lambda e, j=j: e.tensor_copy(hist[:, 4 * j:4 * j + 1], PA[:, j * TW + TW - 1:j * TW + TW]), inc="v")
            mk = P.emit("vector", lambda e, j=j: e.tensor_copy(hist[:, 4 * j + 1:4 * j + 2], PB[:, j * TW + TW - 1:j * TW + TW]), inc="v")
            scan_marks.append(mk)

        tap("PA", t0, lambda j: xs(PA, j))
        tap("PB", t0, lambda j: xs(PB, j))
        def r_cons(m, bank):
            P.emit("vector", lambda e, m=m: e.scalar_tensor_tensor(pool[0][:, :], zb[m][:, 1:TW + 1], euv[:, m:m + 1], xs(PA, m), ALU.mult, ALU.add),
                   wait=[scan_marks[m], "a"], inc="v")
            P.emit("vector", lambda e, m=m: e.scalar_tensor_tensor(pool[1][:, :], ekb[m][:, 1:TW + 1], euv[:, m:m + 1], xs(PB, m), ALU.mult, ALU.add),
                   inc="v")
            P.emit("scalar", lambda e: e.activation(pool[2][:, :], pool[1][:, :], AF.Ln, bias=0.0, scale=1.0),
                   wait=["v"], inc="a")
            P.emit("scalar", lambda e: e.activation(pool[3][:, :], pool[2][:, :], AF.Exp, bias=0.0, scale=-1.0), inc="a")
            P.emit("scalar", lambda e, bank=bank: e.activation(pool[4][:, :], bank[:, 0:TW], AF.Exp, bias=0.0, scale=-1.0),
                   wait=["t"], inc="a")
            P.emit("scalar", lambda e: e.activation(pool[5][:, :], pool[4][:, :], AF.Ln, bias=1.0, scale=1.0), inc="a")
            P.emit("scalar", lambda e: e.activation(pool[4][:, :], pool[5][:, :], AF.Exp, bias=0.0, scale=-1.0), inc="a")
            P.emit("vector", lambda e: e.tensor_tensor(pool[0][:, :], pool[0][:, :], pool[3][:, :], ALU.mult),
                   wait=["a"], inc="v")
            return P.emit("vector", lambda e, m=m: e.tensor_tensor(xs(xkb, m), pool[4][:, :], pool[0][:, :], ALU.mult), inc="v")
        proj("s_r", xrb, (pb[4], pb[5]), r_cons)

        tap("rwkvin", t0, lambda j: xs(xkb, j), bf=True)
        def o_cons(m, bank):
            return P.emit("vector", lambda e, m=m, bank=bank: e.tensor_tensor(xs(xres, m), xs(xres, m), bank[:, 0:TW], ALU.add),
                          wait=["t"], inc="v")
        proj("s_o", xkb, (pb[0], pb[1]), o_cons)

        tap("x1", t0, lambda j: xs(xres, j))
        ln_stats(xres)
        ln_apply(xres, lambda j: h2b[j][:, 1:TW + 1], "ln2_w", "ln2_b")

        for j in range(NCH):
            P.emit("vector", lambda e, j=j: e.tensor_tensor(dl[:, :], h2b[j][:, 1:TW + 1], h2b[j][:, 0:TW], ALU.subtract),
                   wait=(["a", P.now("t")] if j == 0 else ["a"]), inc="v")
            P.emit("vector", lambda e, j=j: e.scalar_tensor_tensor(xs(xkb, j), dl[:, :], vap("cmk", j), h2b[j][:, 0:TW], ALU.mult, ALU.add), inc="v")
            P.emit("vector", lambda e, j=j: e.scalar_tensor_tensor(xs(xrb, j), dl[:, :], vap("cmr", j), h2b[j][:, 0:TW], ALU.mult, ALU.add), inc="v")

        def fr_cons(m, bank):
            P.emit("scalar", lambda e, bank=bank: e.activation(pool[2][:, :], bank[:, 0:TW], AF.Exp, bias=0.0, scale=-1.0),
                   wait=["t", P.now("v")], inc="a")
            P.emit("scalar", lambda e: e.activation(pool[2][:, :], pool[2][:, :], AF.Ln, bias=1.0, scale=1.0), inc="a")
            return P.emit("scalar", lambda e, m=m: e.activation(xs(sig2, m), pool[2][:, :], AF.Exp, bias=0.0, scale=-1.0), inc="a")
        proj("s_fr", xrb, (pb[2], pb[3]), fr_cons)

        for hw in range(2):
            col0 = hw * HW
            kftt_marks = {}
            fk_tmarks = {}
            for m in range(NHC):
                wx = [P.now("t"), P.now("v"), P.now("a")] if m < 2 else [fk_tmarks[m - 2], P.now("v")]
                buf, rdy = stream_w("s_fk", fkw, m, NCH * 128, wx, engine="gpsimd", kind="fk")
                bank = pb[m % 2]
                waits = [rdy, P.now("v"), P.now("a")]
                if m >= 2:
                    waits.append(kftt_marks[m - 2])
                for j in range(NCH):
                    tm = P.emit("tensor", lambda e, j=j, buf=buf, bank=bank, col0=col0:
                                e.matmul(bank[:, 0:HW], buf[:, j * 128:(j + 1) * 128], xkb[:, j * TW + col0:j * TW + col0 + HW],
                                         start=(j == 0), stop=(j == NCH - 1)),
                                wait=(waits if j == 0 else []), inc=("t" if j == NCH - 1 else None))
                fk_tmarks[m] = tm
                rslot = pool[2 + (m % 2)]
                P.emit("scalar", lambda e, bank=bank, rslot=rslot: e.activation(rslot[:, 0:HW], bank[:, 0:HW], AF.Relu, bias=0.0, scale=1.0),
                       wait=(["t", P.now("v")] if m < 2 else ["t", kftt_marks[m - 2], P.now("v")]), inc="a")
                kftt_marks[m] = P.emit("vector", lambda e, m=m, bank=bank, rslot=rslot:
                                       e.tensor_tensor(kf[:, m * HW:(m + 1) * HW], bank[:, 0:HW], rslot[:, 0:HW], ALU.mult),
                                       wait=["a", "t"], inc="v")
            kv_marks = {}
            fv_tmarks = {}
            for m in range(NCH):
                wx = [P.now("t"), P.now("v"), P.now("a")] if m < 2 else [fv_tmarks[m - 2], P.now("v")]
                buf, rdy = stream_w("s_fv", fvw, m, NHC * 128, wx, kind="fv")
                bank = pb[2 + (m % 2)]
                waits = [rdy, P.now("v"), P.now("a")]
                if m >= 2:
                    waits.append(kv_marks[m - 2])
                for o in range(NHC):
                    tm = P.emit("tensor", lambda e, o=o, buf=buf, bank=bank:
                                e.matmul(bank[:, 0:HW], buf[:, o * 128:(o + 1) * 128], kf[:, o * HW:(o + 1) * HW],
                                         start=(o == 0), stop=(o == NHC - 1)),
                                wait=(waits if o == 0 else []), inc=("t" if o == NHC - 1 else None))
                fv_tmarks[m] = tm
                tslot = pool[4 + (m % 2)]
                P.emit("vector", lambda e, m=m, bank=bank, col0=col0, tslot=tslot:
                       e.tensor_tensor(tslot[:, 0:HW], sig2[:, m * TW + col0:m * TW + col0 + HW], bank[:, 0:HW], ALU.mult),
                       wait=["t", "a"], inc="v")
                kv_marks[m] = P.emit("vector", lambda e, m=m, col0=col0, tslot=tslot:
                                     e.tensor_tensor(xres[:, m * TW + col0:m * TW + col0 + HW], xres[:, m * TW + col0:m * TW + col0 + HW], tslot[:, 0:HW], ALU.add),
                                     inc="v")

        tap("x2", t0, lambda j: xs(xres, j))
        tap("sig2", t0, lambda j: xs(sig2, j), bf=True)
        for j in range(NCH):
            P.emit("vector", lambda e, j=j: e.tensor_tensor(xs(PB, j), xs(x_in, j), xs(xres, j), ALU.add), inc="v")

        ln_stats(PB)
        ln_apply(PB, lambda j: xs(PA, j))

        tap("y1", t0, lambda j: xs(PA, j))
        for g in range(4):
            for j in range(NCH):
                P.emit("tensor", lambda e, g=g, j=j: e.transpose(pT[:, j * 128:(j + 1) * 128], PA[:, j * TW + g * 128:j * TW + (g + 1) * 128], idf[:, :]),
                       wait=([P.now("v")] if j == 0 else []), inc=("t" if j == NCH - 1 else None))
            P.emit("vector", lambda e, g=g: e.tensor_reduce(am4[:, g:g + 1], pT[:, 0:1024], AX.X, ALU.max, apply_absolute_value=True),
                   wait=["t"], inc="v")
        P.emit("scalar", lambda e: e.activation(amaxR[:, 0:4], am4[:, 0:4], AF.Ln, bias=0.0, scale=1.0),
               wait=["v"], inc="a")
        P.emit("scalar", lambda e: e.activation(am4[:, 4:8], amaxR[:, 0:4], AF.Exp, bias=cap(CI_LN127), scale=-1.0), inc="a")
        P.emit("scalar", lambda e: e.activation(am4[:, 0:4], amaxR[:, 0:4], AF.Exp, bias=cap(CI_NLN127), scale=1.0), inc="a")
        P.emit("vector", lambda e: None, wait=["a"])
        P.emit("tensor", lambda e: e.transpose(pT[0:8, 0:128], am4[:, 0:8], idf[:, :]), wait=["v", "a"], inc="t")
        P.emit("vector", lambda e: e.tensor_copy(prT[:, :], pT[0:8, 0:128]), wait=["t"], inc="v")
        for g in range(4):
            P.emit("tensor", lambda e, g=g: e.matmul(pT[:, g * 128:(g + 1) * 128], self8[:, (4 + g) * 128:(5 + g) * 128], prT[:, :], start=True, stop=True),
                   wait=(["v"] if g == 0 else []), inc=None)
            P.emit("tensor", lambda e, g=g: e.matmul(pT[:, TW + g * 128:TW + (g + 1) * 128], self8[:, g * 128:(g + 1) * 128], prT[:, :], start=True, stop=True),
                   inc=("t" if g == 3 else None))
        P.emit("vector", lambda e: e.tensor_copy(pool[0][:, :], pT[:, TW:2 * TW]), wait=["t"], inc="v")
        if dbg in ("rinvbc", "am4"):
            if dbg == "rinvbc":
                P.emit("vector", lambda e: e.tensor_copy(pool[2][:, :], pT[:, 0:TW]), wait=["t"], inc="v")
            else:
                P.emit("vector", lambda e: e.tensor_copy(pool[3][:, 0:8], am4[:, :]), inc="v")
                P.emit("vector", lambda e: e.tensor_copy(pool[3][:, 8:16], amaxR[:, 0:8]), inc="v")
                P.emit("vector", lambda e: e.tensor_copy(pool[3][:, 16:24], pool[2][:, 0:8]), inc="v")
            P.emit("sync", lambda e, t0=t0, _src=(2 if dbg == "rinvbc" else 3): e.dma_start(dbgF[0:128, t0:t0 + TW], pool[_src][:, :]),
                   wait=[P.now("v")], inc="so", inc_k=16)
            P.emit("vector", lambda e: None, wait=["so"])
        for j in range(NCH):
            P.emit("vector", lambda e, j=j: e.tensor_tensor(qi8[:, 0:TW], xs(PA, j), pT[:, 0:TW], ALU.mult),
                   wait=([P.now("t")] if j == 0 else []), inc="v")
            P.emit("vector", lambda e, j=j: e.tensor_copy(xs(xkb, j), qi8[:, 0:TW]), inc="v")

        tap("qbf", t0, lambda j: xs(xkb, j), bf=True)
        def ffn_cons(m, bank):
            P.emit("vector", lambda e, m=m, bank=bank: e.scalar_tensor_tensor(pool[3][:, :], bank[:, 0:TW], srhs[:, m:m + 1], pool[0][:, :], ALU.mult, ALU.mult),
                   wait=["t"], inc="v")
            return P.emit("vector", lambda e, m=m: e.tensor_tensor(xs(xres, m), xs(PA, m), pool[3][:, :], ALU.add), inc="v")
        proj("s_q", xkb, (pb[0], pb[1]), ffn_cons)

        tap("s2", t0, lambda j: xs(xres, j))
        ln_stats(xres)
        for j in range(NCH):
            P.emit("vector", lambda e, j=j: e.tensor_tensor(pool[4][:, :], xs(xres, j), pool[1][:, :], ALU.subtract),
                   wait=(["a"] if j == 0 else []), inc="v")
            P.emit("vector", lambda e, j=j: e.tensor_tensor(zb[j][:, 0:TW], pool[4][:, :], pool[0][:, :], ALU.mult), inc="v")
            P.emit("sync", lambda e, j=j, t0=t0: e.dma_start(outT[j * 128:(j + 1) * 128, t0:t0 + TW], zb[j][:, 0:TW]),
                   wait=["v"], inc="so", inc_k=16)

    P.emit("sync", lambda e: None, wait=["so"])

    with nc.Block() as block:
        @block.sync
        def _(e):
            P.run(e, "sync", sems)

        @block.gpsimd
        def _(e):
            P.run(e, "gpsimd", sems)

        @block.tensor
        def _(e):
            P.run(e, "tensor", sems)

        @block.vector
        def _(e):
            P.run(e, "vector", sems)

        @block.scalar
        def _(e):
            P.run(e, "scalar", sems)

    return nc, ctx


def kernel(**inputs):
    per_core = host_prep(inputs)
    nc, ctx = build_program()
    res = run_bass_kernel_spmd(nc, per_core, core_ids=list(range(8)))
    outs = [np.ascontiguousarray(r["outT"].T) for r in res.results]
    return np.stack(outs, axis=0).astype(np.float32)


# version 04
# speedup vs baseline: 1.0049x; 1.0049x over previous
import numpy as np
from contextlib import ExitStack

import concourse.bass as bass
import concourse.mybir as mybir
from concourse.bass_utils import run_bass_kernel_spmd

F32 = mybir.dt.float32
BF16 = mybir.dt.bfloat16
I8 = mybir.dt.int8
ALU = mybir.AluOpType
AF = mybir.ActivationFunctionType
AX = mybir.AxisListType

C = 1024
T = 4096
H = 4096
NCH = C // 128
NHC = H // 128
TW = 512
NT = T // TW
HW = TW // 2
EPS = 1e-5

ENGINES = ("sync", "gpsimd", "tensor", "vector", "scalar")
CI_INVC, CI_EPS, CI_NEG1, CI_127, CI_INV127, CI_LN127, CI_NLN127, CI_ONE = range(8)


class Plan:

    def __init__(self):
        self.items = {e: [] for e in ENGINES}
        self.cnt = {}
        self.seen = {e: {} for e in ENGINES}

    def sem(self, name):
        self.cnt[name] = 0

    def now(self, s):
        return (s, self.cnt[s])

    def emit(self, eng, build, wait=(), inc=None, inc_k=1):
        waits = []
        for w_ in wait:
            s, v = w_ if isinstance(w_, tuple) else (w_, self.cnt[w_])
            if v <= 0 or self.seen[eng].get(s, -1) >= v:
                continue
            self.seen[eng][s] = v
            waits.append((s, v))
        if inc is not None:
            self.cnt[inc] += inc_k
        self.items[eng].append((waits, build, inc, inc_k))
        return (inc, self.cnt[inc]) if inc is not None else None

    def run(self, e, eng_name, sems):
        for waits, build, inc, inc_k in self.items[eng_name]:
            for s, v in waits:
                e.wait_ge(sems[s], v)
            ins = build(e) if build is not None else None
            if inc is not None:
                ins.then_inc(sems[inc], inc_k)


def host_prep(inputs):
    d = inputs
    x = np.asarray(d["L0_attn_x"], dtype=np.float32)
    B = x.shape[0]

    def vec8(v):
        return np.ascontiguousarray(np.asarray(v, np.float32).reshape(NCH, 128).T)

    def wproj(w):
        a = np.asarray(w, np.float32).T.reshape(NCH, 128, NCH, 128)
        return np.ascontiguousarray(a.transpose(2, 1, 0, 3).reshape(C, C))

    def wfk_prep(w):
        a = np.asarray(w, np.float32).T.reshape(NCH, 128, NHC, 128)
        return np.ascontiguousarray(a.transpose(2, 1, 0, 3).reshape(H, C))

    def wfv_prep(w):
        a = np.asarray(w, np.float32).T.reshape(NHC, 128, NCH, 128)
        return np.ascontiguousarray(a.transpose(2, 1, 0, 3).reshape(C, H))

    vecs = np.concatenate([
        vec8(d["L0_attn_ln0_w"]), vec8(d["L0_attn_ln0_b"]),
        vec8(d["L0_attn_ln1_w"]), vec8(d["L0_attn_ln1_b"]),
        vec8(d["L0_attn_ln2_w"]), vec8(d["L0_attn_ln2_b"]),
        vec8(d["L0_attn_time_decay"]), vec8(d["L0_attn_time_first"]),
        vec8(d["L0_attn_tmk"]), vec8(d["L0_attn_tmv"]), vec8(d["L0_attn_tmr"]),
        vec8(d["L0_attn_cmk"]), vec8(d["L0_attn_cmr"]),
    ], axis=1)

    csts = np.zeros((128, 8), np.float32)
    csts[:, CI_INVC] = 1.0 / C
    csts[:, CI_EPS] = EPS
    csts[:, CI_NEG1] = -1.0
    csts[:, CI_127] = 127.0
    csts[:, CI_INV127] = 1.0 / 127.0
    csts[:, CI_LN127] = np.log(127.0)
    csts[:, CI_NLN127] = -np.log(127.0)
    csts[:, CI_ONE] = 1.0

    sel8 = np.zeros((8, 8 * 128), np.float32)
    for j in range(8):
        sel8[j, j * 128:(j + 1) * 128] = 1.0

    shared = {
        "w_k": wproj(d["L0_attn_Wk"]), "w_v": wproj(d["L0_attn_Wv"]),
        "w_r": wproj(d["L0_attn_Wr"]), "w_o": wproj(d["L0_attn_Wo"]),
        "w_fr": wproj(d["L0_attn_Wfr"]),
        "w_fk": wfk_prep(d["L0_attn_Wfk"]), "w_fv": wfv_prep(d["L0_attn_Wfv"]),
        "rhsT": np.ascontiguousarray(np.asarray(d["L0_ffn_rhs"], np.float32).T),
        "vecs": vecs, "csts": csts,
        "ones": np.ones((128, 128), np.float32),
        "ident": np.eye(128, dtype=np.float32),
        "sel8": sel8,
    }
    per_core = []
    for b in range(B):
        m = dict(shared)
        m["xT"] = np.ascontiguousarray(x[b].T)
        per_core.append(m)
    return per_core


def build_program(dbg=None):
    nc = bass.Bass("TRN2", target_bir_lowering=False, detect_race_conditions=False)
    ctx = ExitStack()

    xT = nc.dram_tensor("xT", [C, T], F32, kind="ExternalInput")
    outT = nc.dram_tensor("outT", [C, T], F32, kind="ExternalOutput")
    dbgF = nc.dram_tensor("dbgF", [C, T], F32, kind="ExternalOutput")
    dbgB = nc.dram_tensor("dbgB", [C, T], BF16, kind="ExternalOutput")
    din = {}
    for n, shp in [("w_k", [C, C]), ("w_v", [C, C]), ("w_r", [C, C]),
                   ("w_o", [C, C]), ("w_fr", [C, C]), ("w_fk", [H, C]),
                   ("w_fv", [C, H]), ("rhsT", [C, C]),
                   ("vecs", [128, 13 * 8]), ("csts", [128, 8]),
                   ("ones", [128, 128]), ("ident", [128, 128]),
                   ("sel8", [8, 8 * 128])]:
        din[n] = nc.dram_tensor(n, shp, F32, kind="ExternalInput")
    scr = {}
    for n, shp in [("s_k", [C, C]), ("s_v", [C, C]), ("s_r", [C, C]),
                   ("s_o", [C, C]), ("s_fr", [C, C]), ("s_fk", [H, C]),
                   ("s_fv", [C, H]), ("s_q", [C, C])]:
        scr[n] = nc.dram_tensor(n, shp, BF16, kind="Internal")

    def sb(name, shape, dtype=F32):
        return ctx.enter_context(nc.sbuf_tensor(name, shape, dtype))

    x_in = sb("x_in", [128, NCH * TW])
    xres = sb("xres", [128, NCH * TW])
    h1b = [sb(f"h1_{j}", [128, TW + 1], BF16) for j in range(NCH)]
    h2b = [sb(f"h2_{j}", [128, TW + 1], BF16) for j in range(NCH)]
    xkb = sb("xkb", [128, NCH * TW], BF16)
    xvb = sb("xvb", [128, NCH * TW], BF16)
    xrb = sb("xrb", [128, NCH * TW], BF16)
    sig2 = sb("sig2", [128, NCH * TW], BF16)
    ekb = [sb(f"ek_{j}", [128, TW + 1]) for j in range(NCH)]
    zb = [sb(f"z_{j}", [128, TW + 1]) for j in range(NCH)]
    PA = sb("PA", [128, NCH * TW])
    PB = sb("PB", [128, NCH * TW])
    kf = sb("kf", [128, NHC * HW], BF16)
    pool = [sb(f"pool{i}", [128, TW]) for i in range(6)]
    dl = sb("dl", [128, TW], BF16)
    vcs = sb("vcs", [128, 13 * 8])
    cst = sb("cst", [128, 8])
    dvec = sb("dvec", [128, 8])
    euv = sb("euv", [128, 8])
    srhs = sb("srhs", [128, 8])
    rrv = sb("rrv", [128, 8])
    hist = sb("hist", [128, 4 * NCH])
    am4 = sb("am4", [128, 8])
    amaxR = sb("amaxR", [128, 8])
    prT = sb("prT", [8, 128])
    onesb = sb("onesb", [128, 128], BF16)
    idf = sb("idf", [128, 128])
    idb = sb("idb", [128, 128], BF16)
    self8 = sb("self8", [8, 8 * 128])
    qi8 = sb("qi8", [128, 1024], I8)
    pw = [sb(f"pw{i}", [128, NCH * 128], BF16) for i in range(2)]
    fkw = [sb(f"fkw{i}", [128, NCH * 128], BF16) for i in range(2)]
    fvw = [sb(f"fvw{i}", [128, NHC * 128], BF16) for i in range(2)]
    def alias(name, shape, dtype, backing):
        h = nc.virtual_sbuf_tensor(name, shape, dtype)
        nc._tensor(h.name, list(h.shape), h.dtype, type="SB")
        mv = nc.lookup_mloc(h)
        mv.allocated = True
        mv.addr = nc.lookup_mloc(backing).addr
        return h

    pcf = [alias("pcf0", [128, 2048], F32, kf), alias("pcf1", [128, 2048], F32, xkb)]
    pcb = [alias("pcb0", [128, 2048], BF16, sig2), alias("pcb1", [128, 2048], BF16, xrb)]
    pi8 = alias("pi8", [128, 1024], I8, qi8)

    pb = [ctx.enter_context(nc.psum_tensor(f"pb{i}", [128, 512], F32)) for i in range(6)]
    pT = ctx.enter_context(nc.psum_tensor("pT", [128, 1024], F32))

    P = Plan()
    semnames = ["sx", "sw", "gw", "so", "t", "v", "a",
                "pw0", "pw1", "fk0", "fk1", "fv0", "fv1", "cl0", "cl1", "cg0", "cg1"]
    for s in semnames:
        P.sem(s)
    sems = {s: ctx.enter_context(nc.semaphore(s)) for s in semnames}

    def xs(buf, j, w=TW):
        return buf[:, j * w:(j + 1) * w]

    VC = {n: i for i, n in enumerate(
        ["ln0_w", "ln0_b", "ln1_w", "ln1_b", "ln2_w", "ln2_b",
         "td", "tf", "tmk", "tmv", "tmr", "cmk", "cmr"])}

    def vap(name, j):
        return vcs[:, VC[name] * 8 + j:VC[name] * 8 + j + 1]

    def cap(i):
        return cst[:, i:i + 1]

    P.emit("sync", lambda e: e.dma_start(vcs[:, :], din["vecs"][:, :]), inc="sx", inc_k=16)
    P.emit("sync", lambda e: e.dma_start(cst[:, :], din["csts"][:, :]), inc="sx", inc_k=16)
    P.emit("sync", lambda e: e.dma_start(idf[:, :], din["ident"][:, :]), inc="sx", inc_k=16)
    P.emit("sync", lambda e: e.dma_start(pcf[0][:, 0:128], din["ones"][:, :]), inc="sx", inc_k=16)
    P.emit("sync", lambda e: e.dma_start(self8[:, :], din["sel8"][:, :]), inc="sx", inc_k=16)
    P.emit("vector", lambda e: e.tensor_copy(onesb[:, :], pcf[0][:, 0:128]), wait=["sx"], inc="v")
    P.emit("vector", lambda e: e.tensor_copy(idb[:, :], idf[:, :]), inc="v")
    P.emit("scalar", lambda e: e.activation(am4[:, :], vcs[:, VC["td"] * 8:VC["td"] * 8 + 8], AF.Exp, bias=0.0, scale=1.0), wait=["sx"], inc="a")
    P.emit("scalar", lambda e: e.activation(dvec[:, :], am4[:, :], AF.Exp, bias=0.0, scale=-1.0), inc="a")
    P.emit("scalar", lambda e: e.activation(euv[:, :], vcs[:, VC["tf"] * 8:VC["tf"] * 8 + 8], AF.Exp, bias=0.0, scale=1.0), inc="a")
    for j in range(NCH):
        P.emit("vector", lambda e, j=j: e.memset(zb[j][:, 0:1], 0.0), inc="v")
        P.emit("vector", lambda e, j=j: e.memset(ekb[j][:, 0:1], 0.0), inc="v")
        P.emit("vector", lambda e, j=j: e.memset(h1b[j][:, 0:1], 0.0), inc="v")
        P.emit("vector", lambda e, j=j: e.memset(h2b[j][:, 0:1], 0.0), inc="v")
    P.emit("vector", lambda e: e.memset(hist[:, :], 0.0), inc="v")

    cast_jobs = []
    for n_in, n_scr, rows, cols in [
        ("w_k", "s_k", C, C), ("w_v", "s_v", C, C), ("w_r", "s_r", C, C),
        ("w_o", "s_o", C, C), ("w_fr", "s_fr", C, C),
        ("w_fk", "s_fk", H, C), ("w_fv", "s_fv", C, H),
    ]:
        for r0 in range(0, rows, 128):
            for c0 in range(0, cols, 2048):
                cast_jobs.append((n_in, n_scr, r0, c0, min(2048, cols - c0)))
    store_marks = []
    for par, (n_in, n_scr, r0, c0, cw) in enumerate(cast_jobs):
        b = par % 2
        waits = [P.now("v"), P.now("a")] if par < 2 else [store_marks[par - 2]]
        ld_mk = P.emit("sync", lambda e, n_in=n_in, r0=r0, c0=c0, cw=cw, b=b:
                       e.dma_start(pcf[b][:, 0:cw], din[n_in][r0:r0 + 128, c0:c0 + cw]),
                       wait=waits, inc=f"cl{b}", inc_k=16)
        P.emit("scalar", lambda e, b=b, cw=cw: e.copy(pcb[b][:, 0:cw], pcf[b][:, 0:cw]),
               wait=[ld_mk], inc="a")
        mk = P.emit("gpsimd", lambda e, n_scr=n_scr, r0=r0, c0=c0, cw=cw, b=b:
                    e.dma_start(scr[n_scr][r0:r0 + 128, c0:c0 + cw], pcb[b][:, 0:cw]),
                    wait=["a"], inc=f"cg{b}", inc_k=16)
        store_marks.append(mk)

    red_marks = {}
    for j in range(NCH):
        b = j % 2
        lw = [P.now("a")] if j < 2 else [red_marks[j - 2]]
        ld_mk = P.emit("sync", lambda e, j=j, b=b: e.dma_start(pcf[b][:, 0:1024], din["rhsT"][j * 128:(j + 1) * 128, :]),
               wait=lw, inc=f"cl{b}", inc_k=16)
        red_marks[j] = P.emit("vector", lambda e, j=j, b=b: e.tensor_reduce(amaxR[:, j:j + 1], pcf[b][:, 0:1024], AX.X, ALU.max, apply_absolute_value=True),
               wait=[ld_mk], inc="v")
    P.emit("scalar", lambda e: e.activation(pcf[0][:, 0:8], amaxR[:, :], AF.Ln, bias=0.0, scale=1.0),
           wait=["v", "sx"], inc="a")
    P.emit("scalar", lambda e: e.activation(rrv[:, :], pcf[0][:, 0:8], AF.Exp, bias=cap(CI_LN127), scale=-1.0), inc="a")
    P.emit("scalar", lambda e: e.activation(srhs[:, :], pcf[0][:, 0:8], AF.Exp, bias=cap(CI_NLN127), scale=1.0), inc="a")
    P.emit("vector", lambda e: None, wait=["a"])
    for j in range(NCH):
        BW = ["v", "a", "t", P.now("gw"), P.now("sw"), P.now("sx")]
        ld2 = P.emit("sync", lambda e, j=j: e.dma_start(pcf[0][:, 0:1024], din["rhsT"][j * 128:(j + 1) * 128, :]),
               wait=list(BW), inc="cl0", inc_k=16)
        P.emit("vector", lambda e, j=j: e.tensor_scalar(pi8[:, 0:1024], pcf[0][:, 0:1024], rrv[:, j:j + 1], None, ALU.mult),
               wait=[ld2, "a", "t", P.now("gw")], inc="v")
        P.emit("vector", lambda e: e.tensor_copy(pcf[1][:, 0:1024], pi8[:, 0:1024]),
               wait=[P.now("gw"), P.now("t")], inc="v")
        for m in range(NCH):
            P.emit("tensor", lambda e, m=m: e.transpose(pT[:, m * 128:(m + 1) * 128], pcf[1][:, m * 128:(m + 1) * 128], idf[:, :]),
                   wait=(["v", P.now("a"), P.now("gw")] if m == 0 else []), inc=("t" if m == NCH - 1 else None))
        P.emit("scalar", lambda e: e.copy(pcb[1][:, 0:1024], pT[:, 0:1024]),
               wait=["t", "v", P.now("gw")], inc="a")
        for m in range(NCH):
            P.emit("gpsimd", lambda e, j=j, m=m: e.dma_start(scr["s_q"][j * 128:(j + 1) * 128, m * 128:(m + 1) * 128], pcb[1][:, m * 128:(m + 1) * 128]),
                   wait=["a", "t", "v"], inc="gw", inc_k=16)
    setup_cg = (P.cnt["cg0"], P.cnt["cg1"])
    if dbg == "qrhs":
        for mm_ in range(NCH):
            P.emit("sync", lambda e, mm_=mm_: e.dma_start(pw[0][:, 0:1024], scr["s_q"][mm_ * 128:(mm_ + 1) * 128, :]),
                   wait=[P.now("gw"), ("cg0", P.cnt["cg0"]), ("cg1", P.cnt["cg1"]), P.now("so")], inc="sw", inc_k=16)
            P.emit("sync", lambda e, mm_=mm_: e.dma_start(dbgB[mm_ * 128:(mm_ + 1) * 128, 0:1024], pw[0][:, 0:1024]),
                   wait=["sw"], inc="so", inc_k=16)
        P.emit("sync", lambda e: e.dma_start(dbgF[0:128, 0:8], srhs[:, :]), wait=[P.now("a")], inc="so", inc_k=16)
        P.emit("sync", lambda e: e.dma_start(dbgF[0:128, 8:16], rrv[:, :]), inc="so", inc_k=16)
        P.emit("sync", lambda e: e.dma_start(dbgF[0:128, 16:24], amaxR[:, :]), wait=[P.now("v")], inc="so", inc_k=16)
    setup_gw = P.cnt["gw"]

    SETUP_DONE = [("gw", None), ("cg0", None), ("cg1", None)]

    def setup_done_waits():
        return [("gw", setup_gw), ("cg0", setup_cg[0]), ("cg1", setup_cg[1])]

    def stream_w(scr_name, buf_pair, m, ncols, wait_extra, engine="sync", kind="pw"):
        b = m % 2
        semn = f"{kind}{b}"

        def go(e, scr_name=scr_name, m=m, ncols=ncols, b=b):
            return e.dma_start(buf_pair[b][:, 0:ncols], scr[scr_name][m * 128:(m + 1) * 128, 0:ncols])
        mk = P.emit(engine, go, wait=setup_done_waits() + list(wait_extra), inc=semn, inc_k=16)
        return buf_pair[b], mk

    def proj(scr_name, rhs_buf, banks, consumer):
        start_v = P.now("v")
        start_a = P.now("a")
        cons_marks = {}
        tmarks = {}
        for m in range(NCH):
            wx = [P.now("t")] if m < 2 else [tmarks[m - 2]]
            buf, rdy = stream_w(scr_name, pw, m, NCH * 128, wx)
            waits = [rdy, start_v, start_a]
            if m >= 2:
                waits.append(cons_marks[m - 2])
            bank = banks[m % 2]
            for j in range(NCH):
                tm = P.emit("tensor", lambda e, j=j, buf=buf, bank=bank:
                            e.matmul(bank[:, 0:TW], buf[:, j * 128:(j + 1) * 128], xs(rhs_buf, j),
                                     start=(j == 0), stop=(j == NCH - 1)),
                            wait=(waits if j == 0 else []), inc=("t" if j == NCH - 1 else None))
            tmarks[m] = tm
            cons_marks[m] = consumer(m, bank)

    def ln_stats(src):
        for j in range(NCH):
            P.emit("scalar", lambda e, j=j: e.copy(xs(xkb, j), xs(src, j)),
                   wait=[P.now("v"), P.now("t")], inc="a")
        for j in range(NCH):
            P.emit("vector", lambda e, j=j: e.tensor_tensor(xs(xvb, j), xs(xkb, j), xs(xkb, j), ALU.mult),
                   wait=["a", P.now("t")], inc="v")
        for j in range(NCH):
            P.emit("tensor", lambda e, j=j: e.matmul(pb[4][:, 0:TW], onesb[:, :], xs(xkb, j), start=(j == 0), stop=(j == NCH - 1)),
                   wait=(["a", P.now("v")] if j == 0 else []), inc=("t" if j == NCH - 1 else None))
        for j in range(NCH):
            P.emit("tensor", lambda e, j=j: e.matmul(pb[5][:, 0:TW], onesb[:, :], xs(xvb, j), start=(j == 0), stop=(j == NCH - 1)),
                   wait=(["v"] if j == 0 else []), inc=("t" if j == NCH - 1 else None))
        P.emit("vector", lambda e: e.tensor_scalar(pool[1][:, :], pb[4][:, 0:TW], cap(CI_INVC), None, ALU.mult),
               wait=["t"], inc="v")
        P.emit("vector", lambda e: e.tensor_scalar(pool[2][:, :], pb[5][:, 0:TW], cap(CI_INVC), cap(CI_EPS), ALU.mult, ALU.add),
               inc="v")
        P.emit("vector", lambda e: e.scalar_tensor_tensor(pool[3][:, :], pool[1][:, :], cap(CI_NEG1), pool[1][:, :], ALU.mult, ALU.mult),
               inc="v")
        P.emit("vector", lambda e: e.tensor_tensor(pool[2][:, :], pool[2][:, :], pool[3][:, :], ALU.add), inc="v")
        P.emit("scalar", lambda e: e.activation(pool[3][:, :], pool[2][:, :], AF.Ln, bias=0.0, scale=1.0),
               wait=["v"], inc="a")
        P.emit("scalar", lambda e: e.activation(pool[0][:, :], pool[3][:, :], AF.Exp, bias=0.0, scale=-0.5), inc="a")

    def ln_apply(src, dst_fn, wname=None, bname=None):
        for j in range(NCH):
            P.emit("vector", lambda e, j=j: e.tensor_tensor(pool[4][:, :], xs(src, j), pool[1][:, :], ALU.subtract),
                   wait=(["a"] if j == 0 else []), inc="v")
            if wname is None:
                P.emit("vector", lambda e, j=j: e.tensor_tensor(dst_fn(j), pool[4][:, :], pool[0][:, :], ALU.mult), inc="v")
            else:
                P.emit("vector", lambda e, j=j: e.scalar_tensor_tensor(pool[5][:, :], pool[4][:, :], vap(wname, j), pool[0][:, :], ALU.mult, ALU.mult),
                       inc="v")
                P.emit("vector", lambda e, j=j: e.tensor_scalar(dst_fn(j), pool[5][:, :], vap(bname, j), None, ALU.add), inc="v")

    def tap(name, t0, ap_fn, bf=False):
        if dbg != name:
            return
        dst = dbgB if bf else dbgF
        for j in range(NCH):
            P.emit("sync", lambda e, j=j, t0=t0: e.dma_start(dst[j * 128:(j + 1) * 128, t0:t0 + TW], ap_fn(j)),
                   wait=[P.now("v"), P.now("a"), P.now("t")], inc="so", inc_k=16)
        P.emit("vector", lambda e: None, wait=["so"])
        P.emit("scalar", lambda e: None, wait=["so"])
        P.emit("tensor", lambda e: None, wait=["so"])

    P.emit("vector", lambda e: None, wait=setup_done_waits() + [P.now("t"), P.now("a")])
    P.emit("scalar", lambda e: None, wait=setup_done_waits() + [P.now("t"), P.now("v")])
    P.emit("tensor", lambda e: None, wait=setup_done_waits())
    for it in range(NT):
        t0 = it * TW
        for j in range(NCH):
            P.emit("sync", lambda e, j=j, t0=t0: e.dma_start(xs(x_in, j), xT[j * 128:(j + 1) * 128, t0:t0 + TW]),
                   wait=[P.now("v")], inc="sx", inc_k=16)
        if it > 0:
            P.emit("scalar", lambda e: e.copy(h1b[0][:, 0:1], h1b[0][:, TW:TW + 1]),
                   wait=[P.now("v")], inc="a")
            for j in range(NCH):
                if j > 0:
                    P.emit("scalar", lambda e, j=j: e.copy(h1b[j][:, 0:1], h1b[j][:, TW:TW + 1]), inc="a")
                P.emit("scalar", lambda e, j=j: e.copy(h2b[j][:, 0:1], h2b[j][:, TW:TW + 1]), inc="a")
                P.emit("scalar", lambda e, j=j: e.copy(ekb[j][:, 0:1], ekb[j][:, TW:TW + 1]), inc="a")
            for j in range(NCH):
                P.emit("vector", lambda e, j=j: e.tensor_copy(zb[j][:, 0:1], zb[j][:, TW:TW + 1]),
                       wait=[P.now("so")], inc="v")
        P.emit("vector", lambda e: None, wait=["sx"])
        P.emit("scalar", lambda e: None, wait=["sx"])

        ln_stats(x_in)
        ln_apply(x_in, lambda j: xs(xres, j), "ln0_w", "ln0_b")
        tap("x0", t0, lambda j: xs(xres, j))
        ln_stats(xres)
        ln_apply(xres, lambda j: h1b[j][:, 1:TW + 1], "ln1_w", "ln1_b")

        tap("h1", t0, lambda j: h1b[j][:, 1:TW + 1], bf=True)
        for j in range(NCH):
            P.emit("vector", lambda e, j=j: e.tensor_tensor(dl[:, :], h1b[j][:, 1:TW + 1], h1b[j][:, 0:TW], ALU.subtract),
                   wait=(["a", P.now("t")] if j == 0 else ["a"]), inc="v")
            P.emit("vector", lambda e, j=j: e.scalar_tensor_tensor(xs(xkb, j), dl[:, :], vap("tmk", j), h1b[j][:, 0:TW], ALU.mult, ALU.add), inc="v")
            P.emit("vector", lambda e, j=j: e.scalar_tensor_tensor(xs(xvb, j), dl[:, :], vap("tmv", j), h1b[j][:, 0:TW], ALU.mult, ALU.add), inc="v")
            P.emit("vector", lambda e, j=j: e.scalar_tensor_tensor(xs(xrb, j), dl[:, :], vap("tmr", j), h1b[j][:, 0:TW], ALU.mult, ALU.add), inc="v")

        tap("xk", t0, lambda j: xs(xkb, j), bf=True)
        tap("xv", t0, lambda j: xs(xvb, j), bf=True)
        tap("xr", t0, lambda j: xs(xrb, j), bf=True)
        def k_cons(m, bank):
            return P.emit("scalar", lambda e, m=m, bank=bank: e.activation(ekb[m][:, 1:TW + 1], bank[:, 0:TW], AF.Exp, bias=0.0, scale=1.0),
                          wait=["t"], inc="a")
        proj("s_k", xkb, (pb[0], pb[1]), k_cons)

        def v_cons(m, bank):
            return P.emit("vector", lambda e, m=m, bank=bank: e.tensor_tensor(zb[m][:, 1:TW + 1], ekb[m][:, 1:TW + 1], bank[:, 0:TW], ALU.mult),
                          wait=["t", "a"], inc="v")
        proj("s_v", xvb, (pb[2], pb[3]), v_cons)

        tap("ek", t0, lambda j: ekb[j][:, 1:TW + 1])
        tap("z", t0, lambda j: zb[j][:, 1:TW + 1])
        scan_marks = []
        for j in range(NCH):
            P.emit("vector", lambda e, j=j: e.tensor_tensor_scan(xs(PA, j), bass.AP(dvec, j, [[8, 128], [0, TW]]), zb[j][:, 0:TW], hist[:, 4 * j:4 * j + 1], ALU.mult, ALU.add),
                   wait=(["a"] if j == 0 else []), inc="v")
            P.emit("vector", lambda e, j=j: e.tensor_tensor_scan(xs(PB, j), bass.AP(dvec, j, [[8, 128], [0, TW]]), ekb[j][:, 0:TW], hist[:, 4 * j + 1:4 * j + 2], ALU.mult, ALU.add),
                   inc="v")
            P.emit("vector", lambda e, j=j: e.tensor_copy(hist[:, 4 * j:4 * j + 1], PA[:, j * TW + TW - 1:j * TW + TW]), inc="v")
            mk = P.emit("vector", lambda e, j=j: e.tensor_copy(hist[:, 4 * j + 1:4 * j + 2], PB[:, j * TW + TW - 1:j * TW + TW]), inc="v")
            scan_marks.append(mk)

        tap("PA", t0, lambda j: xs(PA, j))
        tap("PB", t0, lambda j: xs(PB, j))
        def r_cons(m, bank):
            P.emit("vector", lambda e, m=m: e.scalar_tensor_tensor(pool[0][:, :], zb[m][:, 1:TW + 1], euv[:, m:m + 1], xs(PA, m), ALU.mult, ALU.add),
                   wait=[scan_marks[m], "a"], inc="v")
            P.emit("vector", lambda e, m=m: e.scalar_tensor_tensor(pool[1][:, :], ekb[m][:, 1:TW + 1], euv[:, m:m + 1], xs(PB, m), ALU.mult, ALU.add),
                   inc="v")
            P.emit("scalar", lambda e: e.activation(pool[2][:, :], pool[1][:, :], AF.Ln, bias=0.0, scale=1.0),
                   wait=["v"], inc="a")
            P.emit("scalar", lambda e: e.activation(pool[3][:, :], pool[2][:, :], AF.Exp, bias=0.0, scale=-1.0), inc="a")
            P.emit("scalar", lambda e, bank=bank: e.activation(pool[4][:, :], bank[:, 0:TW], AF.Exp, bias=0.0, scale=-1.0),
                   wait=["t"], inc="a")
            P.emit("scalar", lambda e: e.activation(pool[5][:, :], pool[4][:, :], AF.Ln, bias=1.0, scale=1.0), inc="a")
            P.emit("scalar", lambda e: e.activation(pool[4][:, :], pool[5][:, :], AF.Exp, bias=0.0, scale=-1.0), inc="a")
            P.emit("vector", lambda e: e.tensor_tensor(pool[0][:, :], pool[0][:, :], pool[3][:, :], ALU.mult),
                   wait=["a"], inc="v")
            return P.emit("vector", lambda e, m=m: e.tensor_tensor(xs(xkb, m), pool[4][:, :], pool[0][:, :], ALU.mult), inc="v")
        proj("s_r", xrb, (pb[4], pb[5]), r_cons)

        tap("rwkvin", t0, lambda j: xs(xkb, j), bf=True)
        def o_cons(m, bank):
            return P.emit("vector", lambda e, m=m, bank=bank: e.tensor_tensor(xs(xres, m), xs(xres, m), bank[:, 0:TW], ALU.add),
                          wait=["t"], inc="v")
        proj("s_o", xkb, (pb[0], pb[1]), o_cons)

        tap("x1", t0, lambda j: xs(xres, j))
        ln_stats(xres)
        ln_apply(xres, lambda j: h2b[j][:, 1:TW + 1], "ln2_w", "ln2_b")

        for j in range(NCH):
            P.emit("vector", lambda e, j=j: e.tensor_tensor(dl[:, :], h2b[j][:, 1:TW + 1], h2b[j][:, 0:TW], ALU.subtract),
                   wait=(["a", P.now("t")] if j == 0 else ["a"]), inc="v")
            P.emit("vector", lambda e, j=j: e.scalar_tensor_tensor(xs(xkb, j), dl[:, :], vap("cmk", j), h2b[j][:, 0:TW], ALU.mult, ALU.add), inc="v")
            P.emit("vector", lambda e, j=j: e.scalar_tensor_tensor(xs(xrb, j), dl[:, :], vap("cmr", j), h2b[j][:, 0:TW], ALU.mult, ALU.add), inc="v")

        def fr_cons(m, bank):
            P.emit("scalar", lambda e, bank=bank: e.activation(pool[2][:, :], bank[:, 0:TW], AF.Exp, bias=0.0, scale=-1.0),
                   wait=["t", P.now("v")], inc="a")
            P.emit("scalar", lambda e: e.activation(pool[2][:, :], pool[2][:, :], AF.Ln, bias=1.0, scale=1.0), inc="a")
            return P.emit("scalar", lambda e, m=m: e.activation(xs(sig2, m), pool[2][:, :], AF.Exp, bias=0.0, scale=-1.0), inc="a")
        proj("s_fr", xrb, (pb[2], pb[3]), fr_cons)

        for hw in range(2):
            col0 = hw * HW
            kftt_marks = {}
            fk_tmarks = {}
            for m in range(NHC):
                wx = [P.now("t"), P.now("v"), P.now("a")] if m < 2 else [fk_tmarks[m - 2], P.now("v")]
                buf, rdy = stream_w("s_fk", fkw, m, NCH * 128, wx, engine="gpsimd", kind="fk")
                bank = pb[m % 2]
                waits = [rdy, P.now("v"), P.now("a")]
                if m >= 2:
                    waits.append(kftt_marks[m - 2])
                for j in range(NCH):
                    tm = P.emit("tensor", lambda e, j=j, buf=buf, bank=bank, col0=col0:
                                e.matmul(bank[:, 0:HW], buf[:, j * 128:(j + 1) * 128], xkb[:, j * TW + col0:j * TW + col0 + HW],
                                         start=(j == 0), stop=(j == NCH - 1)),
                                wait=(waits if j == 0 else []), inc=("t" if j == NCH - 1 else None))
                fk_tmarks[m] = tm
                rslot = pool[2 + (m % 2)]
                P.emit("scalar", lambda e, bank=bank, rslot=rslot: e.activation(rslot[:, 0:HW], bank[:, 0:HW], AF.Relu, bias=0.0, scale=1.0),
                       wait=(["t", P.now("v")] if m < 2 else ["t", kftt_marks[m - 2], P.now("v")]), inc="a")
                kftt_marks[m] = P.emit("vector", lambda e, m=m, bank=bank, rslot=rslot:
                                       e.tensor_tensor(kf[:, m * HW:(m + 1) * HW], bank[:, 0:HW], rslot[:, 0:HW], ALU.mult),
                                       wait=["a", "t"], inc="v")
            kv_marks = {}
            fv_tmarks = {}
            for m in range(NCH):
                wx = [P.now("t"), P.now("v"), P.now("a")] if m < 2 else [fv_tmarks[m - 2], P.now("v")]
                buf, rdy = stream_w("s_fv", fvw, m, NHC * 128, wx, kind="fv")
                bank = pb[2 + (m % 2)]
                waits = [rdy, P.now("v"), P.now("a")]
                if m >= 2:
                    waits.append(kv_marks[m - 2])
                for o in range(NHC):
                    tm = P.emit("tensor", lambda e, o=o, buf=buf, bank=bank:
                                e.matmul(bank[:, 0:HW], buf[:, o * 128:(o + 1) * 128], kf[:, o * HW:(o + 1) * HW],
                                         start=(o == 0), stop=(o == NHC - 1)),
                                wait=(waits if o == 0 else []), inc=("t" if o == NHC - 1 else None))
                fv_tmarks[m] = tm
                tslot = pool[4 + (m % 2)]
                P.emit("vector", lambda e, m=m, bank=bank, col0=col0, tslot=tslot:
                       e.tensor_tensor(tslot[:, 0:HW], sig2[:, m * TW + col0:m * TW + col0 + HW], bank[:, 0:HW], ALU.mult),
                       wait=["t", "a"], inc="v")
                kv_marks[m] = P.emit("vector", lambda e, m=m, col0=col0, tslot=tslot:
                                     e.tensor_tensor(xres[:, m * TW + col0:m * TW + col0 + HW], xres[:, m * TW + col0:m * TW + col0 + HW], tslot[:, 0:HW], ALU.add),
                                     inc="v")

        tap("x2", t0, lambda j: xs(xres, j))
        tap("sig2", t0, lambda j: xs(sig2, j), bf=True)
        for j in range(NCH):
            P.emit("vector", lambda e, j=j: e.tensor_tensor(xs(PB, j), xs(x_in, j), xs(xres, j), ALU.add), inc="v")

        ln_stats(PB)
        ln_apply(PB, lambda j: xs(PA, j))

        tap("y1", t0, lambda j: xs(PA, j))
        for g in range(4):
            for j in range(NCH):
                P.emit("tensor", lambda e, g=g, j=j: e.transpose(pT[:, j * 128:(j + 1) * 128], PA[:, j * TW + g * 128:j * TW + (g + 1) * 128], idf[:, :]),
                       wait=([P.now("v")] if j == 0 else []), inc=("t" if j == NCH - 1 else None))
            P.emit("vector", lambda e, g=g: e.tensor_reduce(am4[:, g:g + 1], pT[:, 0:1024], AX.X, ALU.max, apply_absolute_value=True),
                   wait=["t"], inc="v")
        P.emit("scalar", lambda e: e.activation(amaxR[:, 0:4], am4[:, 0:4], AF.Ln, bias=0.0, scale=1.0),
               wait=["v"], inc="a")
        P.emit("scalar", lambda e: e.activation(am4[:, 4:8], amaxR[:, 0:4], AF.Exp, bias=cap(CI_LN127), scale=-1.0), inc="a")
        P.emit("scalar", lambda e: e.activation(am4[:, 0:4], amaxR[:, 0:4], AF.Exp, bias=cap(CI_NLN127), scale=1.0), inc="a")
        P.emit("vector", lambda e: None, wait=["a"])
        P.emit("tensor", lambda e: e.transpose(pT[0:8, 0:128], am4[:, 0:8], idf[:, :]), wait=["v", "a"], inc="t")
        P.emit("vector", lambda e: e.tensor_copy(prT[:, :], pT[0:8, 0:128]), wait=["t"], inc="v")
        for g in range(4):
            P.emit("tensor", lambda e, g=g: e.matmul(pT[:, g * 128:(g + 1) * 128], self8[:, (4 + g) * 128:(5 + g) * 128], prT[:, :], start=True, stop=True),
                   wait=(["v"] if g == 0 else []), inc=None)
            P.emit("tensor", lambda e, g=g: e.matmul(pT[:, TW + g * 128:TW + (g + 1) * 128], self8[:, g * 128:(g + 1) * 128], prT[:, :], start=True, stop=True),
                   inc=("t" if g == 3 else None))
        P.emit("vector", lambda e: e.tensor_copy(pool[0][:, :], pT[:, TW:2 * TW]), wait=["t"], inc="v")
        if dbg in ("rinvbc", "am4"):
            if dbg == "rinvbc":
                P.emit("vector", lambda e: e.tensor_copy(pool[2][:, :], pT[:, 0:TW]), wait=["t"], inc="v")
            else:
                P.emit("vector", lambda e: e.tensor_copy(pool[3][:, 0:8], am4[:, :]), inc="v")
                P.emit("vector", lambda e: e.tensor_copy(pool[3][:, 8:16], amaxR[:, 0:8]), inc="v")
                P.emit("vector", lambda e: e.tensor_copy(pool[3][:, 16:24], pool[2][:, 0:8]), inc="v")
            P.emit("sync", lambda e, t0=t0, _src=(2 if dbg == "rinvbc" else 3): e.dma_start(dbgF[0:128, t0:t0 + TW], pool[_src][:, :]),
                   wait=[P.now("v")], inc="so", inc_k=16)
            P.emit("vector", lambda e: None, wait=["so"])
        for j in range(NCH):
            P.emit("vector", lambda e, j=j: e.tensor_tensor(qi8[:, 0:TW], xs(PA, j), pT[:, 0:TW], ALU.mult),
                   wait=([P.now("t")] if j == 0 else []), inc="v")
            P.emit("vector", lambda e, j=j: e.tensor_copy(xs(xkb, j), qi8[:, 0:TW]), inc="v")

        tap("qbf", t0, lambda j: xs(xkb, j), bf=True)
        def ffn_cons(m, bank):
            P.emit("vector", lambda e, m=m, bank=bank: e.scalar_tensor_tensor(pool[3][:, :], bank[:, 0:TW], srhs[:, m:m + 1], pool[0][:, :], ALU.mult, ALU.mult),
                   wait=["t"], inc="v")
            return P.emit("vector", lambda e, m=m: e.tensor_tensor(xs(xres, m), xs(PA, m), pool[3][:, :], ALU.add), inc="v")
        proj("s_q", xkb, (pb[0], pb[1]), ffn_cons)

        tap("s2", t0, lambda j: xs(xres, j))
        ln_stats(xres)
        for j in range(NCH):
            P.emit("vector", lambda e, j=j: e.tensor_tensor(pool[4][:, :], xs(xres, j), pool[1][:, :], ALU.subtract),
                   wait=(["a"] if j == 0 else []), inc="v")
            P.emit("vector", lambda e, j=j: e.tensor_tensor(zb[j][:, 0:TW], pool[4][:, :], pool[0][:, :], ALU.mult), inc="v")
            P.emit("sync", lambda e, j=j, t0=t0: e.dma_start(outT[j * 128:(j + 1) * 128, t0:t0 + TW], zb[j][:, 0:TW]),
                   wait=["v"], inc="so", inc_k=16)

    P.emit("sync", lambda e: None, wait=["so"])

    with nc.Block() as block:
        @block.sync
        def _(e):
            P.run(e, "sync", sems)

        @block.gpsimd
        def _(e):
            P.run(e, "gpsimd", sems)

        @block.tensor
        def _(e):
            P.run(e, "tensor", sems)

        @block.vector
        def _(e):
            P.run(e, "vector", sems)

        @block.scalar
        def _(e):
            P.run(e, "scalar", sems)

    return nc, ctx


def kernel(**inputs):
    per_core = host_prep(inputs)
    nc, ctx = build_program()
    res = run_bass_kernel_spmd(nc, per_core, core_ids=list(range(8)))
    outs = [np.ascontiguousarray(r["outT"].T) for r in res.results]
    return np.stack(outs, axis=0).astype(np.float32)


# version 05
# speedup vs baseline: 1.4468x; 1.4397x over previous
import numpy as np
from contextlib import ExitStack

import concourse.bass as bass
import concourse.mybir as mybir
from concourse.bass_utils import run_bass_kernel_spmd

F32 = mybir.dt.float32
BF16 = mybir.dt.bfloat16
I8 = mybir.dt.int8
ALU = mybir.AluOpType
AF = mybir.ActivationFunctionType
AX = mybir.AxisListType

C = 1024
T = 4096
H = 4096
NCH = C // 128
NHC = H // 128
TW = 512
NT = T // TW
HW = TW // 2
EPS = 1e-5

ENGINES = ("sync", "gpsimd", "tensor", "vector", "scalar")
CI_INVC, CI_EPS, CI_NEG1, CI_127, CI_INV127, CI_LN127, CI_NLN127, CI_ONE = range(8)


class Plan:

    def __init__(self):
        self.items = {e: [] for e in ENGINES}
        self.cnt = {}
        self.seen = {e: {} for e in ENGINES}

    def sem(self, name):
        self.cnt[name] = 0

    def now(self, s):
        return (s, self.cnt[s])

    def emit(self, eng, build, wait=(), inc=None, inc_k=1):
        waits = []
        for w_ in wait:
            s, v = w_ if isinstance(w_, tuple) else (w_, self.cnt[w_])
            if v <= 0 or self.seen[eng].get(s, -1) >= v:
                continue
            self.seen[eng][s] = v
            waits.append((s, v))
        if inc is not None:
            self.cnt[inc] += inc_k
        self.items[eng].append((waits, build, inc, inc_k))
        return (inc, self.cnt[inc]) if inc is not None else None

    def run(self, e, eng_name, sems):
        for waits, build, inc, inc_k in self.items[eng_name]:
            for s, v in waits:
                e.wait_ge(sems[s], v)
            ins = build(e) if build is not None else None
            if inc is not None:
                ins.then_inc(sems[inc], inc_k)


def host_prep(inputs):
    d = inputs
    x = np.asarray(d["L0_attn_x"], dtype=np.float32)
    B = x.shape[0]

    def vec8(v):
        return np.ascontiguousarray(np.asarray(v, np.float32).reshape(NCH, 128).T)

    def wproj(w):
        a = np.asarray(w, np.float32).T.reshape(NCH, 128, NCH, 128)
        return np.ascontiguousarray(a.transpose(2, 1, 0, 3).reshape(C, C))

    def wfk_prep(w):
        a = np.asarray(w, np.float32).T.reshape(NCH, 128, NHC, 128)
        return np.ascontiguousarray(a.transpose(2, 1, 0, 3).reshape(H, C))

    def wfv_prep(w):
        a = np.asarray(w, np.float32).T.reshape(NHC, 128, NCH, 128)
        return np.ascontiguousarray(a.transpose(2, 1, 0, 3).reshape(C, H))

    vecs = np.concatenate([
        vec8(d["L0_attn_ln0_w"]), vec8(d["L0_attn_ln0_b"]),
        vec8(d["L0_attn_ln1_w"]), vec8(d["L0_attn_ln1_b"]),
        vec8(d["L0_attn_ln2_w"]), vec8(d["L0_attn_ln2_b"]),
        vec8(d["L0_attn_time_decay"]), vec8(d["L0_attn_time_first"]),
        vec8(d["L0_attn_tmk"]), vec8(d["L0_attn_tmv"]), vec8(d["L0_attn_tmr"]),
        vec8(d["L0_attn_cmk"]), vec8(d["L0_attn_cmr"]),
    ], axis=1)

    csts = np.zeros((128, 8), np.float32)
    csts[:, CI_INVC] = 1.0 / C
    csts[:, CI_EPS] = EPS
    csts[:, CI_NEG1] = -1.0
    csts[:, CI_127] = 127.0
    csts[:, CI_INV127] = 1.0 / 127.0
    csts[:, CI_LN127] = np.log(127.0)
    csts[:, CI_NLN127] = -np.log(127.0)
    csts[:, CI_ONE] = 1.0

    sel8 = np.zeros((8, 8 * 128), np.float32)
    for j in range(8):
        sel8[j, j * 128:(j + 1) * 128] = 1.0

    shared = {
        "w_k": wproj(d["L0_attn_Wk"]), "w_v": wproj(d["L0_attn_Wv"]),
        "w_r": wproj(d["L0_attn_Wr"]), "w_o": wproj(d["L0_attn_Wo"]),
        "w_fr": wproj(d["L0_attn_Wfr"]),
        "w_fk": wfk_prep(d["L0_attn_Wfk"]), "w_fv": wfv_prep(d["L0_attn_Wfv"]),
        "rhsT": np.ascontiguousarray(np.asarray(d["L0_ffn_rhs"], np.float32).T),
        "vecs": vecs, "csts": csts,
        "ones": np.ones((128, 128), np.float32),
        "ident": np.eye(128, dtype=np.float32),
        "sel8": sel8,
    }
    per_core = []
    for b in range(B):
        m = dict(shared)
        m["xT"] = np.ascontiguousarray(x[b].T)
        per_core.append(m)
    return per_core


def build_program(dbg=None):
    nc = bass.Bass("TRN2", target_bir_lowering=False, detect_race_conditions=False)
    ctx = ExitStack()

    xT = nc.dram_tensor("xT", [C, T], F32, kind="ExternalInput")
    outT = nc.dram_tensor("outT", [C, T], F32, kind="ExternalOutput")
    dbgF = nc.dram_tensor("dbgF", [C, T], F32, kind="ExternalOutput")
    dbgB = nc.dram_tensor("dbgB", [C, T], BF16, kind="ExternalOutput")
    din = {}
    for n, shp in [("w_k", [C, C]), ("w_v", [C, C]), ("w_r", [C, C]),
                   ("w_o", [C, C]), ("w_fr", [C, C]), ("w_fk", [H, C]),
                   ("w_fv", [C, H]), ("rhsT", [C, C]),
                   ("vecs", [128, 13 * 8]), ("csts", [128, 8]),
                   ("ones", [128, 128]), ("ident", [128, 128]),
                   ("sel8", [8, 8 * 128])]:
        din[n] = nc.dram_tensor(n, shp, F32, kind="ExternalInput")
    scr = {}
    for n, shp in [("s_k", [C, C]), ("s_v", [C, C]), ("s_r", [C, C]),
                   ("s_o", [C, C]), ("s_fr", [C, C]), ("s_fk", [H, C]),
                   ("s_fv", [C, H]), ("s_q", [C, C])]:
        scr[n] = nc.dram_tensor(n, shp, BF16, kind="Internal")

    def sb(name, shape, dtype=F32):
        return ctx.enter_context(nc.sbuf_tensor(name, shape, dtype))

    x_in = sb("x_in", [128, NCH * TW])
    xres = sb("xres", [128, NCH * TW])
    h1b = [sb(f"h1_{j}", [128, TW + 1], BF16) for j in range(NCH)]
    h2b = [sb(f"h2_{j}", [128, TW + 1], BF16) for j in range(NCH)]
    xkb = sb("xkb", [128, NCH * TW], BF16)
    xvb = sb("xvb", [128, NCH * TW], BF16)
    xrb = sb("xrb", [128, NCH * TW], BF16)
    sig2 = sb("sig2", [128, NCH * TW], BF16)
    ekb = [sb(f"ek_{j}", [128, TW + 1]) for j in range(NCH)]
    zb = [sb(f"z_{j}", [128, TW + 1]) for j in range(NCH)]
    PA = sb("PA", [128, NCH * TW])
    PB = sb("PB", [128, NCH * TW])
    kf = sb("kf", [128, NHC * HW], BF16)
    pool = [sb(f"pool{i}", [128, TW]) for i in range(6)]
    dl = sb("dl", [128, TW], BF16)
    vcs = sb("vcs", [128, 13 * 8])
    cst = sb("cst", [128, 8])
    dvec = sb("dvec", [128, 8])
    euv = sb("euv", [128, 8])
    srhs = sb("srhs", [128, 8])
    rrv = sb("rrv", [128, 8])
    hist = sb("hist", [128, 4 * NCH])
    am4 = sb("am4", [128, 8])
    amaxR = sb("amaxR", [128, 8])
    prT = sb("prT", [8, 128])
    onesb = sb("onesb", [128, 128], BF16)
    idf = sb("idf", [128, 128])
    idb = sb("idb", [128, 128], BF16)
    self8 = sb("self8", [8, 8 * 128])
    qi8 = sb("qi8", [128, 1024], I8)
    pw = [sb(f"pw{i}", [128, NCH * 128], BF16) for i in range(2)]
    fkw = [sb(f"fkw{i}", [128, NCH * 128], BF16) for i in range(2)]
    fvw = [sb(f"fvw{i}", [128, NHC * 128], BF16) for i in range(2)]
    def alias(name, shape, dtype, backing):
        h = nc.virtual_sbuf_tensor(name, shape, dtype)
        nc._tensor(h.name, list(h.shape), h.dtype, type="SB")
        mv = nc.lookup_mloc(h)
        mv.allocated = True
        mv.addr = nc.lookup_mloc(backing).addr
        return h

    pcf = [alias("pcf0", [128, 2048], F32, kf), alias("pcf1", [128, 2048], F32, xkb)]
    pcb = [alias("pcb0", [128, 2048], BF16, sig2), alias("pcb1", [128, 2048], BF16, xrb)]
    pi8 = alias("pi8", [128, 1024], I8, qi8)

    pb = [ctx.enter_context(nc.psum_tensor(f"pb{i}", [128, 512], F32)) for i in range(6)]
    pT = ctx.enter_context(nc.psum_tensor("pT", [128, 1024], F32))

    P = Plan()
    semnames = ["sx", "sw", "gw", "so", "t", "v", "a",
                "pw0", "pw1", "fk0", "fk1", "fv0", "fv1", "cl0", "cl1", "cg0", "cg1"]
    for s in semnames:
        P.sem(s)
    sems = {s: ctx.enter_context(nc.semaphore(s)) for s in semnames}

    def xs(buf, j, w=TW):
        return buf[:, j * w:(j + 1) * w]

    VC = {n: i for i, n in enumerate(
        ["ln0_w", "ln0_b", "ln1_w", "ln1_b", "ln2_w", "ln2_b",
         "td", "tf", "tmk", "tmv", "tmr", "cmk", "cmr"])}

    def vap(name, j):
        return vcs[:, VC[name] * 8 + j:VC[name] * 8 + j + 1]

    def cap(i):
        return cst[:, i:i + 1]

    P.emit("sync", lambda e: e.dma_start(vcs[:, :], din["vecs"][:, :]), inc="sx", inc_k=16)
    P.emit("sync", lambda e: e.dma_start(cst[:, :], din["csts"][:, :]), inc="sx", inc_k=16)
    P.emit("sync", lambda e: e.dma_start(idf[:, :], din["ident"][:, :]), inc="sx", inc_k=16)
    P.emit("sync", lambda e: e.dma_start(pcf[0][:, 0:128], din["ones"][:, :]), inc="sx", inc_k=16)
    P.emit("sync", lambda e: e.dma_start(self8[:, :], din["sel8"][:, :]), inc="sx", inc_k=16)
    P.emit("vector", lambda e: e.tensor_copy(onesb[:, :], pcf[0][:, 0:128]), wait=["sx"], inc="v")
    P.emit("vector", lambda e: e.tensor_copy(idb[:, :], idf[:, :]), inc="v")
    P.emit("scalar", lambda e: e.activation(am4[:, :], vcs[:, VC["td"] * 8:VC["td"] * 8 + 8], AF.Exp, bias=0.0, scale=1.0), wait=["sx"], inc="a")
    P.emit("scalar", lambda e: e.activation(dvec[:, :], am4[:, :], AF.Exp, bias=0.0, scale=-1.0), inc="a")
    P.emit("scalar", lambda e: e.activation(euv[:, :], vcs[:, VC["tf"] * 8:VC["tf"] * 8 + 8], AF.Exp, bias=0.0, scale=1.0), inc="a")
    for j in range(NCH):
        P.emit("vector", lambda e, j=j: e.memset(zb[j][:, 0:1], 0.0), inc="v")
        P.emit("vector", lambda e, j=j: e.memset(ekb[j][:, 0:1], 0.0), inc="v")
        P.emit("vector", lambda e, j=j: e.memset(h1b[j][:, 0:1], 0.0), inc="v")
        P.emit("vector", lambda e, j=j: e.memset(h2b[j][:, 0:1], 0.0), inc="v")
    P.emit("vector", lambda e: e.memset(hist[:, :], 0.0), inc="v")

    cast_jobs = []
    for n_in, n_scr, rows, cols in [
        ("w_k", "s_k", C, C), ("w_v", "s_v", C, C), ("w_r", "s_r", C, C),
        ("w_o", "s_o", C, C), ("w_fr", "s_fr", C, C),
        ("w_fk", "s_fk", H, C), ("w_fv", "s_fv", C, H),
    ]:
        for r0 in range(0, rows, 128):
            for c0 in range(0, cols, 2048):
                cast_jobs.append((n_in, n_scr, r0, c0, min(2048, cols - c0)))
    store_marks = []
    for par, (n_in, n_scr, r0, c0, cw) in enumerate(cast_jobs):
        b = par % 2
        waits = [P.now("v"), P.now("a")] if par < 2 else [store_marks[par - 2]]
        ld_mk = P.emit("sync", lambda e, n_in=n_in, r0=r0, c0=c0, cw=cw, b=b:
                       e.dma_start(pcf[b][:, 0:cw], din[n_in][r0:r0 + 128, c0:c0 + cw]),
                       wait=waits, inc=f"cl{b}", inc_k=16)
        P.emit("scalar", lambda e, b=b, cw=cw: e.copy(pcb[b][:, 0:cw], pcf[b][:, 0:cw]),
               wait=[ld_mk], inc="a")
        mk = P.emit("gpsimd", lambda e, n_scr=n_scr, r0=r0, c0=c0, cw=cw, b=b:
                    e.dma_start(scr[n_scr][r0:r0 + 128, c0:c0 + cw], pcb[b][:, 0:cw]),
                    wait=["a"], inc=f"cg{b}", inc_k=16)
        store_marks.append(mk)

    red_marks = {}
    for j in range(NCH):
        b = j % 2
        lw = [P.now("a")] if j < 2 else [red_marks[j - 2]]
        ld_mk = P.emit("sync", lambda e, j=j, b=b: e.dma_start(pcf[b][:, 0:1024], din["rhsT"][j * 128:(j + 1) * 128, :]),
               wait=lw, inc=f"cl{b}", inc_k=16)
        red_marks[j] = P.emit("vector", lambda e, j=j, b=b: e.tensor_reduce(amaxR[:, j:j + 1], pcf[b][:, 0:1024], AX.X, ALU.max, apply_absolute_value=True),
               wait=[ld_mk], inc="v")
    P.emit("scalar", lambda e: e.activation(pcf[0][:, 0:8], amaxR[:, :], AF.Ln, bias=0.0, scale=1.0),
           wait=["v", "sx"], inc="a")
    P.emit("scalar", lambda e: e.activation(rrv[:, :], pcf[0][:, 0:8], AF.Exp, bias=cap(CI_LN127), scale=-1.0), inc="a")
    P.emit("scalar", lambda e: e.activation(srhs[:, :], pcf[0][:, 0:8], AF.Exp, bias=cap(CI_NLN127), scale=1.0), inc="a")
    P.emit("vector", lambda e: None, wait=["a"])
    for j in range(NCH):
        BW = ["v", "a", "t", P.now("gw"), P.now("sw"), P.now("sx")]
        ld2 = P.emit("sync", lambda e, j=j: e.dma_start(pcf[0][:, 0:1024], din["rhsT"][j * 128:(j + 1) * 128, :]),
               wait=list(BW), inc="cl0", inc_k=16)
        P.emit("vector", lambda e, j=j: e.tensor_scalar(pi8[:, 0:1024], pcf[0][:, 0:1024], rrv[:, j:j + 1], None, ALU.mult),
               wait=[ld2, "a", "t", P.now("gw")], inc="v")
        P.emit("vector", lambda e: e.tensor_copy(pcf[1][:, 0:1024], pi8[:, 0:1024]),
               wait=[P.now("gw"), P.now("t")], inc="v")
        for m in range(NCH):
            P.emit("tensor", lambda e, m=m: e.transpose(pT[:, m * 128:(m + 1) * 128], pcf[1][:, m * 128:(m + 1) * 128], idf[:, :]),
                   wait=(["v", P.now("a"), P.now("gw")] if m == 0 else []), inc=("t" if m == NCH - 1 else None))
        P.emit("scalar", lambda e: e.copy(pcb[1][:, 0:1024], pT[:, 0:1024]),
               wait=["t", "v", P.now("gw")], inc="a")
        for m in range(NCH):
            P.emit("gpsimd", lambda e, j=j, m=m: e.dma_start(scr["s_q"][j * 128:(j + 1) * 128, m * 128:(m + 1) * 128], pcb[1][:, m * 128:(m + 1) * 128]),
                   wait=["a", "t", "v"], inc="gw", inc_k=16)
    setup_cg = (P.cnt["cg0"], P.cnt["cg1"])
    if dbg == "qrhs":
        for mm_ in range(NCH):
            P.emit("sync", lambda e, mm_=mm_: e.dma_start(pw[0][:, 0:1024], scr["s_q"][mm_ * 128:(mm_ + 1) * 128, :]),
                   wait=[P.now("gw"), ("cg0", P.cnt["cg0"]), ("cg1", P.cnt["cg1"]), P.now("so")], inc="sw", inc_k=16)
            P.emit("sync", lambda e, mm_=mm_: e.dma_start(dbgB[mm_ * 128:(mm_ + 1) * 128, 0:1024], pw[0][:, 0:1024]),
                   wait=["sw"], inc="so", inc_k=16)
        P.emit("sync", lambda e: e.dma_start(dbgF[0:128, 0:8], srhs[:, :]), wait=[P.now("a")], inc="so", inc_k=16)
        P.emit("sync", lambda e: e.dma_start(dbgF[0:128, 8:16], rrv[:, :]), inc="so", inc_k=16)
        P.emit("sync", lambda e: e.dma_start(dbgF[0:128, 16:24], amaxR[:, :]), wait=[P.now("v")], inc="so", inc_k=16)
    setup_gw = P.cnt["gw"]

    SETUP_DONE = [("gw", None), ("cg0", None), ("cg1", None)]

    def setup_done_waits():
        return [("gw", setup_gw), ("cg0", setup_cg[0]), ("cg1", setup_cg[1])]

    def stream_w(scr_name, buf_pair, m, ncols, wait_extra, engine="sync", kind="pw"):
        b = m % 2
        semn = f"{kind}{b}"

        def go(e, scr_name=scr_name, m=m, ncols=ncols, b=b):
            return e.dma_start(buf_pair[b][:, 0:ncols], scr[scr_name][m * 128:(m + 1) * 128, 0:ncols])
        mk = P.emit(engine, go, wait=setup_done_waits() + list(wait_extra), inc=semn, inc_k=16)
        return buf_pair[b], mk

    def proj(scr_name, rhs_buf, banks, consumer):
        start_v = P.now("v")
        start_a = P.now("a")
        cons_marks = {}
        tmarks = {}
        for m in range(NCH):
            wx = [P.now("t")] if m < 2 else [tmarks[m - 2]]
            buf, rdy = stream_w(scr_name, pw, m, NCH * 128, wx)
            waits = [rdy, start_v, start_a]
            if m >= 2:
                waits.append(cons_marks[m - 2])
            bank = banks[m % 2]
            for j in range(NCH):
                tm = P.emit("tensor", lambda e, j=j, buf=buf, bank=bank:
                            e.matmul(bank[:, 0:TW], buf[:, j * 128:(j + 1) * 128], xs(rhs_buf, j),
                                     start=(j == 0), stop=(j == NCH - 1)),
                            wait=(waits if j == 0 else []), inc=("t" if j == NCH - 1 else None))
            tmarks[m] = tm
            cons_marks[m] = consumer(m, bank)

    def ln_stats(src):
        for j in range(NCH):
            P.emit("scalar", lambda e, j=j: e.copy(xs(xkb, j), xs(src, j)),
                   wait=[P.now("v"), P.now("t")], inc="a")
        for j in range(NCH):
            P.emit("vector", lambda e, j=j: e.tensor_tensor(xs(xvb, j), xs(xkb, j), xs(xkb, j), ALU.mult),
                   wait=["a", P.now("t")], inc="v")
        for j in range(NCH):
            P.emit("tensor", lambda e, j=j: e.matmul(pb[4][:, 0:TW], onesb[:, :], xs(xkb, j), start=(j == 0), stop=(j == NCH - 1)),
                   wait=(["a", P.now("v")] if j == 0 else []), inc=("t" if j == NCH - 1 else None))
        for j in range(NCH):
            P.emit("tensor", lambda e, j=j: e.matmul(pb[5][:, 0:TW], onesb[:, :], xs(xvb, j), start=(j == 0), stop=(j == NCH - 1)),
                   wait=(["v"] if j == 0 else []), inc=("t" if j == NCH - 1 else None))
        P.emit("vector", lambda e: e.tensor_scalar(pool[1][:, :], pb[4][:, 0:TW], cap(CI_INVC), None, ALU.mult),
               wait=["t"], inc="v")
        P.emit("vector", lambda e: e.tensor_scalar(pool[2][:, :], pb[5][:, 0:TW], cap(CI_INVC), cap(CI_EPS), ALU.mult, ALU.add),
               inc="v")
        P.emit("vector", lambda e: e.scalar_tensor_tensor(pool[3][:, :], pool[1][:, :], cap(CI_NEG1), pool[1][:, :], ALU.mult, ALU.mult),
               inc="v")
        P.emit("vector", lambda e: e.tensor_tensor(pool[2][:, :], pool[2][:, :], pool[3][:, :], ALU.add), inc="v")
        P.emit("scalar", lambda e: e.activation(pool[3][:, :], pool[2][:, :], AF.Ln, bias=0.0, scale=1.0),
               wait=["v"], inc="a")
        P.emit("scalar", lambda e: e.activation(pool[0][:, :], pool[3][:, :], AF.Exp, bias=0.0, scale=-0.5), inc="a")

    def ln_apply(src, dst_fn, wname=None, bname=None):
        for j in range(NCH):
            P.emit("vector", lambda e, j=j: e.tensor_tensor(pool[4][:, :], xs(src, j), pool[1][:, :], ALU.subtract),
                   wait=(["a"] if j == 0 else []), inc="v")
            if wname is None:
                P.emit("vector", lambda e, j=j: e.tensor_tensor(dst_fn(j), pool[4][:, :], pool[0][:, :], ALU.mult), inc="v")
            else:
                P.emit("vector", lambda e, j=j: e.scalar_tensor_tensor(pool[5][:, :], pool[4][:, :], vap(wname, j), pool[0][:, :], ALU.mult, ALU.mult),
                       inc="v")
                P.emit("vector", lambda e, j=j: e.tensor_scalar(dst_fn(j), pool[5][:, :], vap(bname, j), None, ALU.add), inc="v")

    def tap(name, t0, ap_fn, bf=False):
        if dbg != name:
            return
        dst = dbgB if bf else dbgF
        for j in range(NCH):
            P.emit("sync", lambda e, j=j, t0=t0: e.dma_start(dst[j * 128:(j + 1) * 128, t0:t0 + TW], ap_fn(j)),
                   wait=[P.now("v"), P.now("a"), P.now("t")], inc="so", inc_k=16)
        P.emit("vector", lambda e: None, wait=["so"])
        P.emit("scalar", lambda e: None, wait=["so"])
        P.emit("tensor", lambda e: None, wait=["so"])

    P.emit("vector", lambda e: None, wait=setup_done_waits() + [P.now("t"), P.now("a")])
    P.emit("scalar", lambda e: None, wait=setup_done_waits() + [P.now("t"), P.now("v")])
    P.emit("tensor", lambda e: None, wait=setup_done_waits())
    for it in range(NT):
        t0 = it * TW
        for j in range(NCH):
            P.emit("sync", lambda e, j=j, t0=t0: e.dma_start(xs(x_in, j), xT[j * 128:(j + 1) * 128, t0:t0 + TW]),
                   wait=[P.now("v")], inc="sx", inc_k=16)
        if it > 0:
            P.emit("scalar", lambda e: e.copy(h1b[0][:, 0:1], h1b[0][:, TW:TW + 1]),
                   wait=[P.now("v")], inc="a")
            for j in range(NCH):
                if j > 0:
                    P.emit("scalar", lambda e, j=j: e.copy(h1b[j][:, 0:1], h1b[j][:, TW:TW + 1]), inc="a")
                P.emit("scalar", lambda e, j=j: e.copy(h2b[j][:, 0:1], h2b[j][:, TW:TW + 1]), inc="a")
                P.emit("scalar", lambda e, j=j: e.copy(ekb[j][:, 0:1], ekb[j][:, TW:TW + 1]), inc="a")
            for j in range(NCH):
                P.emit("vector", lambda e, j=j: e.tensor_copy(zb[j][:, 0:1], zb[j][:, TW:TW + 1]),
                       wait=[P.now("so")], inc="v")
        P.emit("vector", lambda e: None, wait=["sx"])
        P.emit("scalar", lambda e: None, wait=["sx"])

        ln_stats(x_in)
        ln_apply(x_in, lambda j: xs(xres, j), "ln0_w", "ln0_b")
        tap("x0", t0, lambda j: xs(xres, j))
        ln_stats(xres)
        ln_apply(xres, lambda j: h1b[j][:, 1:TW + 1], "ln1_w", "ln1_b")

        tap("h1", t0, lambda j: h1b[j][:, 1:TW + 1], bf=True)
        for j in range(NCH):
            P.emit("vector", lambda e, j=j: e.tensor_tensor(dl[:, :], h1b[j][:, 1:TW + 1], h1b[j][:, 0:TW], ALU.subtract),
                   wait=(["a", P.now("t")] if j == 0 else ["a"]), inc="v")
            P.emit("vector", lambda e, j=j: e.scalar_tensor_tensor(xs(xkb, j), dl[:, :], vap("tmk", j), h1b[j][:, 0:TW], ALU.mult, ALU.add), inc="v")
            P.emit("vector", lambda e, j=j: e.scalar_tensor_tensor(xs(xvb, j), dl[:, :], vap("tmv", j), h1b[j][:, 0:TW], ALU.mult, ALU.add), inc="v")
            P.emit("vector", lambda e, j=j: e.scalar_tensor_tensor(xs(xrb, j), dl[:, :], vap("tmr", j), h1b[j][:, 0:TW], ALU.mult, ALU.add), inc="v")

        tap("xk", t0, lambda j: xs(xkb, j), bf=True)
        tap("xv", t0, lambda j: xs(xvb, j), bf=True)
        tap("xr", t0, lambda j: xs(xrb, j), bf=True)
        def k_cons(m, bank):
            return P.emit("scalar", lambda e, m=m, bank=bank: e.activation(ekb[m][:, 1:TW + 1], bank[:, 0:TW], AF.Exp, bias=0.0, scale=1.0),
                          wait=["t"], inc="a")
        proj("s_k", xkb, (pb[0], pb[1]), k_cons)

        def v_cons(m, bank):
            return P.emit("vector", lambda e, m=m, bank=bank: e.tensor_tensor(zb[m][:, 1:TW + 1], ekb[m][:, 1:TW + 1], bank[:, 0:TW], ALU.mult),
                          wait=["t", "a"], inc="v")
        proj("s_v", xvb, (pb[2], pb[3]), v_cons)

        tap("ek", t0, lambda j: ekb[j][:, 1:TW + 1])
        tap("z", t0, lambda j: zb[j][:, 1:TW + 1])
        scan_marks = []
        for j in range(NCH):
            P.emit("vector", lambda e, j=j: e.tensor_tensor_scan(xs(PA, j), bass.AP(dvec, j, [[8, 128], [0, TW]]), zb[j][:, 0:TW], hist[:, 4 * j:4 * j + 1], ALU.mult, ALU.add),
                   wait=(["a"] if j == 0 else []), inc="v")
            P.emit("vector", lambda e, j=j: e.tensor_tensor_scan(xs(PB, j), bass.AP(dvec, j, [[8, 128], [0, TW]]), ekb[j][:, 0:TW], hist[:, 4 * j + 1:4 * j + 2], ALU.mult, ALU.add),
                   inc="v")
            P.emit("vector", lambda e, j=j: e.tensor_copy(hist[:, 4 * j:4 * j + 1], PA[:, j * TW + TW - 1:j * TW + TW]), inc="v")
            mk = P.emit("vector", lambda e, j=j: e.tensor_copy(hist[:, 4 * j + 1:4 * j + 2], PB[:, j * TW + TW - 1:j * TW + TW]), inc="v")
            scan_marks.append(mk)

        tap("PA", t0, lambda j: xs(PA, j))
        tap("PB", t0, lambda j: xs(PB, j))
        def r_cons(m, bank):
            P.emit("vector", lambda e, m=m: e.scalar_tensor_tensor(pool[0][:, :], zb[m][:, 1:TW + 1], euv[:, m:m + 1], xs(PA, m), ALU.mult, ALU.add),
                   wait=[scan_marks[m], "a"], inc="v")
            P.emit("vector", lambda e, m=m: e.scalar_tensor_tensor(pool[1][:, :], ekb[m][:, 1:TW + 1], euv[:, m:m + 1], xs(PB, m), ALU.mult, ALU.add),
                   inc="v")
            P.emit("scalar", lambda e: e.activation(pool[2][:, :], pool[1][:, :], AF.Ln, bias=0.0, scale=1.0),
                   wait=["v"], inc="a")
            P.emit("scalar", lambda e: e.activation(pool[3][:, :], pool[2][:, :], AF.Exp, bias=0.0, scale=-1.0), inc="a")
            P.emit("scalar", lambda e, bank=bank: e.activation(pool[4][:, :], bank[:, 0:TW], AF.Exp, bias=0.0, scale=-1.0),
                   wait=["t"], inc="a")
            P.emit("scalar", lambda e: e.activation(pool[5][:, :], pool[4][:, :], AF.Ln, bias=1.0, scale=1.0), inc="a")
            P.emit("scalar", lambda e: e.activation(pool[4][:, :], pool[5][:, :], AF.Exp, bias=0.0, scale=-1.0), inc="a")
            P.emit("vector", lambda e: e.tensor_tensor(pool[0][:, :], pool[0][:, :], pool[3][:, :], ALU.mult),
                   wait=["a"], inc="v")
            return P.emit("vector", lambda e, m=m: e.tensor_tensor(xs(xkb, m), pool[4][:, :], pool[0][:, :], ALU.mult), inc="v")
        proj("s_r", xrb, (pb[4], pb[5]), r_cons)

        tap("rwkvin", t0, lambda j: xs(xkb, j), bf=True)
        def o_cons(m, bank):
            return P.emit("vector", lambda e, m=m, bank=bank: e.tensor_tensor(xs(xres, m), xs(xres, m), bank[:, 0:TW], ALU.add),
                          wait=["t"], inc="v")
        proj("s_o", xkb, (pb[0], pb[1]), o_cons)

        tap("x1", t0, lambda j: xs(xres, j))
        ln_stats(xres)
        ln_apply(xres, lambda j: h2b[j][:, 1:TW + 1], "ln2_w", "ln2_b")

        for j in range(NCH):
            P.emit("vector", lambda e, j=j: e.tensor_tensor(dl[:, :], h2b[j][:, 1:TW + 1], h2b[j][:, 0:TW], ALU.subtract),
                   wait=(["a", P.now("t")] if j == 0 else ["a"]), inc="v")
            P.emit("vector", lambda e, j=j: e.scalar_tensor_tensor(xs(xkb, j), dl[:, :], vap("cmk", j), h2b[j][:, 0:TW], ALU.mult, ALU.add), inc="v")
            P.emit("vector", lambda e, j=j: e.scalar_tensor_tensor(xs(xrb, j), dl[:, :], vap("cmr", j), h2b[j][:, 0:TW], ALU.mult, ALU.add), inc="v")

        def fr_cons(m, bank):
            P.emit("scalar", lambda e, bank=bank: e.activation(pool[2][:, :], bank[:, 0:TW], AF.Exp, bias=0.0, scale=-1.0),
                   wait=["t", P.now("v")], inc="a")
            P.emit("scalar", lambda e: e.activation(pool[2][:, :], pool[2][:, :], AF.Ln, bias=1.0, scale=1.0), inc="a")
            return P.emit("scalar", lambda e, m=m: e.activation(xs(sig2, m), pool[2][:, :], AF.Exp, bias=0.0, scale=-1.0), inc="a")
        proj("s_fr", xrb, (pb[2], pb[3]), fr_cons)

        for hw in range(2):
            col0 = hw * HW
            kftt_marks = {}
            fk_tmarks = {}
            for m in range(NHC):
                wx = [P.now("t")] if m < 2 else [fk_tmarks[m - 2]]
                buf, rdy = stream_w("s_fk", fkw, m, NCH * 128, wx, engine="gpsimd", kind="fk")
                bank = pb[m % 2]
                waits = [rdy]
                if m < 2:
                    waits += [P.now("v"), P.now("a")]
                else:
                    waits.append(kftt_marks[m - 2])
                for j in range(NCH):
                    tm = P.emit("tensor", lambda e, j=j, buf=buf, bank=bank, col0=col0:
                                e.matmul(bank[:, 0:HW], buf[:, j * 128:(j + 1) * 128], xkb[:, j * TW + col0:j * TW + col0 + HW],
                                         start=(j == 0), stop=(j == NCH - 1)),
                                wait=(waits if j == 0 else []), inc=("t" if j == NCH - 1 else None))
                fk_tmarks[m] = tm
                rslot = pool[2 + (m % 2)]
                P.emit("scalar", lambda e, bank=bank, rslot=rslot: e.activation(rslot[:, 0:HW], bank[:, 0:HW], AF.Relu, bias=0.0, scale=1.0),
                       wait=(["t"] if m < 2 else ["t", kftt_marks[m - 2]]), inc="a")
                kftt_marks[m] = P.emit("vector", lambda e, m=m, bank=bank, rslot=rslot:
                                       e.tensor_tensor(kf[:, m * HW:(m + 1) * HW], bank[:, 0:HW], rslot[:, 0:HW], ALU.mult),
                                       wait=["a"], inc="v")
            kv_marks = {}
            fv_tmarks = {}
            for m in range(NCH):
                wx = [P.now("t")] if m < 2 else [fv_tmarks[m - 2]]
                buf, rdy = stream_w("s_fv", fvw, m, NHC * 128, wx, kind="fv")
                bank = pb[2 + (m % 2)]
                waits = [rdy, P.now("v")]
                if m >= 2:
                    waits.append(kv_marks[m - 2])
                for o in range(NHC):
                    tm = P.emit("tensor", lambda e, o=o, buf=buf, bank=bank:
                                e.matmul(bank[:, 0:HW], buf[:, o * 128:(o + 1) * 128], kf[:, o * HW:(o + 1) * HW],
                                         start=(o == 0), stop=(o == NHC - 1)),
                                wait=(waits if o == 0 else []), inc=("t" if o == NHC - 1 else None))
                fv_tmarks[m] = tm
                tslot = pool[4 + (m % 2)]
                P.emit("vector", lambda e, m=m, bank=bank, col0=col0, tslot=tslot:
                       e.tensor_tensor(tslot[:, 0:HW], sig2[:, m * TW + col0:m * TW + col0 + HW], bank[:, 0:HW], ALU.mult),
                       wait=["t"], inc="v")
                kv_marks[m] = P.emit("vector", lambda e, m=m, col0=col0, tslot=tslot:
                                     e.tensor_tensor(xres[:, m * TW + col0:m * TW + col0 + HW], xres[:, m * TW + col0:m * TW + col0 + HW], tslot[:, 0:HW], ALU.add),
                                     inc="v")

        tap("x2", t0, lambda j: xs(xres, j))
        tap("sig2", t0, lambda j: xs(sig2, j), bf=True)
        for j in range(NCH):
            P.emit("vector", lambda e, j=j: e.tensor_tensor(xs(PB, j), xs(x_in, j), xs(xres, j), ALU.add), inc="v")

        ln_stats(PB)
        ln_apply(PB, lambda j: xs(PA, j))

        tap("y1", t0, lambda j: xs(PA, j))
        for g in range(4):
            for j in range(NCH):
                P.emit("tensor", lambda e, g=g, j=j: e.transpose(pT[:, j * 128:(j + 1) * 128], PA[:, j * TW + g * 128:j * TW + (g + 1) * 128], idf[:, :]),
                       wait=([P.now("v")] if j == 0 else []), inc=("t" if j == NCH - 1 else None))
            P.emit("vector", lambda e, g=g: e.tensor_reduce(am4[:, g:g + 1], pT[:, 0:1024], AX.X, ALU.max, apply_absolute_value=True),
                   wait=["t"], inc="v")
        P.emit("scalar", lambda e: e.activation(amaxR[:, 0:4], am4[:, 0:4], AF.Ln, bias=0.0, scale=1.0),
               wait=["v"], inc="a")
        P.emit("scalar", lambda e: e.activation(am4[:, 4:8], amaxR[:, 0:4], AF.Exp, bias=cap(CI_LN127), scale=-1.0), inc="a")
        P.emit("scalar", lambda e: e.activation(am4[:, 0:4], amaxR[:, 0:4], AF.Exp, bias=cap(CI_NLN127), scale=1.0), inc="a")
        P.emit("vector", lambda e: None, wait=["a"])
        P.emit("tensor", lambda e: e.transpose(pT[0:8, 0:128], am4[:, 0:8], idf[:, :]), wait=["v", "a"], inc="t")
        P.emit("vector", lambda e: e.tensor_copy(prT[:, :], pT[0:8, 0:128]), wait=["t"], inc="v")
        for g in range(4):
            P.emit("tensor", lambda e, g=g: e.matmul(pT[:, g * 128:(g + 1) * 128], self8[:, (4 + g) * 128:(5 + g) * 128], prT[:, :], start=True, stop=True),
                   wait=(["v"] if g == 0 else []), inc=None)
            P.emit("tensor", lambda e, g=g: e.matmul(pT[:, TW + g * 128:TW + (g + 1) * 128], self8[:, g * 128:(g + 1) * 128], prT[:, :], start=True, stop=True),
                   inc=("t" if g == 3 else None))
        P.emit("vector", lambda e: e.tensor_copy(pool[0][:, :], pT[:, TW:2 * TW]), wait=["t"], inc="v")
        if dbg in ("rinvbc", "am4"):
            if dbg == "rinvbc":
                P.emit("vector", lambda e: e.tensor_copy(pool[2][:, :], pT[:, 0:TW]), wait=["t"], inc="v")
            else:
                P.emit("vector", lambda e: e.tensor_copy(pool[3][:, 0:8], am4[:, :]), inc="v")
                P.emit("vector", lambda e: e.tensor_copy(pool[3][:, 8:16], amaxR[:, 0:8]), inc="v")
                P.emit("vector", lambda e: e.tensor_copy(pool[3][:, 16:24], pool[2][:, 0:8]), inc="v")
            P.emit("sync", lambda e, t0=t0, _src=(2 if dbg == "rinvbc" else 3): e.dma_start(dbgF[0:128, t0:t0 + TW], pool[_src][:, :]),
                   wait=[P.now("v")], inc="so", inc_k=16)
            P.emit("vector", lambda e: None, wait=["so"])
        for j in range(NCH):
            P.emit("vector", lambda e, j=j: e.tensor_tensor(qi8[:, 0:TW], xs(PA, j), pT[:, 0:TW], ALU.mult),
                   wait=([P.now("t")] if j == 0 else []), inc="v")
            P.emit("vector", lambda e, j=j: e.tensor_copy(xs(xkb, j), qi8[:, 0:TW]), inc="v")

        tap("qbf", t0, lambda j: xs(xkb, j), bf=True)
        def ffn_cons(m, bank):
            P.emit("vector", lambda e, m=m, bank=bank: e.scalar_tensor_tensor(pool[3][:, :], bank[:, 0:TW], srhs[:, m:m + 1], pool[0][:, :], ALU.mult, ALU.mult),
                   wait=["t"], inc="v")
            return P.emit("vector", lambda e, m=m: e.tensor_tensor(xs(xres, m), xs(PA, m), pool[3][:, :], ALU.add), inc="v")
        proj("s_q", xkb, (pb[0], pb[1]), ffn_cons)

        tap("s2", t0, lambda j: xs(xres, j))
        ln_stats(xres)
        for j in range(NCH):
            P.emit("vector", lambda e, j=j: e.tensor_tensor(pool[4][:, :], xs(xres, j), pool[1][:, :], ALU.subtract),
                   wait=(["a"] if j == 0 else []), inc="v")
            P.emit("vector", lambda e, j=j: e.tensor_tensor(zb[j][:, 0:TW], pool[4][:, :], pool[0][:, :], ALU.mult), inc="v")
            P.emit("sync", lambda e, j=j, t0=t0: e.dma_start(outT[j * 128:(j + 1) * 128, t0:t0 + TW], zb[j][:, 0:TW]),
                   wait=["v"], inc="so", inc_k=16)

    P.emit("sync", lambda e: None, wait=["so"])

    with nc.Block() as block:
        @block.sync
        def _(e):
            P.run(e, "sync", sems)

        @block.gpsimd
        def _(e):
            P.run(e, "gpsimd", sems)

        @block.tensor
        def _(e):
            P.run(e, "tensor", sems)

        @block.vector
        def _(e):
            P.run(e, "vector", sems)

        @block.scalar
        def _(e):
            P.run(e, "scalar", sems)

    return nc, ctx


def kernel(**inputs):
    per_core = host_prep(inputs)
    nc, ctx = build_program()
    res = run_bass_kernel_spmd(nc, per_core, core_ids=list(range(8)))
    outs = [np.ascontiguousarray(r["outT"].T) for r in res.results]
    return np.stack(outs, axis=0).astype(np.float32)


# version 06
# speedup vs baseline: 1.4693x; 1.0156x over previous
import numpy as np
from contextlib import ExitStack

import concourse.bass as bass
import concourse.mybir as mybir
from concourse.bass_utils import run_bass_kernel_spmd

F32 = mybir.dt.float32
BF16 = mybir.dt.bfloat16
I8 = mybir.dt.int8
ALU = mybir.AluOpType
AF = mybir.ActivationFunctionType
AX = mybir.AxisListType

C = 1024
T = 4096
H = 4096
NCH = C // 128
NHC = H // 128
TW = 512
NT = T // TW
HW = TW // 2
EPS = 1e-5

ENGINES = ("sync", "gpsimd", "tensor", "vector", "scalar")
CI_INVC, CI_EPS, CI_NEG1, CI_127, CI_INV127, CI_LN127, CI_NLN127, CI_ONE = range(8)


class Plan:

    def __init__(self):
        self.items = {e: [] for e in ENGINES}
        self.cnt = {}
        self.seen = {e: {} for e in ENGINES}

    def sem(self, name):
        self.cnt[name] = 0

    def now(self, s):
        return (s, self.cnt[s])

    def emit(self, eng, build, wait=(), inc=None, inc_k=1):
        waits = []
        for w_ in wait:
            s, v = w_ if isinstance(w_, tuple) else (w_, self.cnt[w_])
            if v <= 0 or self.seen[eng].get(s, -1) >= v:
                continue
            self.seen[eng][s] = v
            waits.append((s, v))
        if inc is not None:
            self.cnt[inc] += inc_k
        self.items[eng].append((waits, build, inc, inc_k))
        return (inc, self.cnt[inc]) if inc is not None else None

    def run(self, e, eng_name, sems):
        for waits, build, inc, inc_k in self.items[eng_name]:
            for s, v in waits:
                e.wait_ge(sems[s], v)
            ins = build(e) if build is not None else None
            if inc is not None:
                ins.then_inc(sems[inc], inc_k)


def host_prep(inputs):
    d = inputs
    x = np.asarray(d["L0_attn_x"], dtype=np.float32)
    B = x.shape[0]

    def vec8(v):
        return np.ascontiguousarray(np.asarray(v, np.float32).reshape(NCH, 128).T)

    def wproj(w):
        a = np.asarray(w, np.float32).T.reshape(NCH, 128, NCH, 128)
        return np.ascontiguousarray(a.transpose(2, 1, 0, 3).reshape(C, C))

    def wfk_prep(w):
        a = np.asarray(w, np.float32).T.reshape(NCH, 128, NHC, 128)
        return np.ascontiguousarray(a.transpose(2, 1, 0, 3).reshape(H, C))

    def wfv_prep(w):
        a = np.asarray(w, np.float32).T.reshape(NHC, 128, NCH, 128)
        return np.ascontiguousarray(a.transpose(2, 1, 0, 3).reshape(C, H))

    vecs = np.concatenate([
        vec8(d["L0_attn_ln0_w"]), vec8(d["L0_attn_ln0_b"]),
        vec8(d["L0_attn_ln1_w"]), vec8(d["L0_attn_ln1_b"]),
        vec8(d["L0_attn_ln2_w"]), vec8(d["L0_attn_ln2_b"]),
        vec8(d["L0_attn_time_decay"]), vec8(d["L0_attn_time_first"]),
        vec8(d["L0_attn_tmk"]), vec8(d["L0_attn_tmv"]), vec8(d["L0_attn_tmr"]),
        vec8(d["L0_attn_cmk"]), vec8(d["L0_attn_cmr"]),
    ], axis=1)

    csts = np.zeros((128, 8), np.float32)
    csts[:, CI_INVC] = 1.0 / C
    csts[:, CI_EPS] = EPS
    csts[:, CI_NEG1] = -1.0
    csts[:, CI_127] = 127.0
    csts[:, CI_INV127] = 1.0 / 127.0
    csts[:, CI_LN127] = np.log(127.0)
    csts[:, CI_NLN127] = -np.log(127.0)
    csts[:, CI_ONE] = 1.0

    sel8 = np.zeros((8, 8 * 128), np.float32)
    for j in range(8):
        sel8[j, j * 128:(j + 1) * 128] = 1.0

    shared = {
        "w_k": wproj(d["L0_attn_Wk"]), "w_v": wproj(d["L0_attn_Wv"]),
        "w_r": wproj(d["L0_attn_Wr"]), "w_o": wproj(d["L0_attn_Wo"]),
        "w_fr": wproj(d["L0_attn_Wfr"]),
        "w_fk": wfk_prep(d["L0_attn_Wfk"]), "w_fv": wfv_prep(d["L0_attn_Wfv"]),
        "rhsT": np.ascontiguousarray(np.asarray(d["L0_ffn_rhs"], np.float32).T),
        "vecs": vecs, "csts": csts,
        "ones": np.ones((128, 128), np.float32),
        "ident": np.eye(128, dtype=np.float32),
        "sel8": sel8,
    }
    per_core = []
    for b in range(B):
        m = dict(shared)
        m["xT"] = np.ascontiguousarray(x[b].T)
        per_core.append(m)
    return per_core


def build_program(dbg=None):
    nc = bass.Bass("TRN2", target_bir_lowering=False, detect_race_conditions=False)
    ctx = ExitStack()

    xT = nc.dram_tensor("xT", [C, T], F32, kind="ExternalInput")
    outT = nc.dram_tensor("outT", [C, T], F32, kind="ExternalOutput")
    dbgF = nc.dram_tensor("dbgF", [C, T], F32, kind="ExternalOutput")
    dbgB = nc.dram_tensor("dbgB", [C, T], BF16, kind="ExternalOutput")
    din = {}
    for n, shp in [("w_k", [C, C]), ("w_v", [C, C]), ("w_r", [C, C]),
                   ("w_o", [C, C]), ("w_fr", [C, C]), ("w_fk", [H, C]),
                   ("w_fv", [C, H]), ("rhsT", [C, C]),
                   ("vecs", [128, 13 * 8]), ("csts", [128, 8]),
                   ("ones", [128, 128]), ("ident", [128, 128]),
                   ("sel8", [8, 8 * 128])]:
        din[n] = nc.dram_tensor(n, shp, F32, kind="ExternalInput")
    scr = {}
    for n, shp in [("s_k", [C, C]), ("s_v", [C, C]), ("s_r", [C, C]),
                   ("s_o", [C, C]), ("s_fr", [C, C]), ("s_fk", [H, C]),
                   ("s_fv", [C, H]), ("s_q", [C, C])]:
        scr[n] = nc.dram_tensor(n, shp, BF16, kind="Internal")

    def sb(name, shape, dtype=F32):
        return ctx.enter_context(nc.sbuf_tensor(name, shape, dtype))

    x_in = sb("x_in", [128, NCH * TW])
    xres = sb("xres", [128, NCH * TW])
    h1b = [sb(f"h1_{j}", [128, TW + 1], BF16) for j in range(NCH)]
    h2b = [sb(f"h2_{j}", [128, TW + 1], BF16) for j in range(NCH)]
    xkb = sb("xkb", [128, NCH * TW], BF16)
    xvb = sb("xvb", [128, NCH * TW], BF16)
    xrb = sb("xrb", [128, NCH * TW], BF16)
    sig2 = sb("sig2", [128, NCH * TW], BF16)
    ekb = [sb(f"ek_{j}", [128, TW + 1]) for j in range(NCH)]
    zb = [sb(f"z_{j}", [128, TW + 1]) for j in range(NCH)]
    PA = sb("PA", [128, NCH * TW])
    PB = sb("PB", [128, NCH * TW])
    kf = sb("kf", [128, NHC * HW], BF16)
    pool = [sb(f"pool{i}", [128, TW]) for i in range(6)]
    dl = sb("dl", [128, TW], BF16)
    vcs = sb("vcs", [128, 13 * 8])
    cst = sb("cst", [128, 8])
    dvec = sb("dvec", [128, 8])
    euv = sb("euv", [128, 8])
    srhs = sb("srhs", [128, 8])
    rrv = sb("rrv", [128, 8])
    hist = sb("hist", [128, 4 * NCH])
    am4 = sb("am4", [128, 8])
    amaxR = sb("amaxR", [128, 8])
    prT = sb("prT", [8, 128])
    onesb = sb("onesb", [128, 128], BF16)
    idf = sb("idf", [128, 128])
    idb = sb("idb", [128, 128], BF16)
    self8 = sb("self8", [8, 8 * 128])
    qi8 = sb("qi8", [128, 1024], I8)
    pw = [sb(f"pw{i}", [128, NCH * 128], BF16) for i in range(2)]
    fkw = [sb(f"fkw{i}", [128, NCH * 128], BF16) for i in range(2)]
    fvw = [sb(f"fvw{i}", [128, NHC * 128], BF16) for i in range(2)]
    def alias(name, shape, dtype, backing):
        h = nc.virtual_sbuf_tensor(name, shape, dtype)
        nc._tensor(h.name, list(h.shape), h.dtype, type="SB")
        mv = nc.lookup_mloc(h)
        mv.allocated = True
        mv.addr = nc.lookup_mloc(backing).addr
        return h

    pcf = [alias("pcf0", [128, 2048], F32, kf), alias("pcf1", [128, 2048], F32, xkb)]
    pcb = [alias("pcb0", [128, 2048], BF16, sig2), alias("pcb1", [128, 2048], BF16, xrb)]
    pi8 = alias("pi8", [128, 1024], I8, qi8)

    pb = [ctx.enter_context(nc.psum_tensor(f"pb{i}", [128, 512], F32)) for i in range(6)]
    pT = ctx.enter_context(nc.psum_tensor("pT", [128, 1024], F32))

    P = Plan()
    semnames = ["sx", "sw", "gw", "so", "t", "v", "a",
                "pw0", "pw1", "fk0", "fk1", "fv0", "fv1", "cl0", "cl1", "cg0", "cg1"]
    for s in semnames:
        P.sem(s)
    sems = {s: ctx.enter_context(nc.semaphore(s)) for s in semnames}

    def xs(buf, j, w=TW):
        return buf[:, j * w:(j + 1) * w]

    VC = {n: i for i, n in enumerate(
        ["ln0_w", "ln0_b", "ln1_w", "ln1_b", "ln2_w", "ln2_b",
         "td", "tf", "tmk", "tmv", "tmr", "cmk", "cmr"])}

    def vap(name, j):
        return vcs[:, VC[name] * 8 + j:VC[name] * 8 + j + 1]

    def cap(i):
        return cst[:, i:i + 1]

    P.emit("sync", lambda e: e.dma_start(vcs[:, :], din["vecs"][:, :]), inc="sx", inc_k=16)
    P.emit("sync", lambda e: e.dma_start(cst[:, :], din["csts"][:, :]), inc="sx", inc_k=16)
    P.emit("sync", lambda e: e.dma_start(idf[:, :], din["ident"][:, :]), inc="sx", inc_k=16)
    P.emit("sync", lambda e: e.dma_start(pcf[0][:, 0:128], din["ones"][:, :]), inc="sx", inc_k=16)
    P.emit("sync", lambda e: e.dma_start(self8[:, :], din["sel8"][:, :]), inc="sx", inc_k=16)
    P.emit("vector", lambda e: e.tensor_copy(onesb[:, :], pcf[0][:, 0:128]), wait=["sx"], inc="v")
    P.emit("vector", lambda e: e.tensor_copy(idb[:, :], idf[:, :]), inc="v")
    P.emit("scalar", lambda e: e.activation(am4[:, :], vcs[:, VC["td"] * 8:VC["td"] * 8 + 8], AF.Exp, bias=0.0, scale=1.0), wait=["sx"], inc="a")
    P.emit("scalar", lambda e: e.activation(dvec[:, :], am4[:, :], AF.Exp, bias=0.0, scale=-1.0), inc="a")
    P.emit("scalar", lambda e: e.activation(euv[:, :], vcs[:, VC["tf"] * 8:VC["tf"] * 8 + 8], AF.Exp, bias=0.0, scale=1.0), inc="a")
    for j in range(NCH):
        P.emit("vector", lambda e, j=j: e.memset(zb[j][:, 0:1], 0.0), inc="v")
        P.emit("vector", lambda e, j=j: e.memset(ekb[j][:, 0:1], 0.0), inc="v")
        P.emit("vector", lambda e, j=j: e.memset(h1b[j][:, 0:1], 0.0), inc="v")
        P.emit("vector", lambda e, j=j: e.memset(h2b[j][:, 0:1], 0.0), inc="v")
    P.emit("vector", lambda e: e.memset(hist[:, :], 0.0), inc="v")

    cast_jobs = []
    for n_in, n_scr, rows, cols in [
        ("w_k", "s_k", C, C), ("w_v", "s_v", C, C), ("w_r", "s_r", C, C),
        ("w_o", "s_o", C, C), ("w_fr", "s_fr", C, C),
        ("w_fk", "s_fk", H, C), ("w_fv", "s_fv", C, H),
    ]:
        for r0 in range(0, rows, 128):
            for c0 in range(0, cols, 2048):
                cast_jobs.append((n_in, n_scr, r0, c0, min(2048, cols - c0)))
    store_marks = []
    for par, (n_in, n_scr, r0, c0, cw) in enumerate(cast_jobs):
        b = par % 2
        waits = [P.now("v"), P.now("a")] if par < 2 else [store_marks[par - 2]]
        ld_mk = P.emit("sync", lambda e, n_in=n_in, r0=r0, c0=c0, cw=cw, b=b:
                       e.dma_start(pcf[b][:, 0:cw], din[n_in][r0:r0 + 128, c0:c0 + cw]),
                       wait=waits, inc=f"cl{b}", inc_k=16)
        P.emit("scalar", lambda e, b=b, cw=cw: e.copy(pcb[b][:, 0:cw], pcf[b][:, 0:cw]),
               wait=[ld_mk], inc="a")
        mk = P.emit("gpsimd", lambda e, n_scr=n_scr, r0=r0, c0=c0, cw=cw, b=b:
                    e.dma_start(scr[n_scr][r0:r0 + 128, c0:c0 + cw], pcb[b][:, 0:cw]),
                    wait=["a"], inc=f"cg{b}", inc_k=16)
        store_marks.append(mk)

    red_marks = {}
    for j in range(NCH):
        b = j % 2
        lw = [P.now("a")] if j < 2 else [red_marks[j - 2]]
        ld_mk = P.emit("sync", lambda e, j=j, b=b: e.dma_start(pcf[b][:, 0:1024], din["rhsT"][j * 128:(j + 1) * 128, :]),
               wait=lw, inc=f"cl{b}", inc_k=16)
        red_marks[j] = P.emit("vector", lambda e, j=j, b=b: e.tensor_reduce(amaxR[:, j:j + 1], pcf[b][:, 0:1024], AX.X, ALU.max, apply_absolute_value=True),
               wait=[ld_mk], inc="v")
    P.emit("scalar", lambda e: e.activation(pcf[0][:, 0:8], amaxR[:, :], AF.Ln, bias=0.0, scale=1.0),
           wait=["v", "sx"], inc="a")
    P.emit("scalar", lambda e: e.activation(rrv[:, :], pcf[0][:, 0:8], AF.Exp, bias=cap(CI_LN127), scale=-1.0), inc="a")
    P.emit("scalar", lambda e: e.activation(srhs[:, :], pcf[0][:, 0:8], AF.Exp, bias=cap(CI_NLN127), scale=1.0), inc="a")
    P.emit("vector", lambda e: None, wait=["a"])
    for j in range(NCH):
        BW = ["v", "a", "t", P.now("gw"), P.now("sw"), P.now("sx")]
        ld2 = P.emit("sync", lambda e, j=j: e.dma_start(pcf[0][:, 0:1024], din["rhsT"][j * 128:(j + 1) * 128, :]),
               wait=list(BW), inc="cl0", inc_k=16)
        P.emit("vector", lambda e, j=j: e.tensor_scalar(pi8[:, 0:1024], pcf[0][:, 0:1024], rrv[:, j:j + 1], None, ALU.mult),
               wait=[ld2, "a", "t", P.now("gw")], inc="v")
        P.emit("vector", lambda e: e.tensor_copy(pcf[1][:, 0:1024], pi8[:, 0:1024]),
               wait=[P.now("gw"), P.now("t")], inc="v")
        for m in range(NCH):
            P.emit("tensor", lambda e, m=m: e.transpose(pT[:, m * 128:(m + 1) * 128], pcf[1][:, m * 128:(m + 1) * 128], idf[:, :]),
                   wait=(["v", P.now("a"), P.now("gw")] if m == 0 else []), inc=("t" if m == NCH - 1 else None))
        P.emit("scalar", lambda e: e.copy(pcb[1][:, 0:1024], pT[:, 0:1024]),
               wait=["t", "v", P.now("gw")], inc="a")
        for m in range(NCH):
            P.emit("gpsimd", lambda e, j=j, m=m: e.dma_start(scr["s_q"][j * 128:(j + 1) * 128, m * 128:(m + 1) * 128], pcb[1][:, m * 128:(m + 1) * 128]),
                   wait=["a", "t", "v"], inc="gw", inc_k=16)
    setup_cg = (P.cnt["cg0"], P.cnt["cg1"])
    if dbg == "qrhs":
        for mm_ in range(NCH):
            P.emit("sync", lambda e, mm_=mm_: e.dma_start(pw[0][:, 0:1024], scr["s_q"][mm_ * 128:(mm_ + 1) * 128, :]),
                   wait=[P.now("gw"), ("cg0", P.cnt["cg0"]), ("cg1", P.cnt["cg1"]), P.now("so")], inc="sw", inc_k=16)
            P.emit("sync", lambda e, mm_=mm_: e.dma_start(dbgB[mm_ * 128:(mm_ + 1) * 128, 0:1024], pw[0][:, 0:1024]),
                   wait=["sw"], inc="so", inc_k=16)
        P.emit("sync", lambda e: e.dma_start(dbgF[0:128, 0:8], srhs[:, :]), wait=[P.now("a")], inc="so", inc_k=16)
        P.emit("sync", lambda e: e.dma_start(dbgF[0:128, 8:16], rrv[:, :]), inc="so", inc_k=16)
        P.emit("sync", lambda e: e.dma_start(dbgF[0:128, 16:24], amaxR[:, :]), wait=[P.now("v")], inc="so", inc_k=16)
    setup_gw = P.cnt["gw"]

    SETUP_DONE = [("gw", None), ("cg0", None), ("cg1", None)]

    def setup_done_waits():
        return [("gw", setup_gw), ("cg0", setup_cg[0]), ("cg1", setup_cg[1])]

    def stream_w(scr_name, buf_pair, m, ncols, wait_extra, engine="sync", kind="pw"):
        b = m % 2
        semn = f"{kind}{b}"

        def go(e, scr_name=scr_name, m=m, ncols=ncols, b=b):
            return e.dma_start(buf_pair[b][:, 0:ncols], scr[scr_name][m * 128:(m + 1) * 128, 0:ncols])
        mk = P.emit(engine, go, wait=setup_done_waits() + list(wait_extra), inc=semn, inc_k=16)
        return buf_pair[b], mk

    def proj(scr_name, rhs_buf, banks, consumer, extra_start=()):
        start_v = P.now("v")
        cons_marks = {}
        tmarks = {}
        for m in range(NCH):
            wx = [P.now("t")] if m < 2 else [tmarks[m - 2]]
            buf, rdy = stream_w(scr_name, pw, m, NCH * 128, wx)
            waits = [rdy, start_v] + list(extra_start)
            if m >= 2:
                waits.append(cons_marks[m - 2])
            bank = banks[m % 2]
            for j in range(NCH):
                tm = P.emit("tensor", lambda e, j=j, buf=buf, bank=bank:
                            e.matmul(bank[:, 0:TW], buf[:, j * 128:(j + 1) * 128], xs(rhs_buf, j),
                                     start=(j == 0), stop=(j == NCH - 1)),
                            wait=(waits if j == 0 else []), inc=("t" if j == NCH - 1 else None))
            tmarks[m] = tm
            cons_marks[m] = consumer(m, bank)

    def ln_stats(src):
        cmarks, qmarks = [], []
        wv0, wt0 = P.now("v"), P.now("t")
        for j in range(NCH):
            cmarks.append(P.emit("scalar", lambda e, j=j: e.copy(xs(xkb, j), xs(src, j)),
                                 wait=([wv0, wt0] if j == 0 else []), inc="a"))
        for j in range(NCH):
            qmarks.append(P.emit("vector", lambda e, j=j: e.tensor_tensor(xs(xvb, j), xs(xkb, j), xs(xkb, j), ALU.mult),
                                 wait=([cmarks[j], wt0] if j == 0 else [cmarks[j]]), inc="v"))
        for j in range(NCH):
            P.emit("tensor", lambda e, j=j: e.matmul(pb[4][:, 0:TW], onesb[:, :], xs(xkb, j), start=(j == 0), stop=(j == NCH - 1)),
                   wait=([cmarks[j], P.now("v")] if j == 0 else [cmarks[j]]), inc=("t" if j == NCH - 1 else None))
        for j in range(NCH):
            P.emit("tensor", lambda e, j=j: e.matmul(pb[5][:, 0:TW], onesb[:, :], xs(xvb, j), start=(j == 0), stop=(j == NCH - 1)),
                   wait=[qmarks[j]], inc=("t" if j == NCH - 1 else None))
        P.emit("vector", lambda e: e.tensor_scalar(pool[1][:, :], pb[4][:, 0:TW], cap(CI_INVC), None, ALU.mult),
               wait=["t"], inc="v")
        P.emit("vector", lambda e: e.tensor_scalar(pool[2][:, :], pb[5][:, 0:TW], cap(CI_INVC), cap(CI_EPS), ALU.mult, ALU.add),
               inc="v")
        P.emit("vector", lambda e: e.scalar_tensor_tensor(pool[3][:, :], pool[1][:, :], cap(CI_NEG1), pool[1][:, :], ALU.mult, ALU.mult),
               inc="v")
        P.emit("vector", lambda e: e.tensor_tensor(pool[2][:, :], pool[2][:, :], pool[3][:, :], ALU.add), inc="v")
        P.emit("scalar", lambda e: e.activation(pool[3][:, :], pool[2][:, :], AF.Ln, bias=0.0, scale=1.0),
               wait=["v"], inc="a")
        P.emit("scalar", lambda e: e.activation(pool[0][:, :], pool[3][:, :], AF.Exp, bias=0.0, scale=-0.5), inc="a")

    def ln_apply(src, dst_fn, wname=None, bname=None):
        for j in range(NCH):
            P.emit("vector", lambda e, j=j: e.tensor_tensor(pool[4][:, :], xs(src, j), pool[1][:, :], ALU.subtract),
                   wait=(["a"] if j == 0 else []), inc="v")
            if wname is None:
                P.emit("vector", lambda e, j=j: e.tensor_tensor(dst_fn(j), pool[4][:, :], pool[0][:, :], ALU.mult), inc="v")
            else:
                P.emit("vector", lambda e, j=j: e.scalar_tensor_tensor(pool[5][:, :], pool[4][:, :], vap(wname, j), pool[0][:, :], ALU.mult, ALU.mult),
                       inc="v")
                P.emit("vector", lambda e, j=j: e.tensor_scalar(dst_fn(j), pool[5][:, :], vap(bname, j), None, ALU.add), inc="v")

    def tap(name, t0, ap_fn, bf=False):
        if dbg != name:
            return
        dst = dbgB if bf else dbgF
        for j in range(NCH):
            P.emit("sync", lambda e, j=j, t0=t0: e.dma_start(dst[j * 128:(j + 1) * 128, t0:t0 + TW], ap_fn(j)),
                   wait=[P.now("v"), P.now("a"), P.now("t")], inc="so", inc_k=16)
        P.emit("vector", lambda e: None, wait=["so"])
        P.emit("scalar", lambda e: None, wait=["so"])
        P.emit("tensor", lambda e: None, wait=["so"])

    P.emit("vector", lambda e: None, wait=setup_done_waits() + [P.now("t"), P.now("a")])
    P.emit("scalar", lambda e: None, wait=setup_done_waits() + [P.now("t"), P.now("v")])
    P.emit("tensor", lambda e: None, wait=setup_done_waits())
    for it in range(NT):
        t0 = it * TW
        for j in range(NCH):
            P.emit("sync", lambda e, j=j, t0=t0: e.dma_start(xs(x_in, j), xT[j * 128:(j + 1) * 128, t0:t0 + TW]),
                   wait=[P.now("v")], inc="sx", inc_k=16)
        if it > 0:
            P.emit("scalar", lambda e: e.copy(h1b[0][:, 0:1], h1b[0][:, TW:TW + 1]),
                   wait=[P.now("v")], inc="a")
            for j in range(NCH):
                if j > 0:
                    P.emit("scalar", lambda e, j=j: e.copy(h1b[j][:, 0:1], h1b[j][:, TW:TW + 1]), inc="a")
                P.emit("scalar", lambda e, j=j: e.copy(h2b[j][:, 0:1], h2b[j][:, TW:TW + 1]), inc="a")
                P.emit("scalar", lambda e, j=j: e.copy(ekb[j][:, 0:1], ekb[j][:, TW:TW + 1]), inc="a")
            for j in range(NCH):
                P.emit("vector", lambda e, j=j: e.tensor_copy(zb[j][:, 0:1], zb[j][:, TW:TW + 1]),
                       wait=[P.now("so")], inc="v")
        P.emit("vector", lambda e: None, wait=["sx"])
        P.emit("scalar", lambda e: None, wait=["sx"])

        ln_stats(x_in)
        ln_apply(x_in, lambda j: xs(xres, j), "ln0_w", "ln0_b")
        tap("x0", t0, lambda j: xs(xres, j))
        ln_stats(xres)
        ln_apply(xres, lambda j: h1b[j][:, 1:TW + 1], "ln1_w", "ln1_b")

        tap("h1", t0, lambda j: h1b[j][:, 1:TW + 1], bf=True)
        for j in range(NCH):
            P.emit("vector", lambda e, j=j: e.tensor_tensor(dl[:, :], h1b[j][:, 1:TW + 1], h1b[j][:, 0:TW], ALU.subtract),
                   wait=(["a", P.now("t")] if j == 0 else ["a"]), inc="v")
            P.emit("vector", lambda e, j=j: e.scalar_tensor_tensor(xs(xkb, j), dl[:, :], vap("tmk", j), h1b[j][:, 0:TW], ALU.mult, ALU.add), inc="v")
            P.emit("vector", lambda e, j=j: e.scalar_tensor_tensor(xs(xvb, j), dl[:, :], vap("tmv", j), h1b[j][:, 0:TW], ALU.mult, ALU.add), inc="v")
            P.emit("vector", lambda e, j=j: e.scalar_tensor_tensor(xs(xrb, j), dl[:, :], vap("tmr", j), h1b[j][:, 0:TW], ALU.mult, ALU.add), inc="v")

        tap("xk", t0, lambda j: xs(xkb, j), bf=True)
        tap("xv", t0, lambda j: xs(xvb, j), bf=True)
        tap("xr", t0, lambda j: xs(xrb, j), bf=True)
        k_last = [None]

        def k_cons(m, bank):
            k_last[0] = P.emit("scalar", lambda e, m=m, bank=bank: e.activation(ekb[m][:, 1:TW + 1], bank[:, 0:TW], AF.Exp, bias=0.0, scale=1.0),
                               wait=["t"], inc="a")
            return k_last[0]
        proj("s_k", xkb, (pb[0], pb[1]), k_cons)

        def v_cons(m, bank):
            return P.emit("vector", lambda e, m=m, bank=bank: e.tensor_tensor(zb[m][:, 1:TW + 1], ekb[m][:, 1:TW + 1], bank[:, 0:TW], ALU.mult),
                          wait=["t", "a"], inc="v")
        proj("s_v", xvb, (pb[2], pb[3]), v_cons)

        tap("ek", t0, lambda j: ekb[j][:, 1:TW + 1])
        tap("z", t0, lambda j: zb[j][:, 1:TW + 1])
        scan_marks = []
        for j in range(NCH):
            P.emit("vector", lambda e, j=j: e.tensor_tensor_scan(xs(PA, j), bass.AP(dvec, j, [[8, 128], [0, TW]]), zb[j][:, 0:TW], hist[:, 4 * j:4 * j + 1], ALU.mult, ALU.add),
                   wait=(["a"] if j == 0 else []), inc="v")
            P.emit("vector", lambda e, j=j: e.tensor_tensor_scan(xs(PB, j), bass.AP(dvec, j, [[8, 128], [0, TW]]), ekb[j][:, 0:TW], hist[:, 4 * j + 1:4 * j + 2], ALU.mult, ALU.add),
                   inc="v")
            P.emit("vector", lambda e, j=j: e.tensor_copy(hist[:, 4 * j:4 * j + 1], PA[:, j * TW + TW - 1:j * TW + TW]), inc="v")
            mk = P.emit("vector", lambda e, j=j: e.tensor_copy(hist[:, 4 * j + 1:4 * j + 2], PB[:, j * TW + TW - 1:j * TW + TW]), inc="v")
            scan_marks.append(mk)

        tap("PA", t0, lambda j: xs(PA, j))
        tap("PB", t0, lambda j: xs(PB, j))
        def r_cons(m, bank):
            pn = pool[0 + 3 * (m % 2)]
            pd = pool[1 + 3 * (m % 2)]
            ps_ = pool[2 + 3 * (m % 2)]
            nmark = P.emit("vector", lambda e, m=m, pn=pn: e.scalar_tensor_tensor(pn[:, :], zb[m][:, 1:TW + 1], euv[:, m:m + 1], xs(PA, m), ALU.mult, ALU.add),
                           wait=[scan_marks[m]], inc="v")
            dmark = P.emit("vector", lambda e, m=m, pd=pd: e.scalar_tensor_tensor(pd[:, :], ekb[m][:, 1:TW + 1], euv[:, m:m + 1], xs(PB, m), ALU.mult, ALU.add),
                           inc="v")
            P.emit("scalar", lambda e, pd=pd: e.activation(pd[:, :], pd[:, :], AF.Ln, bias=0.0, scale=1.0),
                   wait=[dmark], inc="a")
            P.emit("scalar", lambda e, pd=pd: e.activation(pd[:, :], pd[:, :], AF.Exp, bias=0.0, scale=-1.0), inc="a")
            P.emit("scalar", lambda e, bank=bank, ps_=ps_: e.activation(ps_[:, :], bank[:, 0:TW], AF.Exp, bias=0.0, scale=-1.0),
                   wait=["t"], inc="a")
            P.emit("scalar", lambda e, ps_=ps_: e.activation(ps_[:, :], ps_[:, :], AF.Ln, bias=1.0, scale=1.0), inc="a")
            amark = P.emit("scalar", lambda e, ps_=ps_: e.activation(ps_[:, :], ps_[:, :], AF.Exp, bias=0.0, scale=-1.0), inc="a")
            P.emit("vector", lambda e, pn=pn, pd=pd: e.tensor_tensor(pn[:, :], pn[:, :], pd[:, :], ALU.mult),
                   wait=[amark], inc="v")
            return P.emit("vector", lambda e, m=m, pn=pn, ps_=ps_: e.tensor_tensor(xs(xkb, m), ps_[:, :], pn[:, :], ALU.mult), inc="v")
        proj("s_r", xrb, (pb[4], pb[5]), r_cons)

        tap("rwkvin", t0, lambda j: xs(xkb, j), bf=True)
        def o_cons(m, bank):
            return P.emit("vector", lambda e, m=m, bank=bank: e.tensor_tensor(xs(xres, m), xs(xres, m), bank[:, 0:TW], ALU.add),
                          wait=["t"], inc="v")
        proj("s_o", xkb, (pb[0], pb[1]), o_cons, extra_start=[k_last[0]])

        tap("x1", t0, lambda j: xs(xres, j))
        ln_stats(xres)
        ln_apply(xres, lambda j: h2b[j][:, 1:TW + 1], "ln2_w", "ln2_b")

        for j in range(NCH):
            P.emit("vector", lambda e, j=j: e.tensor_tensor(dl[:, :], h2b[j][:, 1:TW + 1], h2b[j][:, 0:TW], ALU.subtract),
                   wait=(["a", P.now("t")] if j == 0 else ["a"]), inc="v")
            P.emit("vector", lambda e, j=j: e.scalar_tensor_tensor(xs(xkb, j), dl[:, :], vap("cmk", j), h2b[j][:, 0:TW], ALU.mult, ALU.add), inc="v")
            P.emit("vector", lambda e, j=j: e.scalar_tensor_tensor(xs(xrb, j), dl[:, :], vap("cmr", j), h2b[j][:, 0:TW], ALU.mult, ALU.add), inc="v")

        def fr_cons(m, bank):
            P.emit("scalar", lambda e, bank=bank: e.activation(pool[2][:, :], bank[:, 0:TW], AF.Exp, bias=0.0, scale=-1.0),
                   wait=["t", P.now("v")], inc="a")
            P.emit("scalar", lambda e: e.activation(pool[2][:, :], pool[2][:, :], AF.Ln, bias=1.0, scale=1.0), inc="a")
            return P.emit("scalar", lambda e, m=m: e.activation(xs(sig2, m), pool[2][:, :], AF.Exp, bias=0.0, scale=-1.0), inc="a")
        proj("s_fr", xrb, (pb[2], pb[3]), fr_cons)

        for hw in range(2):
            col0 = hw * HW
            kftt_marks = {}
            fk_tmarks = {}
            for m in range(NHC):
                wx = [P.now("t")] if m < 2 else [fk_tmarks[m - 2]]
                buf, rdy = stream_w("s_fk", fkw, m, NCH * 128, wx, engine="gpsimd", kind="fk")
                bank = pb[m % 2]
                waits = [rdy]
                if m < 2:
                    waits += [P.now("v"), P.now("a")]
                else:
                    waits.append(kftt_marks[m - 2])
                for j in range(NCH):
                    tm = P.emit("tensor", lambda e, j=j, buf=buf, bank=bank, col0=col0:
                                e.matmul(bank[:, 0:HW], buf[:, j * 128:(j + 1) * 128], xkb[:, j * TW + col0:j * TW + col0 + HW],
                                         start=(j == 0), stop=(j == NCH - 1)),
                                wait=(waits if j == 0 else []), inc=("t" if j == NCH - 1 else None))
                fk_tmarks[m] = tm
                rslot = pool[2 + (m % 2)]
                P.emit("scalar", lambda e, bank=bank, rslot=rslot: e.activation(rslot[:, 0:HW], bank[:, 0:HW], AF.Relu, bias=0.0, scale=1.0),
                       wait=(["t"] if m < 2 else ["t", kftt_marks[m - 2]]), inc="a")
                kftt_marks[m] = P.emit("vector", lambda e, m=m, bank=bank, rslot=rslot:
                                       e.tensor_tensor(kf[:, m * HW:(m + 1) * HW], bank[:, 0:HW], rslot[:, 0:HW], ALU.mult),
                                       wait=["a"], inc="v")
            kv_marks = {}
            fv_tmarks = {}
            for m in range(NCH):
                wx = [P.now("t")] if m < 2 else [fv_tmarks[m - 2]]
                buf, rdy = stream_w("s_fv", fvw, m, NHC * 128, wx, kind="fv")
                bank = pb[2 + (m % 2)]
                waits = [rdy, P.now("v")]
                if m >= 2:
                    waits.append(kv_marks[m - 2])
                for o in range(NHC):
                    tm = P.emit("tensor", lambda e, o=o, buf=buf, bank=bank:
                                e.matmul(bank[:, 0:HW], buf[:, o * 128:(o + 1) * 128], kf[:, o * HW:(o + 1) * HW],
                                         start=(o == 0), stop=(o == NHC - 1)),
                                wait=(waits if o == 0 else []), inc=("t" if o == NHC - 1 else None))
                fv_tmarks[m] = tm
                tslot = pool[4 + (m % 2)]
                P.emit("vector", lambda e, m=m, bank=bank, col0=col0, tslot=tslot:
                       e.tensor_tensor(tslot[:, 0:HW], sig2[:, m * TW + col0:m * TW + col0 + HW], bank[:, 0:HW], ALU.mult),
                       wait=["t"], inc="v")
                kv_marks[m] = P.emit("vector", lambda e, m=m, col0=col0, tslot=tslot:
                                     e.tensor_tensor(xres[:, m * TW + col0:m * TW + col0 + HW], xres[:, m * TW + col0:m * TW + col0 + HW], tslot[:, 0:HW], ALU.add),
                                     inc="v")

        tap("x2", t0, lambda j: xs(xres, j))
        tap("sig2", t0, lambda j: xs(sig2, j), bf=True)
        for j in range(NCH):
            P.emit("vector", lambda e, j=j: e.tensor_tensor(xs(PB, j), xs(x_in, j), xs(xres, j), ALU.add), inc="v")

        ln_stats(PB)
        ln_apply(PB, lambda j: xs(PA, j))

        tap("y1", t0, lambda j: xs(PA, j))
        for g in range(4):
            for j in range(NCH):
                P.emit("tensor", lambda e, g=g, j=j: e.transpose(pT[:, j * 128:(j + 1) * 128], PA[:, j * TW + g * 128:j * TW + (g + 1) * 128], idf[:, :]),
                       wait=([P.now("v")] if j == 0 else []), inc=("t" if j == NCH - 1 else None))
            P.emit("vector", lambda e, g=g: e.tensor_reduce(am4[:, g:g + 1], pT[:, 0:1024], AX.X, ALU.max, apply_absolute_value=True),
                   wait=["t"], inc="v")
        P.emit("scalar", lambda e: e.activation(amaxR[:, 0:4], am4[:, 0:4], AF.Ln, bias=0.0, scale=1.0),
               wait=["v"], inc="a")
        P.emit("scalar", lambda e: e.activation(am4[:, 4:8], amaxR[:, 0:4], AF.Exp, bias=cap(CI_LN127), scale=-1.0), inc="a")
        P.emit("scalar", lambda e: e.activation(am4[:, 0:4], amaxR[:, 0:4], AF.Exp, bias=cap(CI_NLN127), scale=1.0), inc="a")
        P.emit("vector", lambda e: None, wait=["a"])
        P.emit("tensor", lambda e: e.transpose(pT[0:8, 0:128], am4[:, 0:8], idf[:, :]), wait=["v", "a"], inc="t")
        P.emit("vector", lambda e: e.tensor_copy(prT[:, :], pT[0:8, 0:128]), wait=["t"], inc="v")
        for g in range(4):
            P.emit("tensor", lambda e, g=g: e.matmul(pT[:, g * 128:(g + 1) * 128], self8[:, (4 + g) * 128:(5 + g) * 128], prT[:, :], start=True, stop=True),
                   wait=(["v"] if g == 0 else []), inc=None)
            P.emit("tensor", lambda e, g=g: e.matmul(pT[:, TW + g * 128:TW + (g + 1) * 128], self8[:, g * 128:(g + 1) * 128], prT[:, :], start=True, stop=True),
                   inc=("t" if g == 3 else None))
        P.emit("vector", lambda e: e.tensor_copy(pool[0][:, :], pT[:, TW:2 * TW]), wait=["t"], inc="v")
        if dbg in ("rinvbc", "am4"):
            if dbg == "rinvbc":
                P.emit("vector", lambda e: e.tensor_copy(pool[2][:, :], pT[:, 0:TW]), wait=["t"], inc="v")
            else:
                P.emit("vector", lambda e: e.tensor_copy(pool[3][:, 0:8], am4[:, :]), inc="v")
                P.emit("vector", lambda e: e.tensor_copy(pool[3][:, 8:16], amaxR[:, 0:8]), inc="v")
                P.emit("vector", lambda e: e.tensor_copy(pool[3][:, 16:24], pool[2][:, 0:8]), inc="v")
            P.emit("sync", lambda e, t0=t0, _src=(2 if dbg == "rinvbc" else 3): e.dma_start(dbgF[0:128, t0:t0 + TW], pool[_src][:, :]),
                   wait=[P.now("v")], inc="so", inc_k=16)
            P.emit("vector", lambda e: None, wait=["so"])
        for j in range(NCH):
            P.emit("vector", lambda e, j=j: e.tensor_tensor(qi8[:, 0:TW], xs(PA, j), pT[:, 0:TW], ALU.mult),
                   wait=([P.now("t")] if j == 0 else []), inc="v")
            P.emit("vector", lambda e, j=j: e.tensor_copy(xs(xkb, j), qi8[:, 0:TW]), inc="v")

        tap("qbf", t0, lambda j: xs(xkb, j), bf=True)
        def ffn_cons(m, bank):
            P.emit("vector", lambda e, m=m, bank=bank: e.scalar_tensor_tensor(pool[3][:, :], bank[:, 0:TW], srhs[:, m:m + 1], pool[0][:, :], ALU.mult, ALU.mult),
                   wait=["t"], inc="v")
            return P.emit("vector", lambda e, m=m: e.tensor_tensor(xs(xres, m), xs(PA, m), pool[3][:, :], ALU.add), inc="v")
        proj("s_q", xkb, (pb[0], pb[1]), ffn_cons)

        tap("s2", t0, lambda j: xs(xres, j))
        ln_stats(xres)
        for j in range(NCH):
            P.emit("vector", lambda e, j=j: e.tensor_tensor(pool[4][:, :], xs(xres, j), pool[1][:, :], ALU.subtract),
                   wait=(["a"] if j == 0 else []), inc="v")
            P.emit("vector", lambda e, j=j: e.tensor_tensor(zb[j][:, 0:TW], pool[4][:, :], pool[0][:, :], ALU.mult), inc="v")
            P.emit("sync", lambda e, j=j, t0=t0: e.dma_start(outT[j * 128:(j + 1) * 128, t0:t0 + TW], zb[j][:, 0:TW]),
                   wait=["v"], inc="so", inc_k=16)

    P.emit("sync", lambda e: None, wait=["so"])

    with nc.Block() as block:
        @block.sync
        def _(e):
            P.run(e, "sync", sems)

        @block.gpsimd
        def _(e):
            P.run(e, "gpsimd", sems)

        @block.tensor
        def _(e):
            P.run(e, "tensor", sems)

        @block.vector
        def _(e):
            P.run(e, "vector", sems)

        @block.scalar
        def _(e):
            P.run(e, "scalar", sems)

    return nc, ctx


def kernel(**inputs):
    per_core = host_prep(inputs)
    nc, ctx = build_program()
    res = run_bass_kernel_spmd(nc, per_core, core_ids=list(range(8)))
    outs = [np.ascontiguousarray(r["outT"].T) for r in res.results]
    return np.stack(outs, axis=0).astype(np.float32)


# version 07
# speedup vs baseline: 1.7118x; 1.1651x over previous
import numpy as np
from contextlib import ExitStack

import concourse.bass as bass
import concourse.mybir as mybir
from concourse.bass_utils import run_bass_kernel_spmd

F32 = mybir.dt.float32
BF16 = mybir.dt.bfloat16
I8 = mybir.dt.int8
ALU = mybir.AluOpType
AF = mybir.ActivationFunctionType
AX = mybir.AxisListType

C = 1024
T = 4096
H = 4096
NCH = C // 128
NHC = H // 128
TW = 512
NT = T // TW
HW = TW // 2
EPS = 1e-5

ENGINES = ("sync", "gpsimd", "tensor", "vector", "scalar")
CI_INVC, CI_EPS, CI_NEG1, CI_127, CI_INV127, CI_LN127, CI_NLN127, CI_ONE = range(8)


class Plan:

    def __init__(self):
        self.items = {e: [] for e in ENGINES}
        self.cnt = {}
        self.seen = {e: {} for e in ENGINES}

    def sem(self, name):
        self.cnt[name] = 0

    def now(self, s):
        return (s, self.cnt[s])

    def emit(self, eng, build, wait=(), inc=None, inc_k=1):
        waits = []
        for w_ in wait:
            s, v = w_ if isinstance(w_, tuple) else (w_, self.cnt[w_])
            if v <= 0 or self.seen[eng].get(s, -1) >= v:
                continue
            self.seen[eng][s] = v
            waits.append((s, v))
        if inc is not None:
            self.cnt[inc] += inc_k
        self.items[eng].append((waits, build, inc, inc_k))
        return (inc, self.cnt[inc]) if inc is not None else None

    def run(self, e, eng_name, sems):
        for waits, build, inc, inc_k in self.items[eng_name]:
            for s, v in waits:
                e.wait_ge(sems[s], v)
            ins = build(e) if build is not None else None
            if inc is not None:
                ins.then_inc(sems[inc], inc_k)


def host_prep(inputs):
    d = inputs
    x = np.asarray(d["L0_attn_x"], dtype=np.float32)
    B = x.shape[0]

    def vec8(v):
        return np.ascontiguousarray(np.asarray(v, np.float32).reshape(NCH, 128).T)

    def wproj(w):
        a = np.asarray(w, np.float32).T.reshape(NCH, 128, NCH, 128)
        return np.ascontiguousarray(a.transpose(2, 1, 0, 3).reshape(C, C))

    def wfk_prep(w):
        a = np.asarray(w, np.float32).T.reshape(NCH, 128, NHC, 128)
        return np.ascontiguousarray(a.transpose(2, 1, 0, 3).reshape(H, C))

    def wfv_prep(w):
        a = np.asarray(w, np.float32).T.reshape(NHC, 128, NCH, 128)
        return np.ascontiguousarray(a.transpose(2, 1, 0, 3).reshape(C, H))

    vecs = np.concatenate([
        vec8(d["L0_attn_ln0_w"]), vec8(d["L0_attn_ln0_b"]),
        vec8(d["L0_attn_ln1_w"]), vec8(d["L0_attn_ln1_b"]),
        vec8(d["L0_attn_ln2_w"]), vec8(d["L0_attn_ln2_b"]),
        vec8(d["L0_attn_time_decay"]), vec8(d["L0_attn_time_first"]),
        vec8(d["L0_attn_tmk"]), vec8(d["L0_attn_tmv"]), vec8(d["L0_attn_tmr"]),
        vec8(d["L0_attn_cmk"]), vec8(d["L0_attn_cmr"]),
    ], axis=1)

    csts = np.zeros((128, 8), np.float32)
    csts[:, CI_INVC] = 1.0 / C
    csts[:, CI_EPS] = EPS
    csts[:, CI_NEG1] = -1.0
    csts[:, CI_127] = 127.0
    csts[:, CI_INV127] = 1.0 / 127.0
    csts[:, CI_LN127] = np.log(127.0)
    csts[:, CI_NLN127] = -np.log(127.0)
    csts[:, CI_ONE] = 1.0

    sel8 = np.zeros((8, 8 * 128), np.float32)
    for j in range(8):
        sel8[j, j * 128:(j + 1) * 128] = 1.0

    shared = {
        "w_k": wproj(d["L0_attn_Wk"]), "w_v": wproj(d["L0_attn_Wv"]),
        "w_r": wproj(d["L0_attn_Wr"]), "w_o": wproj(d["L0_attn_Wo"]),
        "w_fr": wproj(d["L0_attn_Wfr"]),
        "w_fk": wfk_prep(d["L0_attn_Wfk"]), "w_fv": wfv_prep(d["L0_attn_Wfv"]),
        "rhsT": np.ascontiguousarray(np.asarray(d["L0_ffn_rhs"], np.float32).T),
        "vecs": vecs, "csts": csts,
        "ones": np.ones((128, 128), np.float32),
        "ident": np.eye(128, dtype=np.float32),
        "sel8": sel8,
    }
    per_core = []
    for b in range(B):
        m = dict(shared)
        m["xT"] = np.ascontiguousarray(x[b].T)
        per_core.append(m)
    return per_core


def build_program(dbg=None):
    nc = bass.Bass("TRN2", target_bir_lowering=False, detect_race_conditions=False)
    ctx = ExitStack()

    xT = nc.dram_tensor("xT", [C, T], F32, kind="ExternalInput")
    outT = nc.dram_tensor("outT", [C, T], F32, kind="ExternalOutput")
    dbgF = nc.dram_tensor("dbgF", [C, T], F32, kind="ExternalOutput")
    dbgB = nc.dram_tensor("dbgB", [C, T], BF16, kind="ExternalOutput")
    din = {}
    for n, shp in [("w_k", [C, C]), ("w_v", [C, C]), ("w_r", [C, C]),
                   ("w_o", [C, C]), ("w_fr", [C, C]), ("w_fk", [H, C]),
                   ("w_fv", [C, H]), ("rhsT", [C, C]),
                   ("vecs", [128, 13 * 8]), ("csts", [128, 8]),
                   ("ones", [128, 128]), ("ident", [128, 128]),
                   ("sel8", [8, 8 * 128])]:
        din[n] = nc.dram_tensor(n, shp, F32, kind="ExternalInput")
    scr = {}
    for n, shp in [("s_k", [C, C]), ("s_v", [C, C]), ("s_r", [C, C]),
                   ("s_o", [C, C]), ("s_fr", [C, C]), ("s_fk", [H, C]),
                   ("s_fv", [C, H]), ("s_q", [C, C])]:
        scr[n] = nc.dram_tensor(n, shp, BF16, kind="Internal")

    def sb(name, shape, dtype=F32):
        return ctx.enter_context(nc.sbuf_tensor(name, shape, dtype))

    x_in = sb("x_in", [128, NCH * TW])
    xres = sb("xres", [128, NCH * TW])
    h1b = [sb(f"h1_{j}", [128, TW + 1], BF16) for j in range(NCH)]
    h2b = [sb(f"h2_{j}", [128, TW + 1], BF16) for j in range(NCH)]
    xkb = sb("xkb", [128, NCH * TW], BF16)
    xvb = sb("xvb", [128, NCH * TW], BF16)
    xrb = sb("xrb", [128, NCH * TW], BF16)
    sig2 = sb("sig2", [128, NCH * TW], BF16)
    ekb = [sb(f"ek_{j}", [128, TW + 1], BF16) for j in range(NCH)]
    zb = [sb(f"z_{j}", [128, TW + 1], BF16) for j in range(NCH)]
    PA = sb("PA", [128, NCH * TW])
    PB = sb("PB", [128, NCH * TW])
    kf = sb("kf", [128, NHC * TW], BF16)
    pool = [sb(f"pool{i}", [128, TW]) for i in range(6)]
    dl = sb("dl", [128, TW], BF16)
    vcs = sb("vcs", [128, 13 * 8])
    cst = sb("cst", [128, 8])
    dvec = sb("dvec", [128, 8])
    euv = sb("euv", [128, 8])
    srhs = sb("srhs", [128, 8])
    rrv = sb("rrv", [128, 8])
    hist = sb("hist", [128, 4 * NCH])
    am4 = sb("am4", [128, 8])
    amaxR = sb("amaxR", [128, 8])
    prT = sb("prT", [8, 128])
    onesb = sb("onesb", [128, 128], BF16)
    idf = sb("idf", [128, 128])
    idb = sb("idb", [128, 128], BF16)
    self8 = sb("self8", [8, 8 * 128])
    qi8 = sb("qi8", [128, 1024], I8)
    pw = [sb(f"pw{i}", [128, NCH * 128], BF16) for i in range(2)]
    fkw = [sb(f"fkw{i}", [128, NCH * 128], BF16) for i in range(2)]
    fvw = [sb(f"fvw{i}", [128, NHC * 128], BF16) for i in range(2)]
    def alias(name, shape, dtype, backing):
        h = nc.virtual_sbuf_tensor(name, shape, dtype)
        nc._tensor(h.name, list(h.shape), h.dtype, type="SB")
        mv = nc.lookup_mloc(h)
        mv.allocated = True
        mv.addr = nc.lookup_mloc(backing).addr
        return h

    pcf = [alias("pcf0", [128, 2048], F32, kf), alias("pcf1", [128, 2048], F32, xkb)]
    pcb = [alias("pcb0", [128, 2048], BF16, sig2), alias("pcb1", [128, 2048], BF16, xrb)]
    pi8 = alias("pi8", [128, 1024], I8, qi8)

    pb = [ctx.enter_context(nc.psum_tensor(f"pb{i}", [128, 512], F32)) for i in range(6)]
    pT = ctx.enter_context(nc.psum_tensor("pT", [128, 1024], F32))

    P = Plan()
    semnames = ["sx", "sw", "gw", "so", "t", "v", "a",
                "pw0", "pw1", "fk0", "fk1", "fv0", "fv1", "cl0", "cl1", "cg0", "cg1"]
    for s in semnames:
        P.sem(s)
    sems = {s: ctx.enter_context(nc.semaphore(s)) for s in semnames}

    def xs(buf, j, w=TW):
        return buf[:, j * w:(j + 1) * w]

    VC = {n: i for i, n in enumerate(
        ["ln0_w", "ln0_b", "ln1_w", "ln1_b", "ln2_w", "ln2_b",
         "td", "tf", "tmk", "tmv", "tmr", "cmk", "cmr"])}

    def vap(name, j):
        return vcs[:, VC[name] * 8 + j:VC[name] * 8 + j + 1]

    def cap(i):
        return cst[:, i:i + 1]

    P.emit("sync", lambda e: e.dma_start(vcs[:, :], din["vecs"][:, :]), inc="sx", inc_k=16)
    P.emit("sync", lambda e: e.dma_start(cst[:, :], din["csts"][:, :]), inc="sx", inc_k=16)
    P.emit("sync", lambda e: e.dma_start(idf[:, :], din["ident"][:, :]), inc="sx", inc_k=16)
    P.emit("sync", lambda e: e.dma_start(pcf[0][:, 0:128], din["ones"][:, :]), inc="sx", inc_k=16)
    P.emit("sync", lambda e: e.dma_start(self8[:, :], din["sel8"][:, :]), inc="sx", inc_k=16)
    P.emit("vector", lambda e: e.tensor_copy(onesb[:, :], pcf[0][:, 0:128]), wait=["sx"], inc="v")
    P.emit("vector", lambda e: e.tensor_copy(idb[:, :], idf[:, :]), inc="v")
    P.emit("scalar", lambda e: e.activation(am4[:, :], vcs[:, VC["td"] * 8:VC["td"] * 8 + 8], AF.Exp, bias=0.0, scale=1.0), wait=["sx"], inc="a")
    P.emit("scalar", lambda e: e.activation(dvec[:, :], am4[:, :], AF.Exp, bias=0.0, scale=-1.0), inc="a")
    P.emit("scalar", lambda e: e.activation(euv[:, :], vcs[:, VC["tf"] * 8:VC["tf"] * 8 + 8], AF.Exp, bias=0.0, scale=1.0), inc="a")
    for j in range(NCH):
        P.emit("vector", lambda e, j=j: e.memset(zb[j][:, 0:1], 0.0), inc="v")
        P.emit("vector", lambda e, j=j: e.memset(ekb[j][:, 0:1], 0.0), inc="v")
        P.emit("vector", lambda e, j=j: e.memset(h1b[j][:, 0:1], 0.0), inc="v")
        P.emit("vector", lambda e, j=j: e.memset(h2b[j][:, 0:1], 0.0), inc="v")
    P.emit("vector", lambda e: e.memset(hist[:, :], 0.0), inc="v")

    cast_jobs = []
    for n_in, n_scr, rows, cols in [
        ("w_k", "s_k", C, C), ("w_v", "s_v", C, C), ("w_r", "s_r", C, C),
        ("w_o", "s_o", C, C), ("w_fr", "s_fr", C, C),
        ("w_fk", "s_fk", H, C), ("w_fv", "s_fv", C, H),
    ]:
        for r0 in range(0, rows, 128):
            for c0 in range(0, cols, 2048):
                cast_jobs.append((n_in, n_scr, r0, c0, min(2048, cols - c0)))
    store_marks = []
    for par, (n_in, n_scr, r0, c0, cw) in enumerate(cast_jobs):
        b = par % 2
        waits = [P.now("v"), P.now("a")] if par < 2 else [store_marks[par - 2]]
        ld_mk = P.emit("sync", lambda e, n_in=n_in, r0=r0, c0=c0, cw=cw, b=b:
                       e.dma_start(pcf[b][:, 0:cw], din[n_in][r0:r0 + 128, c0:c0 + cw]),
                       wait=waits, inc=f"cl{b}", inc_k=16)
        P.emit("scalar", lambda e, b=b, cw=cw: e.copy(pcb[b][:, 0:cw], pcf[b][:, 0:cw]),
               wait=[ld_mk], inc="a")
        mk = P.emit("gpsimd", lambda e, n_scr=n_scr, r0=r0, c0=c0, cw=cw, b=b:
                    e.dma_start(scr[n_scr][r0:r0 + 128, c0:c0 + cw], pcb[b][:, 0:cw]),
                    wait=["a"], inc=f"cg{b}", inc_k=16)
        store_marks.append(mk)

    red_marks = {}
    for j in range(NCH):
        b = j % 2
        lw = [P.now("a")] if j < 2 else [red_marks[j - 2]]
        ld_mk = P.emit("sync", lambda e, j=j, b=b: e.dma_start(pcf[b][:, 0:1024], din["rhsT"][j * 128:(j + 1) * 128, :]),
               wait=lw, inc=f"cl{b}", inc_k=16)
        red_marks[j] = P.emit("vector", lambda e, j=j, b=b: e.tensor_reduce(amaxR[:, j:j + 1], pcf[b][:, 0:1024], AX.X, ALU.max, apply_absolute_value=True),
               wait=[ld_mk], inc="v")
    P.emit("scalar", lambda e: e.activation(pcf[0][:, 0:8], amaxR[:, :], AF.Ln, bias=0.0, scale=1.0),
           wait=["v", "sx"], inc="a")
    P.emit("scalar", lambda e: e.activation(rrv[:, :], pcf[0][:, 0:8], AF.Exp, bias=cap(CI_LN127), scale=-1.0), inc="a")
    P.emit("scalar", lambda e: e.activation(srhs[:, :], pcf[0][:, 0:8], AF.Exp, bias=cap(CI_NLN127), scale=1.0), inc="a")
    P.emit("vector", lambda e: None, wait=["a"])
    for j in range(NCH):
        BW = ["v", "a", "t", P.now("gw"), P.now("sw"), P.now("sx")]
        ld2 = P.emit("sync", lambda e, j=j: e.dma_start(pcf[0][:, 0:1024], din["rhsT"][j * 128:(j + 1) * 128, :]),
               wait=list(BW), inc="cl0", inc_k=16)
        P.emit("vector", lambda e, j=j: e.tensor_scalar(pi8[:, 0:1024], pcf[0][:, 0:1024], rrv[:, j:j + 1], None, ALU.mult),
               wait=[ld2, "a", "t", P.now("gw")], inc="v")
        P.emit("vector", lambda e: e.tensor_copy(pcf[1][:, 0:1024], pi8[:, 0:1024]),
               wait=[P.now("gw"), P.now("t")], inc="v")
        for m in range(NCH):
            P.emit("tensor", lambda e, m=m: e.transpose(pT[:, m * 128:(m + 1) * 128], pcf[1][:, m * 128:(m + 1) * 128], idf[:, :]),
                   wait=(["v", P.now("a"), P.now("gw")] if m == 0 else []), inc=("t" if m == NCH - 1 else None))
        P.emit("scalar", lambda e: e.copy(pcb[1][:, 0:1024], pT[:, 0:1024]),
               wait=["t", "v", P.now("gw")], inc="a")
        for m in range(NCH):
            P.emit("gpsimd", lambda e, j=j, m=m: e.dma_start(scr["s_q"][j * 128:(j + 1) * 128, m * 128:(m + 1) * 128], pcb[1][:, m * 128:(m + 1) * 128]),
                   wait=["a", "t", "v"], inc="gw", inc_k=16)
    setup_cg = (P.cnt["cg0"], P.cnt["cg1"])
    if dbg == "qrhs":
        for mm_ in range(NCH):
            P.emit("sync", lambda e, mm_=mm_: e.dma_start(pw[0][:, 0:1024], scr["s_q"][mm_ * 128:(mm_ + 1) * 128, :]),
                   wait=[P.now("gw"), ("cg0", P.cnt["cg0"]), ("cg1", P.cnt["cg1"]), P.now("so")], inc="sw", inc_k=16)
            P.emit("sync", lambda e, mm_=mm_: e.dma_start(dbgB[mm_ * 128:(mm_ + 1) * 128, 0:1024], pw[0][:, 0:1024]),
                   wait=["sw"], inc="so", inc_k=16)
        P.emit("sync", lambda e: e.dma_start(dbgF[0:128, 0:8], srhs[:, :]), wait=[P.now("a")], inc="so", inc_k=16)
        P.emit("sync", lambda e: e.dma_start(dbgF[0:128, 8:16], rrv[:, :]), inc="so", inc_k=16)
        P.emit("sync", lambda e: e.dma_start(dbgF[0:128, 16:24], amaxR[:, :]), wait=[P.now("v")], inc="so", inc_k=16)
    setup_gw = P.cnt["gw"]

    SETUP_DONE = [("gw", None), ("cg0", None), ("cg1", None)]

    def setup_done_waits():
        return [("gw", setup_gw), ("cg0", setup_cg[0]), ("cg1", setup_cg[1])]

    def stream_w(scr_name, buf_pair, m, ncols, wait_extra, engine="sync", kind="pw"):
        b = m % 2
        semn = f"{kind}{b}"

        def go(e, scr_name=scr_name, m=m, ncols=ncols, b=b):
            return e.dma_start(buf_pair[b][:, 0:ncols], scr[scr_name][m * 128:(m + 1) * 128, 0:ncols])
        mk = P.emit(engine, go, wait=setup_done_waits() + list(wait_extra), inc=semn, inc_k=16)
        return buf_pair[b], mk

    def proj(scr_name, rhs_buf, banks, consumer, extra_start=()):
        start_v = P.now("v")
        cons_marks = {}
        tmarks = {}
        for m in range(NCH):
            wx = [P.now("t")] if m < 2 else [tmarks[m - 2]]
            buf, rdy = stream_w(scr_name, pw, m, NCH * 128, wx)
            waits = [rdy, start_v] + list(extra_start)
            if m >= 2:
                waits.append(cons_marks[m - 2])
            bank = banks[m % 2]
            for j in range(NCH):
                tm = P.emit("tensor", lambda e, j=j, buf=buf, bank=bank:
                            e.matmul(bank[:, 0:TW], buf[:, j * 128:(j + 1) * 128], xs(rhs_buf, j),
                                     start=(j == 0), stop=(j == NCH - 1)),
                            wait=(waits if j == 0 else []), inc=("t" if j == NCH - 1 else None))
            tmarks[m] = tm
            cons_marks[m] = consumer(m, bank)

    def ln_stats(src):
        cmarks, qmarks = [], []
        wv0, wt0 = P.now("v"), P.now("t")
        for j in range(NCH):
            cmarks.append(P.emit("scalar", lambda e, j=j: e.copy(xs(xkb, j), xs(src, j)),
                                 wait=([wv0, wt0] if j == 0 else []), inc="a"))
        for j in range(NCH):
            qmarks.append(P.emit("vector", lambda e, j=j: e.tensor_tensor(xs(xvb, j), xs(xkb, j), xs(xkb, j), ALU.mult),
                                 wait=([cmarks[j], wt0] if j == 0 else [cmarks[j]]), inc="v"))
        for j in range(NCH):
            P.emit("tensor", lambda e, j=j: e.matmul(pb[4][:, 0:TW], onesb[:, :], xs(xkb, j), start=(j == 0), stop=(j == NCH - 1)),
                   wait=([cmarks[j], P.now("v")] if j == 0 else [cmarks[j]]), inc=("t" if j == NCH - 1 else None))
        for j in range(NCH):
            P.emit("tensor", lambda e, j=j: e.matmul(pb[5][:, 0:TW], onesb[:, :], xs(xvb, j), start=(j == 0), stop=(j == NCH - 1)),
                   wait=[qmarks[j]], inc=("t" if j == NCH - 1 else None))
        P.emit("vector", lambda e: e.tensor_scalar(pool[1][:, :], pb[4][:, 0:TW], cap(CI_INVC), None, ALU.mult),
               wait=["t"], inc="v")
        P.emit("vector", lambda e: e.tensor_scalar(pool[2][:, :], pb[5][:, 0:TW], cap(CI_INVC), cap(CI_EPS), ALU.mult, ALU.add),
               inc="v")
        P.emit("vector", lambda e: e.scalar_tensor_tensor(pool[3][:, :], pool[1][:, :], cap(CI_NEG1), pool[1][:, :], ALU.mult, ALU.mult),
               inc="v")
        P.emit("vector", lambda e: e.tensor_tensor(pool[2][:, :], pool[2][:, :], pool[3][:, :], ALU.add), inc="v")
        P.emit("scalar", lambda e: e.activation(pool[3][:, :], pool[2][:, :], AF.Ln, bias=0.0, scale=1.0),
               wait=["v"], inc="a")
        P.emit("scalar", lambda e: e.activation(pool[0][:, :], pool[3][:, :], AF.Exp, bias=0.0, scale=-0.5), inc="a")

    def ln_apply(src, dst_fn, wname=None, bname=None):
        for j in range(NCH):
            P.emit("vector", lambda e, j=j: e.tensor_tensor(pool[4][:, :], xs(src, j), pool[1][:, :], ALU.subtract),
                   wait=(["a"] if j == 0 else []), inc="v")
            if wname is None:
                P.emit("vector", lambda e, j=j: e.tensor_tensor(dst_fn(j), pool[4][:, :], pool[0][:, :], ALU.mult), inc="v")
            else:
                P.emit("vector", lambda e, j=j: e.scalar_tensor_tensor(pool[5][:, :], pool[4][:, :], vap(wname, j), pool[0][:, :], ALU.mult, ALU.mult),
                       inc="v")
                P.emit("vector", lambda e, j=j: e.tensor_scalar(dst_fn(j), pool[5][:, :], vap(bname, j), None, ALU.add), inc="v")

    def tap(name, t0, ap_fn, bf=False):
        if dbg != name:
            return
        dst = dbgB if bf else dbgF
        for j in range(NCH):
            P.emit("sync", lambda e, j=j, t0=t0: e.dma_start(dst[j * 128:(j + 1) * 128, t0:t0 + TW], ap_fn(j)),
                   wait=[P.now("v"), P.now("a"), P.now("t")], inc="so", inc_k=16)
        P.emit("vector", lambda e: None, wait=["so"])
        P.emit("scalar", lambda e: None, wait=["so"])
        P.emit("tensor", lambda e: None, wait=["so"])

    P.emit("vector", lambda e: None, wait=setup_done_waits() + [P.now("t"), P.now("a")])
    P.emit("scalar", lambda e: None, wait=setup_done_waits() + [P.now("t"), P.now("v")])
    P.emit("tensor", lambda e: None, wait=setup_done_waits())
    for it in range(NT):
        t0 = it * TW
        for j in range(NCH):
            P.emit("sync", lambda e, j=j, t0=t0: e.dma_start(xs(x_in, j), xT[j * 128:(j + 1) * 128, t0:t0 + TW]),
                   wait=[P.now("v")], inc="sx", inc_k=16)
        if it > 0:
            P.emit("scalar", lambda e: e.copy(h1b[0][:, 0:1], h1b[0][:, TW:TW + 1]),
                   wait=[P.now("v")], inc="a")
            for j in range(NCH):
                if j > 0:
                    P.emit("scalar", lambda e, j=j: e.copy(h1b[j][:, 0:1], h1b[j][:, TW:TW + 1]), inc="a")
                P.emit("scalar", lambda e, j=j: e.copy(h2b[j][:, 0:1], h2b[j][:, TW:TW + 1]), inc="a")
                P.emit("scalar", lambda e, j=j: e.copy(ekb[j][:, 0:1], ekb[j][:, TW:TW + 1]), inc="a")
            for j in range(NCH):
                P.emit("vector", lambda e, j=j: e.tensor_copy(zb[j][:, 0:1], zb[j][:, TW:TW + 1]), inc="v")
        P.emit("vector", lambda e: None, wait=["sx"])
        P.emit("scalar", lambda e: None, wait=["sx"])

        ln_stats(x_in)
        ln_apply(x_in, lambda j: xs(xres, j), "ln0_w", "ln0_b")
        tap("x0", t0, lambda j: xs(xres, j))
        ln_stats(xres)
        ln_apply(xres, lambda j: h1b[j][:, 1:TW + 1], "ln1_w", "ln1_b")

        tap("h1", t0, lambda j: h1b[j][:, 1:TW + 1], bf=True)
        for j in range(NCH):
            P.emit("vector", lambda e, j=j: e.tensor_tensor(dl[:, :], h1b[j][:, 1:TW + 1], h1b[j][:, 0:TW], ALU.subtract),
                   wait=(["a", P.now("t")] if j == 0 else ["a"]), inc="v")
            P.emit("vector", lambda e, j=j: e.scalar_tensor_tensor(xs(xkb, j), dl[:, :], vap("tmk", j), h1b[j][:, 0:TW], ALU.mult, ALU.add), inc="v")
            P.emit("vector", lambda e, j=j: e.scalar_tensor_tensor(xs(xvb, j), dl[:, :], vap("tmv", j), h1b[j][:, 0:TW], ALU.mult, ALU.add), inc="v")
            P.emit("vector", lambda e, j=j: e.scalar_tensor_tensor(xs(xrb, j), dl[:, :], vap("tmr", j), h1b[j][:, 0:TW], ALU.mult, ALU.add), inc="v")

        tap("xk", t0, lambda j: xs(xkb, j), bf=True)
        tap("xv", t0, lambda j: xs(xvb, j), bf=True)
        tap("xr", t0, lambda j: xs(xrb, j), bf=True)
        k_last = [None]

        def k_cons(m, bank):
            k_last[0] = P.emit("scalar", lambda e, m=m, bank=bank: e.activation(ekb[m][:, 1:TW + 1], bank[:, 0:TW], AF.Exp, bias=0.0, scale=1.0),
                               wait=["t"], inc="a")
            return k_last[0]
        proj("s_k", xkb, (pb[0], pb[1]), k_cons)

        def v_cons(m, bank):
            return P.emit("vector", lambda e, m=m, bank=bank: e.tensor_tensor(zb[m][:, 1:TW + 1], ekb[m][:, 1:TW + 1], bank[:, 0:TW], ALU.mult),
                          wait=["t", "a"], inc="v")
        proj("s_v", xvb, (pb[2], pb[3]), v_cons)

        tap("ek", t0, lambda j: ekb[j][:, 1:TW + 1])
        tap("z", t0, lambda j: zb[j][:, 1:TW + 1])
        P.emit("vector", lambda e: None, wait=[P.now("so")])
        scan_marks = []
        for j in range(NCH):
            P.emit("vector", lambda e, j=j: e.tensor_tensor_scan(xs(PA, j), bass.AP(dvec, j, [[8, 128], [0, TW]]), zb[j][:, 0:TW], hist[:, 4 * j:4 * j + 1], ALU.mult, ALU.add),
                   wait=(["a"] if j == 0 else []), inc="v")
            P.emit("vector", lambda e, j=j: e.tensor_tensor_scan(xs(PB, j), bass.AP(dvec, j, [[8, 128], [0, TW]]), ekb[j][:, 0:TW], hist[:, 4 * j + 1:4 * j + 2], ALU.mult, ALU.add),
                   inc="v")
            P.emit("vector", lambda e, j=j: e.tensor_copy(hist[:, 4 * j:4 * j + 1], PA[:, j * TW + TW - 1:j * TW + TW]), inc="v")
            mk = P.emit("vector", lambda e, j=j: e.tensor_copy(hist[:, 4 * j + 1:4 * j + 2], PB[:, j * TW + TW - 1:j * TW + TW]), inc="v")
            scan_marks.append(mk)

        tap("PA", t0, lambda j: xs(PA, j))
        tap("PB", t0, lambda j: xs(PB, j))
        def r_cons(m, bank):
            pn = pool[0 + 3 * (m % 2)]
            pd = pool[1 + 3 * (m % 2)]
            ps_ = pool[2 + 3 * (m % 2)]
            nmark = P.emit("vector", lambda e, m=m, pn=pn: e.scalar_tensor_tensor(pn[:, :], zb[m][:, 1:TW + 1], euv[:, m:m + 1], xs(PA, m), ALU.mult, ALU.add),
                           wait=[scan_marks[m]], inc="v")
            dmark = P.emit("vector", lambda e, m=m, pd=pd: e.scalar_tensor_tensor(pd[:, :], ekb[m][:, 1:TW + 1], euv[:, m:m + 1], xs(PB, m), ALU.mult, ALU.add),
                           inc="v")
            P.emit("scalar", lambda e, pd=pd: e.activation(pd[:, :], pd[:, :], AF.Ln, bias=0.0, scale=1.0),
                   wait=[dmark], inc="a")
            P.emit("scalar", lambda e, pd=pd: e.activation(pd[:, :], pd[:, :], AF.Exp, bias=0.0, scale=-1.0), inc="a")
            P.emit("scalar", lambda e, bank=bank, ps_=ps_: e.activation(ps_[:, :], bank[:, 0:TW], AF.Exp, bias=0.0, scale=-1.0),
                   wait=["t"], inc="a")
            P.emit("scalar", lambda e, ps_=ps_: e.activation(ps_[:, :], ps_[:, :], AF.Ln, bias=1.0, scale=1.0), inc="a")
            amark = P.emit("scalar", lambda e, ps_=ps_: e.activation(ps_[:, :], ps_[:, :], AF.Exp, bias=0.0, scale=-1.0), inc="a")
            P.emit("vector", lambda e, pn=pn, pd=pd: e.tensor_tensor(pn[:, :], pn[:, :], pd[:, :], ALU.mult),
                   wait=[amark], inc="v")
            return P.emit("vector", lambda e, m=m, pn=pn, ps_=ps_: e.tensor_tensor(xs(xkb, m), ps_[:, :], pn[:, :], ALU.mult), inc="v")
        proj("s_r", xrb, (pb[4], pb[5]), r_cons)

        tap("rwkvin", t0, lambda j: xs(xkb, j), bf=True)
        def o_cons(m, bank):
            return P.emit("vector", lambda e, m=m, bank=bank: e.tensor_tensor(xs(xres, m), xs(xres, m), bank[:, 0:TW], ALU.add),
                          wait=["t"], inc="v")
        proj("s_o", xkb, (pb[0], pb[1]), o_cons, extra_start=[k_last[0]])

        tap("x1", t0, lambda j: xs(xres, j))
        ln_stats(xres)
        ln_apply(xres, lambda j: h2b[j][:, 1:TW + 1], "ln2_w", "ln2_b")

        for j in range(NCH):
            P.emit("vector", lambda e, j=j: e.tensor_tensor(dl[:, :], h2b[j][:, 1:TW + 1], h2b[j][:, 0:TW], ALU.subtract),
                   wait=(["a", P.now("t")] if j == 0 else ["a"]), inc="v")
            P.emit("vector", lambda e, j=j: e.scalar_tensor_tensor(xs(xkb, j), dl[:, :], vap("cmk", j), h2b[j][:, 0:TW], ALU.mult, ALU.add), inc="v")
            P.emit("vector", lambda e, j=j: e.scalar_tensor_tensor(xs(xrb, j), dl[:, :], vap("cmr", j), h2b[j][:, 0:TW], ALU.mult, ALU.add), inc="v")

        def fr_cons(m, bank):
            P.emit("scalar", lambda e, bank=bank: e.activation(pool[2][:, :], bank[:, 0:TW], AF.Exp, bias=0.0, scale=-1.0),
                   wait=["t", P.now("v")], inc="a")
            P.emit("scalar", lambda e: e.activation(pool[2][:, :], pool[2][:, :], AF.Ln, bias=1.0, scale=1.0), inc="a")
            return P.emit("scalar", lambda e, m=m: e.activation(xs(sig2, m), pool[2][:, :], AF.Exp, bias=0.0, scale=-1.0), inc="a")
        proj("s_fr", xrb, (pb[2], pb[3]), fr_cons)

        kftt_marks = {}
        fk_tmarks = {}
        for m in range(NHC):
            wx = [P.now("t")] if m < 2 else [fk_tmarks[m - 2]]
            buf, rdy = stream_w("s_fk", fkw, m, NCH * 128, wx, engine="gpsimd", kind="fk")
            bank = pb[m % 2]
            waits = [rdy]
            if m < 2:
                waits += [P.now("v"), P.now("a")]
            else:
                waits.append(kftt_marks[m - 2])
            for j in range(NCH):
                tm = P.emit("tensor", lambda e, j=j, buf=buf, bank=bank:
                            e.matmul(bank[:, 0:TW], buf[:, j * 128:(j + 1) * 128], xs(xkb, j),
                                     start=(j == 0), stop=(j == NCH - 1)),
                            wait=(waits if j == 0 else []), inc=("t" if j == NCH - 1 else None))
            fk_tmarks[m] = tm
            rslot = pool[2 + (m % 2)]
            P.emit("scalar", lambda e, bank=bank, rslot=rslot: e.activation(rslot[:, :], bank[:, 0:TW], AF.Relu, bias=0.0, scale=1.0),
                   wait=(["t"] if m < 2 else ["t", kftt_marks[m - 2]]), inc="a")
            kftt_marks[m] = P.emit("vector", lambda e, m=m, bank=bank, rslot=rslot:
                                   e.tensor_tensor(kf[:, m * TW:(m + 1) * TW], bank[:, 0:TW], rslot[:, :], ALU.mult),
                                   wait=["a"], inc="v")
        kv_marks = {}
        fv_tmarks = {}
        for m in range(NCH):
            wx = [P.now("t")] if m < 2 else [fv_tmarks[m - 2]]
            buf, rdy = stream_w("s_fv", fvw, m, NHC * 128, wx, kind="fv")
            bank = pb[2 + (m % 2)]
            waits = [rdy, P.now("v")]
            if m >= 2:
                waits.append(kv_marks[m - 2])
            for o in range(NHC):
                tm = P.emit("tensor", lambda e, o=o, buf=buf, bank=bank:
                            e.matmul(bank[:, 0:TW], buf[:, o * 128:(o + 1) * 128], kf[:, o * TW:(o + 1) * TW],
                                     start=(o == 0), stop=(o == NHC - 1)),
                            wait=(waits if o == 0 else []), inc=("t" if o == NHC - 1 else None))
            fv_tmarks[m] = tm
            tslot = pool[4 + (m % 2)]
            P.emit("vector", lambda e, m=m, bank=bank, tslot=tslot:
                   e.tensor_tensor(tslot[:, :], xs(sig2, m), bank[:, 0:TW], ALU.mult),
                   wait=["t"], inc="v")
            kv_marks[m] = P.emit("vector", lambda e, m=m, tslot=tslot:
                                 e.tensor_tensor(xs(xres, m), xs(xres, m), tslot[:, :], ALU.add),
                                 inc="v")

        tap("x2", t0, lambda j: xs(xres, j))
        tap("sig2", t0, lambda j: xs(sig2, j), bf=True)
        for j in range(NCH):
            P.emit("vector", lambda e, j=j: e.tensor_tensor(xs(PB, j), xs(x_in, j), xs(xres, j), ALU.add), inc="v")

        ln_stats(PB)
        ln_apply(PB, lambda j: xs(PA, j))

        tap("y1", t0, lambda j: xs(PA, j))
        for g in range(4):
            for j in range(NCH):
                P.emit("tensor", lambda e, g=g, j=j: e.transpose(pT[:, j * 128:(j + 1) * 128], PA[:, j * TW + g * 128:j * TW + (g + 1) * 128], idf[:, :]),
                       wait=([P.now("v")] if j == 0 else []), inc=("t" if j == NCH - 1 else None))
            P.emit("vector", lambda e, g=g: e.tensor_reduce(am4[:, g:g + 1], pT[:, 0:1024], AX.X, ALU.max, apply_absolute_value=True),
                   wait=["t"], inc="v")
        P.emit("scalar", lambda e: e.activation(amaxR[:, 0:4], am4[:, 0:4], AF.Ln, bias=0.0, scale=1.0),
               wait=["v"], inc="a")
        P.emit("scalar", lambda e: e.activation(am4[:, 4:8], amaxR[:, 0:4], AF.Exp, bias=cap(CI_LN127), scale=-1.0), inc="a")
        P.emit("scalar", lambda e: e.activation(am4[:, 0:4], amaxR[:, 0:4], AF.Exp, bias=cap(CI_NLN127), scale=1.0), inc="a")
        P.emit("vector", lambda e: None, wait=["a"])
        P.emit("tensor", lambda e: e.transpose(pT[0:8, 0:128], am4[:, 0:8], idf[:, :]), wait=["v", "a"], inc="t")
        P.emit("vector", lambda e: e.tensor_copy(prT[:, :], pT[0:8, 0:128]), wait=["t"], inc="v")
        for g in range(4):
            P.emit("tensor", lambda e, g=g: e.matmul(pT[:, g * 128:(g + 1) * 128], self8[:, (4 + g) * 128:(5 + g) * 128], prT[:, :], start=True, stop=True),
                   wait=(["v"] if g == 0 else []), inc=None)
            P.emit("tensor", lambda e, g=g: e.matmul(pT[:, TW + g * 128:TW + (g + 1) * 128], self8[:, g * 128:(g + 1) * 128], prT[:, :], start=True, stop=True),
                   inc=("t" if g == 3 else None))
        P.emit("vector", lambda e: e.tensor_copy(pool[0][:, :], pT[:, TW:2 * TW]), wait=["t"], inc="v")
        if dbg in ("rinvbc", "am4"):
            if dbg == "rinvbc":
                P.emit("vector", lambda e: e.tensor_copy(pool[2][:, :], pT[:, 0:TW]), wait=["t"], inc="v")
            else:
                P.emit("vector", lambda e: e.tensor_copy(pool[3][:, 0:8], am4[:, :]), inc="v")
                P.emit("vector", lambda e: e.tensor_copy(pool[3][:, 8:16], amaxR[:, 0:8]), inc="v")
                P.emit("vector", lambda e: e.tensor_copy(pool[3][:, 16:24], pool[2][:, 0:8]), inc="v")
            P.emit("sync", lambda e, t0=t0, _src=(2 if dbg == "rinvbc" else 3): e.dma_start(dbgF[0:128, t0:t0 + TW], pool[_src][:, :]),
                   wait=[P.now("v")], inc="so", inc_k=16)
            P.emit("vector", lambda e: None, wait=["so"])
        for j in range(NCH):
            P.emit("vector", lambda e, j=j: e.tensor_tensor(qi8[:, 0:TW], xs(PA, j), pT[:, 0:TW], ALU.mult),
                   wait=([P.now("t")] if j == 0 else []), inc="v")
            P.emit("vector", lambda e, j=j: e.tensor_copy(xs(xkb, j), qi8[:, 0:TW]), inc="v")

        tap("qbf", t0, lambda j: xs(xkb, j), bf=True)
        def ffn_cons(m, bank):
            P.emit("vector", lambda e, m=m, bank=bank: e.scalar_tensor_tensor(pool[3][:, :], bank[:, 0:TW], srhs[:, m:m + 1], pool[0][:, :], ALU.mult, ALU.mult),
                   wait=["t"], inc="v")
            return P.emit("vector", lambda e, m=m: e.tensor_tensor(xs(xres, m), xs(PA, m), pool[3][:, :], ALU.add), inc="v")
        proj("s_q", xkb, (pb[0], pb[1]), ffn_cons)

        tap("s2", t0, lambda j: xs(xres, j))
        ln_stats(xres)
        for j in range(NCH):
            P.emit("vector", lambda e, j=j: e.tensor_tensor(pool[4][:, :], xs(xres, j), pool[1][:, :], ALU.subtract),
                   wait=(["a"] if j == 0 else []), inc="v")
            P.emit("vector", lambda e, j=j: e.tensor_tensor(xs(PA, j), pool[4][:, :], pool[0][:, :], ALU.mult), inc="v")
            P.emit("sync", lambda e, j=j, t0=t0: e.dma_start(outT[j * 128:(j + 1) * 128, t0:t0 + TW], xs(PA, j)),
                   wait=["v"], inc="so", inc_k=16)

    P.emit("sync", lambda e: None, wait=["so"])

    with nc.Block() as block:
        @block.sync
        def _(e):
            P.run(e, "sync", sems)

        @block.gpsimd
        def _(e):
            P.run(e, "gpsimd", sems)

        @block.tensor
        def _(e):
            P.run(e, "tensor", sems)

        @block.vector
        def _(e):
            P.run(e, "vector", sems)

        @block.scalar
        def _(e):
            P.run(e, "scalar", sems)

    return nc, ctx


def kernel(**inputs):
    per_core = host_prep(inputs)
    nc, ctx = build_program()
    res = run_bass_kernel_spmd(nc, per_core, core_ids=list(range(8)))
    outs = [np.ascontiguousarray(r["outT"].T) for r in res.results]
    return np.stack(outs, axis=0).astype(np.float32)


# version 08
# speedup vs baseline: 1.8410x; 1.0755x over previous
import numpy as np
from contextlib import ExitStack

import concourse.bass as bass
import concourse.mybir as mybir
from concourse.bass_utils import run_bass_kernel_spmd

F32 = mybir.dt.float32
BF16 = mybir.dt.bfloat16
I8 = mybir.dt.int8
ALU = mybir.AluOpType
AF = mybir.ActivationFunctionType
AX = mybir.AxisListType

C = 1024
T = 4096
H = 4096
NCH = C // 128
NHC = H // 128
TW = 512
NT = T // TW
HW = TW // 2
EPS = 1e-5

ENGINES = ("sync", "gpsimd", "tensor", "vector", "scalar")
CI_INVC, CI_EPS, CI_NEG1, CI_127, CI_INV127, CI_LN127, CI_NLN127, CI_ONE = range(8)


class Plan:

    def __init__(self):
        self.items = {e: [] for e in ENGINES}
        self.cnt = {}
        self.seen = {e: {} for e in ENGINES}

    def sem(self, name):
        self.cnt[name] = 0

    def now(self, s):
        return (s, self.cnt[s])

    def emit(self, eng, build, wait=(), inc=None, inc_k=1):
        waits = []
        for w_ in wait:
            s, v = w_ if isinstance(w_, tuple) else (w_, self.cnt[w_])
            if v <= 0 or self.seen[eng].get(s, -1) >= v:
                continue
            self.seen[eng][s] = v
            waits.append((s, v))
        if inc is not None:
            self.cnt[inc] += inc_k
        self.items[eng].append((waits, build, inc, inc_k))
        return (inc, self.cnt[inc]) if inc is not None else None

    def run(self, e, eng_name, sems):
        for waits, build, inc, inc_k in self.items[eng_name]:
            for s, v in waits:
                e.wait_ge(sems[s], v)
            ins = build(e) if build is not None else None
            if inc is not None:
                ins.then_inc(sems[inc], inc_k)


def host_prep(inputs):
    d = inputs
    x = np.asarray(d["L0_attn_x"], dtype=np.float32)
    B = x.shape[0]

    def vec8(v):
        return np.ascontiguousarray(np.asarray(v, np.float32).reshape(NCH, 128).T)

    def wproj(w):
        a = np.asarray(w, np.float32).T.reshape(NCH, 128, NCH, 128)
        return np.ascontiguousarray(a.transpose(2, 1, 0, 3).reshape(C, C))

    def wfk_prep(w):
        a = np.asarray(w, np.float32).T.reshape(NCH, 128, NHC, 128)
        return np.ascontiguousarray(a.transpose(2, 1, 0, 3).reshape(H, C))

    def wfv_prep(w):
        a = np.asarray(w, np.float32).T.reshape(NHC, 128, NCH, 128)
        return np.ascontiguousarray(a.transpose(2, 1, 0, 3).reshape(C, H))

    vecs = np.concatenate([
        vec8(d["L0_attn_ln0_w"]), vec8(d["L0_attn_ln0_b"]),
        vec8(d["L0_attn_ln1_w"]), vec8(d["L0_attn_ln1_b"]),
        vec8(d["L0_attn_ln2_w"]), vec8(d["L0_attn_ln2_b"]),
        vec8(d["L0_attn_time_decay"]), vec8(d["L0_attn_time_first"]),
        vec8(d["L0_attn_tmk"]), vec8(d["L0_attn_tmv"]), vec8(d["L0_attn_tmr"]),
        vec8(d["L0_attn_cmk"]), vec8(d["L0_attn_cmr"]),
    ], axis=1)

    csts = np.zeros((128, 8), np.float32)
    csts[:, CI_INVC] = 1.0 / C
    csts[:, CI_EPS] = EPS
    csts[:, CI_NEG1] = -1.0
    csts[:, CI_127] = 127.0
    csts[:, CI_INV127] = 1.0 / 127.0
    csts[:, CI_LN127] = np.log(127.0)
    csts[:, CI_NLN127] = -np.log(127.0)
    csts[:, CI_ONE] = 1.0

    sel8 = np.zeros((8, 8 * 128), np.float32)
    for j in range(8):
        sel8[j, j * 128:(j + 1) * 128] = 1.0

    shared = {
        "w_k": wproj(d["L0_attn_Wk"]), "w_v": wproj(d["L0_attn_Wv"]),
        "w_r": wproj(d["L0_attn_Wr"]), "w_o": wproj(d["L0_attn_Wo"]),
        "w_fr": wproj(d["L0_attn_Wfr"]),
        "w_fk": wfk_prep(d["L0_attn_Wfk"]), "w_fv": wfv_prep(d["L0_attn_Wfv"]),
        "rhsT": np.ascontiguousarray(np.asarray(d["L0_ffn_rhs"], np.float32).T),
        "vecs": vecs, "csts": csts,
        "ones": np.ones((128, 128), np.float32),
        "ident": np.eye(128, dtype=np.float32),
        "sel8": sel8,
    }
    per_core = []
    for b in range(B):
        m = dict(shared)
        m["xT"] = np.ascontiguousarray(x[b].T)
        per_core.append(m)
    return per_core


def build_program(dbg=None):
    nc = bass.Bass("TRN2", target_bir_lowering=False, detect_race_conditions=False)
    ctx = ExitStack()

    xT = nc.dram_tensor("xT", [C, T], F32, kind="ExternalInput")
    outT = nc.dram_tensor("outT", [C, T], F32, kind="ExternalOutput")
    dbgF = nc.dram_tensor("dbgF", [C, T], F32, kind="ExternalOutput")
    dbgB = nc.dram_tensor("dbgB", [C, T], BF16, kind="ExternalOutput")
    din = {}
    for n, shp in [("w_k", [C, C]), ("w_v", [C, C]), ("w_r", [C, C]),
                   ("w_o", [C, C]), ("w_fr", [C, C]), ("w_fk", [H, C]),
                   ("w_fv", [C, H]), ("rhsT", [C, C]),
                   ("vecs", [128, 13 * 8]), ("csts", [128, 8]),
                   ("ones", [128, 128]), ("ident", [128, 128]),
                   ("sel8", [8, 8 * 128])]:
        din[n] = nc.dram_tensor(n, shp, F32, kind="ExternalInput")
    scr = {}
    for n, shp in [("s_k", [C, C]), ("s_v", [C, C]), ("s_r", [C, C]),
                   ("s_o", [C, C]), ("s_fr", [C, C]), ("s_fk", [H, C]),
                   ("s_fv", [C, H]), ("s_q", [C, C])]:
        scr[n] = nc.dram_tensor(n, shp, BF16, kind="Internal")

    def sb(name, shape, dtype=F32):
        return ctx.enter_context(nc.sbuf_tensor(name, shape, dtype))

    x_in = sb("x_in", [128, NCH * TW])
    xres = sb("xres", [128, NCH * TW])
    h1b = [sb(f"h1_{j}", [128, TW + 1], BF16) for j in range(NCH)]
    h2b = [sb(f"h2_{j}", [128, TW + 1], BF16) for j in range(NCH)]
    xkb = sb("xkb", [128, NCH * TW], BF16)
    xvb = sb("xvb", [128, NCH * TW], BF16)
    xrb = sb("xrb", [128, NCH * TW], BF16)
    sig2 = sb("sig2", [128, NCH * TW], BF16)
    ekb = [sb(f"ek_{j}", [128, TW + 1], BF16) for j in range(NCH)]
    zb = [sb(f"z_{j}", [128, TW + 1], BF16) for j in range(NCH)]
    PA = sb("PA", [128, NCH * TW])
    PB = sb("PB", [128, NCH * TW])
    kf = sb("kf", [128, NHC * TW], BF16)
    pool = [sb(f"pool{i}", [128, TW]) for i in range(6)]
    dl = sb("dl", [128, TW], BF16)
    vcs = sb("vcs", [128, 13 * 8])
    cst = sb("cst", [128, 8])
    dvec = sb("dvec", [128, 8])
    euv = sb("euv", [128, 8])
    srhs = sb("srhs", [128, 8])
    rrv = sb("rrv", [128, 8])
    hist = sb("hist", [128, 4 * NCH])
    am4 = sb("am4", [128, 8])
    amaxR = sb("amaxR", [128, 8])
    prT = sb("prT", [8, 128])
    onesb = sb("onesb", [128, 128], BF16)
    idf = sb("idf", [128, 128])
    self8 = sb("self8", [8, 8 * 128])
    qi8 = sb("qi8", [128, 1024], I8)
    pw = [sb(f"pw{i}", [128, 2 * NCH * 128], BF16) for i in range(2)]
    fkw = [sb(f"fkw{i}", [128, 2 * NCH * 128], BF16) for i in range(2)]
    fvw = [sb(f"fvw{i}", [128, 2048], BF16) for i in range(2)]
    def alias(name, shape, dtype, backing):
        h = nc.virtual_sbuf_tensor(name, shape, dtype)
        nc._tensor(h.name, list(h.shape), h.dtype, type="SB")
        mv = nc.lookup_mloc(h)
        mv.allocated = True
        mv.addr = nc.lookup_mloc(backing).addr
        return h

    pcf = [alias("pcf0", [128, 2048], F32, kf), alias("pcf1", [128, 2048], F32, xkb)]
    pcb = [alias("pcb0", [128, 2048], BF16, sig2), alias("pcb1", [128, 2048], BF16, xrb)]
    pi8 = alias("pi8", [128, 1024], I8, qi8)

    pb = [ctx.enter_context(nc.psum_tensor(f"pb{i}", [128, 512], F32)) for i in range(6)]
    pT = ctx.enter_context(nc.psum_tensor("pT", [128, 1024], F32))

    P = Plan()
    semnames = ["sx", "sw", "gw", "so", "t", "v", "a",
                "pw0", "pw1", "fk0", "fk1", "fv0", "fv1", "cl0", "cl1", "cg0", "cg1"]
    for s in semnames:
        P.sem(s)
    sems = {s: ctx.enter_context(nc.semaphore(s)) for s in semnames}

    def xs(buf, j, w=TW):
        return buf[:, j * w:(j + 1) * w]

    VC = {n: i for i, n in enumerate(
        ["ln0_w", "ln0_b", "ln1_w", "ln1_b", "ln2_w", "ln2_b",
         "td", "tf", "tmk", "tmv", "tmr", "cmk", "cmr"])}

    def vap(name, j):
        return vcs[:, VC[name] * 8 + j:VC[name] * 8 + j + 1]

    def cap(i):
        return cst[:, i:i + 1]

    P.emit("sync", lambda e: e.dma_start(vcs[:, :], din["vecs"][:, :]), inc="sx", inc_k=16)
    P.emit("sync", lambda e: e.dma_start(cst[:, :], din["csts"][:, :]), inc="sx", inc_k=16)
    P.emit("sync", lambda e: e.dma_start(idf[:, :], din["ident"][:, :]), inc="sx", inc_k=16)
    P.emit("sync", lambda e: e.dma_start(pcf[0][:, 0:128], din["ones"][:, :]), inc="sx", inc_k=16)
    P.emit("sync", lambda e: e.dma_start(self8[:, :], din["sel8"][:, :]), inc="sx", inc_k=16)
    P.emit("vector", lambda e: e.tensor_copy(onesb[:, :], pcf[0][:, 0:128]), wait=["sx"], inc="v")
    P.emit("scalar", lambda e: e.activation(am4[:, :], vcs[:, VC["td"] * 8:VC["td"] * 8 + 8], AF.Exp, bias=0.0, scale=1.0), wait=["sx"], inc="a")
    P.emit("scalar", lambda e: e.activation(dvec[:, :], am4[:, :], AF.Exp, bias=0.0, scale=-1.0), inc="a")
    P.emit("scalar", lambda e: e.activation(euv[:, :], vcs[:, VC["tf"] * 8:VC["tf"] * 8 + 8], AF.Exp, bias=0.0, scale=1.0), inc="a")
    for j in range(NCH):
        P.emit("vector", lambda e, j=j: e.memset(zb[j][:, 0:1], 0.0), inc="v")
        P.emit("vector", lambda e, j=j: e.memset(ekb[j][:, 0:1], 0.0), inc="v")
        P.emit("vector", lambda e, j=j: e.memset(h1b[j][:, 0:1], 0.0), inc="v")
        P.emit("vector", lambda e, j=j: e.memset(h2b[j][:, 0:1], 0.0), inc="v")
    P.emit("vector", lambda e: e.memset(hist[:, :], 0.0), inc="v")

    cast_jobs = []
    for n_in, n_scr, rows, cols in [
        ("w_k", "s_k", C, C), ("w_v", "s_v", C, C), ("w_r", "s_r", C, C),
        ("w_o", "s_o", C, C), ("w_fr", "s_fr", C, C),
        ("w_fk", "s_fk", H, C), ("w_fv", "s_fv", C, H),
    ]:
        for r0 in range(0, rows, 128):
            for c0 in range(0, cols, 2048):
                cast_jobs.append((n_in, n_scr, r0, c0, min(2048, cols - c0)))
    store_marks = []
    for par, (n_in, n_scr, r0, c0, cw) in enumerate(cast_jobs):
        b = par % 2
        waits = [P.now("v"), P.now("a")] if par < 2 else [store_marks[par - 2]]
        ld_mk = P.emit("sync", lambda e, n_in=n_in, r0=r0, c0=c0, cw=cw, b=b:
                       e.dma_start(pcf[b][:, 0:cw], din[n_in][r0:r0 + 128, c0:c0 + cw]),
                       wait=waits, inc=f"cl{b}", inc_k=16)
        P.emit("scalar", lambda e, b=b, cw=cw: e.copy(pcb[b][:, 0:cw], pcf[b][:, 0:cw]),
               wait=[ld_mk], inc="a")
        mk = P.emit("gpsimd", lambda e, n_scr=n_scr, r0=r0, c0=c0, cw=cw, b=b:
                    e.dma_start(scr[n_scr][r0:r0 + 128, c0:c0 + cw], pcb[b][:, 0:cw]),
                    wait=["a"], inc=f"cg{b}", inc_k=16)
        store_marks.append(mk)

    red_marks = {}
    for j in range(NCH):
        b = j % 2
        lw = [P.now("a")] if j < 2 else [red_marks[j - 2]]
        ld_mk = P.emit("sync", lambda e, j=j, b=b: e.dma_start(pcf[b][:, 0:1024], din["rhsT"][j * 128:(j + 1) * 128, :]),
               wait=lw, inc=f"cl{b}", inc_k=16)
        red_marks[j] = P.emit("vector", lambda e, j=j, b=b: e.tensor_reduce(amaxR[:, j:j + 1], pcf[b][:, 0:1024], AX.X, ALU.max, apply_absolute_value=True),
               wait=[ld_mk], inc="v")
    P.emit("scalar", lambda e: e.activation(pcf[0][:, 0:8], amaxR[:, :], AF.Ln, bias=0.0, scale=1.0),
           wait=["v", "sx"], inc="a")
    P.emit("scalar", lambda e: e.activation(rrv[:, :], pcf[0][:, 0:8], AF.Exp, bias=cap(CI_LN127), scale=-1.0), inc="a")
    P.emit("scalar", lambda e: e.activation(srhs[:, :], pcf[0][:, 0:8], AF.Exp, bias=cap(CI_NLN127), scale=1.0), inc="a")
    P.emit("vector", lambda e: None, wait=["a"])
    for j in range(NCH):
        BW = ["v", "a", "t", P.now("gw"), P.now("sw"), P.now("sx")]
        ld2 = P.emit("sync", lambda e, j=j: e.dma_start(pcf[0][:, 0:1024], din["rhsT"][j * 128:(j + 1) * 128, :]),
               wait=list(BW), inc="cl0", inc_k=16)
        P.emit("vector", lambda e, j=j: e.tensor_scalar(pi8[:, 0:1024], pcf[0][:, 0:1024], rrv[:, j:j + 1], None, ALU.mult),
               wait=[ld2, "a", "t", P.now("gw")], inc="v")
        P.emit("vector", lambda e: e.tensor_copy(pcf[1][:, 0:1024], pi8[:, 0:1024]),
               wait=[P.now("gw"), P.now("t")], inc="v")
        for m in range(NCH):
            P.emit("tensor", lambda e, m=m: e.transpose(pT[:, m * 128:(m + 1) * 128], pcf[1][:, m * 128:(m + 1) * 128], idf[:, :]),
                   wait=(["v", P.now("a"), P.now("gw")] if m == 0 else []), inc=("t" if m == NCH - 1 else None))
        P.emit("scalar", lambda e: e.copy(pcb[1][:, 0:1024], pT[:, 0:1024]),
               wait=["t", "v", P.now("gw")], inc="a")
        for m in range(NCH):
            P.emit("gpsimd", lambda e, j=j, m=m: e.dma_start(scr["s_q"][j * 128:(j + 1) * 128, m * 128:(m + 1) * 128], pcb[1][:, m * 128:(m + 1) * 128]),
                   wait=["a", "t", "v"], inc="gw", inc_k=16)
    setup_cg = (P.cnt["cg0"], P.cnt["cg1"])
    if dbg == "qrhs":
        for mm_ in range(NCH):
            P.emit("sync", lambda e, mm_=mm_: e.dma_start(pw[0][:, 0:1024], scr["s_q"][mm_ * 128:(mm_ + 1) * 128, :]),
                   wait=[P.now("gw"), ("cg0", P.cnt["cg0"]), ("cg1", P.cnt["cg1"]), P.now("so")], inc="sw", inc_k=16)
            P.emit("sync", lambda e, mm_=mm_: e.dma_start(dbgB[mm_ * 128:(mm_ + 1) * 128, 0:1024], pw[0][:, 0:1024]),
                   wait=["sw"], inc="so", inc_k=16)
        P.emit("sync", lambda e: e.dma_start(dbgF[0:128, 0:8], srhs[:, :]), wait=[P.now("a")], inc="so", inc_k=16)
        P.emit("sync", lambda e: e.dma_start(dbgF[0:128, 8:16], rrv[:, :]), inc="so", inc_k=16)
        P.emit("sync", lambda e: e.dma_start(dbgF[0:128, 16:24], amaxR[:, :]), wait=[P.now("v")], inc="so", inc_k=16)
    setup_gw = P.cnt["gw"]

    SETUP_DONE = [("gw", None), ("cg0", None), ("cg1", None)]

    def setup_done_waits():
        return [("gw", setup_gw), ("cg0", setup_cg[0]), ("cg1", setup_cg[1])]

    def stream_w2(scr_name, buf_pair, pair_idx, ncols, wait_extra, engine="sync", kind="pw"):
        b = pair_idx % 2
        semn = f"{kind}{b}"
        r0 = 2 * pair_idx * 128

        def go(e, scr_name=scr_name, r0=r0, ncols=ncols, b=b):
            src = bass.AP(scr[scr_name], r0 * ncols,
                          [[ncols, 128], [128 * ncols, 2], [1, ncols]])
            dst = bass.AP(buf_pair[b], 0, [[2 * ncols, 128], [ncols, 2], [1, ncols]])
            return e.dma_start(dst, src)
        mk = P.emit(engine, go, wait=setup_done_waits() + list(wait_extra), inc=semn, inc_k=16)
        return buf_pair[b], mk

    def proj(scr_name, rhs_buf, banks, consumer, extra_start=()):
        start_v = P.now("v")
        cons_marks = {}
        tmarks = {}
        bufs = {}
        rdys = {}
        for m in range(NCH):
            if m % 2 == 0:
                pi = m // 2
                wx = [P.now("t")] if pi < 2 else [tmarks[m - 3]]
                bufs[pi], rdys[pi] = stream_w2(scr_name, pw, pi, NCH * 128, wx)
            pi = m // 2
            buf = bufs[pi]
            col0 = (m % 2) * NCH * 128
            waits = [rdys[pi], start_v] + list(extra_start)
            if m >= 2:
                waits.append(cons_marks[m - 2])
            bank = banks[m % 2]
            for j in range(NCH):
                tm = P.emit("tensor", lambda e, j=j, buf=buf, bank=bank, col0=col0:
                            e.matmul(bank[:, 0:TW], buf[:, col0 + j * 128:col0 + (j + 1) * 128], xs(rhs_buf, j),
                                     start=(j == 0), stop=(j == NCH - 1)),
                            wait=(waits if j == 0 else []), inc=("t" if j == NCH - 1 else None))
            tmarks[m] = tm
            cons_marks[m] = consumer(m, bank)

    bankfree = {4: None, 5: None}

    def ln_stats(src):
        cmarks, qmarks = [], []
        wv0, wt0 = P.now("v"), P.now("t")
        for j in range(NCH):
            cmarks.append(P.emit("scalar", lambda e, j=j: e.copy(xs(xkb, j), xs(src, j)),
                                 wait=([wv0, wt0] if j == 0 else []), inc="a"))
        for j in range(NCH):
            qmarks.append(P.emit("vector", lambda e, j=j: e.tensor_tensor(xs(xvb, j), xs(xkb, j), xs(xkb, j), ALU.mult),
                                 wait=([cmarks[j], wt0] if j == 0 else [cmarks[j]]), inc="v"))
        for j in range(NCH):
            w0 = [cmarks[j]] + ([bankfree[4]] if j == 0 and bankfree[4] else [])
            P.emit("tensor", lambda e, j=j: e.matmul(pb[4][:, 0:TW], onesb[:, :], xs(xkb, j), start=(j == 0), stop=(j == NCH - 1)),
                   wait=w0, inc=("t" if j == NCH - 1 else None))
        for j in range(NCH):
            w0 = [qmarks[j]] + ([bankfree[5]] if j == 0 and bankfree[5] else [])
            P.emit("tensor", lambda e, j=j: e.matmul(pb[5][:, 0:TW], onesb[:, :], xs(xvb, j), start=(j == 0), stop=(j == NCH - 1)),
                   wait=w0, inc=("t" if j == NCH - 1 else None))
        bankfree[4] = P.emit("vector", lambda e: e.tensor_scalar(pool[1][:, :], pb[4][:, 0:TW], cap(CI_INVC), None, ALU.mult),
               wait=["t"], inc="v")
        bankfree[5] = P.emit("vector", lambda e: e.tensor_scalar(pool[2][:, :], pb[5][:, 0:TW], cap(CI_INVC), cap(CI_EPS), ALU.mult, ALU.add),
               inc="v")
        P.emit("vector", lambda e: e.scalar_tensor_tensor(pool[3][:, :], pool[1][:, :], cap(CI_NEG1), pool[1][:, :], ALU.mult, ALU.mult),
               inc="v")
        P.emit("vector", lambda e: e.tensor_tensor(pool[2][:, :], pool[2][:, :], pool[3][:, :], ALU.add), inc="v")
        P.emit("scalar", lambda e: e.activation(pool[3][:, :], pool[2][:, :], AF.Ln, bias=0.0, scale=1.0),
               wait=["v"], inc="a")
        P.emit("scalar", lambda e: e.activation(pool[0][:, :], pool[3][:, :], AF.Exp, bias=0.0, scale=-0.5), inc="a")

    def ln_apply(src, dst_fn, wname=None, bname=None):
        for j in range(NCH):
            P.emit("vector", lambda e, j=j: e.tensor_tensor(pool[4][:, :], xs(src, j), pool[1][:, :], ALU.subtract),
                   wait=(["a"] if j == 0 else []), inc="v")
            if wname is None:
                P.emit("vector", lambda e, j=j: e.tensor_tensor(dst_fn(j), pool[4][:, :], pool[0][:, :], ALU.mult), inc="v")
            else:
                P.emit("vector", lambda e, j=j: e.scalar_tensor_tensor(pool[5][:, :], pool[4][:, :], vap(wname, j), pool[0][:, :], ALU.mult, ALU.mult),
                       inc="v")
                P.emit("vector", lambda e, j=j: e.tensor_scalar(dst_fn(j), pool[5][:, :], vap(bname, j), None, ALU.add), inc="v")

    def tap(name, t0, ap_fn, bf=False):
        if dbg != name:
            return
        dst = dbgB if bf else dbgF
        for j in range(NCH):
            P.emit("sync", lambda e, j=j, t0=t0: e.dma_start(dst[j * 128:(j + 1) * 128, t0:t0 + TW], ap_fn(j)),
                   wait=[P.now("v"), P.now("a"), P.now("t")], inc="so", inc_k=16)
        P.emit("vector", lambda e: None, wait=["so"])
        P.emit("scalar", lambda e: None, wait=["so"])
        P.emit("tensor", lambda e: None, wait=["so"])

    P.emit("vector", lambda e: None, wait=setup_done_waits() + [P.now("t"), P.now("a")])
    P.emit("scalar", lambda e: None, wait=setup_done_waits() + [P.now("t"), P.now("v")])
    P.emit("tensor", lambda e: None, wait=setup_done_waits())
    s1_mark = [None]
    for it in range(NT):
        t0 = it * TW
        for j in range(NCH):
            P.emit("sync", lambda e, j=j, t0=t0: e.dma_start(xs(x_in, j), xT[j * 128:(j + 1) * 128, t0:t0 + TW]),
                   wait=([s1_mark[0]] if s1_mark[0] else [P.now("v")]) if j == 0 else [], inc="sx", inc_k=16)
        if it > 0:
            P.emit("scalar", lambda e: e.copy(h1b[0][:, 0:1], h1b[0][:, TW:TW + 1]),
                   wait=[P.now("v")], inc="a")
            for j in range(NCH):
                if j > 0:
                    P.emit("scalar", lambda e, j=j: e.copy(h1b[j][:, 0:1], h1b[j][:, TW:TW + 1]), inc="a")
                P.emit("scalar", lambda e, j=j: e.copy(h2b[j][:, 0:1], h2b[j][:, TW:TW + 1]), inc="a")
                P.emit("scalar", lambda e, j=j: e.copy(ekb[j][:, 0:1], ekb[j][:, TW:TW + 1]), inc="a")
            for j in range(NCH):
                P.emit("vector", lambda e, j=j: e.tensor_copy(zb[j][:, 0:1], zb[j][:, TW:TW + 1]), inc="v")
        P.emit("vector", lambda e: None, wait=["sx"])
        P.emit("scalar", lambda e: None, wait=["sx"])

        ln_stats(x_in)
        ln_apply(x_in, lambda j: xs(xres, j), "ln0_w", "ln0_b")
        tap("x0", t0, lambda j: xs(xres, j))
        ln_stats(xres)
        ln_apply(xres, lambda j: h1b[j][:, 1:TW + 1], "ln1_w", "ln1_b")

        tap("h1", t0, lambda j: h1b[j][:, 1:TW + 1], bf=True)
        for j in range(NCH):
            P.emit("vector", lambda e, j=j: e.tensor_tensor(dl[:, :], h1b[j][:, 1:TW + 1], h1b[j][:, 0:TW], ALU.subtract),
                   wait=(["a", P.now("t")] if j == 0 else ["a"]), inc="v")
            P.emit("vector", lambda e, j=j: e.scalar_tensor_tensor(xs(xkb, j), dl[:, :], vap("tmk", j), h1b[j][:, 0:TW], ALU.mult, ALU.add), inc="v")
            P.emit("vector", lambda e, j=j: e.scalar_tensor_tensor(xs(xvb, j), dl[:, :], vap("tmv", j), h1b[j][:, 0:TW], ALU.mult, ALU.add), inc="v")
            P.emit("vector", lambda e, j=j: e.scalar_tensor_tensor(xs(xrb, j), dl[:, :], vap("tmr", j), h1b[j][:, 0:TW], ALU.mult, ALU.add), inc="v")

        tap("xk", t0, lambda j: xs(xkb, j), bf=True)
        tap("xv", t0, lambda j: xs(xvb, j), bf=True)
        tap("xr", t0, lambda j: xs(xrb, j), bf=True)
        k_last = [None]

        def k_cons(m, bank):
            k_last[0] = P.emit("scalar", lambda e, m=m, bank=bank: e.activation(ekb[m][:, 1:TW + 1], bank[:, 0:TW], AF.Exp, bias=0.0, scale=1.0),
                               wait=["t"], inc="a")
            return k_last[0]
        proj("s_k", xkb, (pb[0], pb[1]), k_cons)

        def v_cons(m, bank):
            return P.emit("vector", lambda e, m=m, bank=bank: e.tensor_tensor(zb[m][:, 1:TW + 1], ekb[m][:, 1:TW + 1], bank[:, 0:TW], ALU.mult),
                          wait=["t", "a"], inc="v")
        proj("s_v", xvb, (pb[2], pb[3]), v_cons)

        tap("ek", t0, lambda j: ekb[j][:, 1:TW + 1])
        tap("z", t0, lambda j: zb[j][:, 1:TW + 1])
        P.emit("vector", lambda e: None, wait=[P.now("so")])
        scan_marks = []
        for j in range(NCH):
            P.emit("vector", lambda e, j=j: e.tensor_tensor_scan(xs(PA, j), bass.AP(dvec, j, [[8, 128], [0, TW]]), zb[j][:, 0:TW], hist[:, 4 * j:4 * j + 1], ALU.mult, ALU.add),
                   wait=(["a"] if j == 0 else []), inc="v")
            P.emit("vector", lambda e, j=j: e.tensor_tensor_scan(xs(PB, j), bass.AP(dvec, j, [[8, 128], [0, TW]]), ekb[j][:, 0:TW], hist[:, 4 * j + 1:4 * j + 2], ALU.mult, ALU.add),
                   inc="v")
            P.emit("vector", lambda e, j=j: e.tensor_copy(hist[:, 4 * j:4 * j + 1], PA[:, j * TW + TW - 1:j * TW + TW]), inc="v")
            mk = P.emit("vector", lambda e, j=j: e.tensor_copy(hist[:, 4 * j + 1:4 * j + 2], PB[:, j * TW + TW - 1:j * TW + TW]), inc="v")
            scan_marks.append(mk)

        tap("PA", t0, lambda j: xs(PA, j))
        tap("PB", t0, lambda j: xs(PB, j))
        def r_cons(m, bank):
            pn = pool[0 + 3 * (m % 2)]
            pd = pool[1 + 3 * (m % 2)]
            ps_ = pool[2 + 3 * (m % 2)]
            nmark = P.emit("vector", lambda e, m=m, pn=pn: e.scalar_tensor_tensor(pn[:, :], zb[m][:, 1:TW + 1], euv[:, m:m + 1], xs(PA, m), ALU.mult, ALU.add),
                           wait=[scan_marks[m]], inc="v")
            dmark = P.emit("vector", lambda e, m=m, pd=pd: e.scalar_tensor_tensor(pd[:, :], ekb[m][:, 1:TW + 1], euv[:, m:m + 1], xs(PB, m), ALU.mult, ALU.add),
                           inc="v")
            P.emit("scalar", lambda e, pd=pd: e.activation(pd[:, :], pd[:, :], AF.Ln, bias=0.0, scale=1.0),
                   wait=[dmark], inc="a")
            P.emit("scalar", lambda e, pd=pd: e.activation(pd[:, :], pd[:, :], AF.Exp, bias=0.0, scale=-1.0), inc="a")
            bankfree[4 + (m % 2)] = P.emit("scalar", lambda e, bank=bank, ps_=ps_: e.activation(ps_[:, :], bank[:, 0:TW], AF.Exp, bias=0.0, scale=-1.0),
                   wait=["t"], inc="a")
            P.emit("scalar", lambda e, ps_=ps_: e.activation(ps_[:, :], ps_[:, :], AF.Ln, bias=1.0, scale=1.0), inc="a")
            amark = P.emit("scalar", lambda e, ps_=ps_: e.activation(ps_[:, :], ps_[:, :], AF.Exp, bias=0.0, scale=-1.0), inc="a")
            P.emit("vector", lambda e, pn=pn, pd=pd: e.tensor_tensor(pn[:, :], pn[:, :], pd[:, :], ALU.mult),
                   wait=[amark], inc="v")
            return P.emit("vector", lambda e, m=m, pn=pn, ps_=ps_: e.tensor_tensor(xs(xkb, m), ps_[:, :], pn[:, :], ALU.mult), inc="v")
        proj("s_r", xrb, (pb[4], pb[5]), r_cons, extra_start=[x for x in (bankfree[4], bankfree[5]) if x])

        tap("rwkvin", t0, lambda j: xs(xkb, j), bf=True)
        def o_cons(m, bank):
            return P.emit("vector", lambda e, m=m, bank=bank: e.tensor_tensor(xs(xres, m), xs(xres, m), bank[:, 0:TW], ALU.add),
                          wait=["t"], inc="v")
        proj("s_o", xkb, (pb[0], pb[1]), o_cons, extra_start=[k_last[0]])

        tap("x1", t0, lambda j: xs(xres, j))
        ln_stats(xres)
        ln_apply(xres, lambda j: h2b[j][:, 1:TW + 1], "ln2_w", "ln2_b")

        for j in range(NCH):
            P.emit("vector", lambda e, j=j: e.tensor_tensor(dl[:, :], h2b[j][:, 1:TW + 1], h2b[j][:, 0:TW], ALU.subtract),
                   wait=(["a", P.now("t")] if j == 0 else ["a"]), inc="v")
            P.emit("vector", lambda e, j=j: e.scalar_tensor_tensor(xs(xkb, j), dl[:, :], vap("cmk", j), h2b[j][:, 0:TW], ALU.mult, ALU.add), inc="v")
            P.emit("vector", lambda e, j=j: e.scalar_tensor_tensor(xs(xrb, j), dl[:, :], vap("cmr", j), h2b[j][:, 0:TW], ALU.mult, ALU.add), inc="v")

        def fr_cons(m, bank):
            P.emit("scalar", lambda e, bank=bank: e.activation(pool[2][:, :], bank[:, 0:TW], AF.Exp, bias=0.0, scale=-1.0),
                   wait=["t", P.now("v")], inc="a")
            P.emit("scalar", lambda e: e.activation(pool[2][:, :], pool[2][:, :], AF.Ln, bias=1.0, scale=1.0), inc="a")
            return P.emit("scalar", lambda e, m=m: e.activation(xs(sig2, m), pool[2][:, :], AF.Exp, bias=0.0, scale=-1.0), inc="a")
        proj("s_fr", xrb, (pb[2], pb[3]), fr_cons)

        kftt_marks = {}
        fk_tmarks = {}
        fk_bufs = {}
        fk_rdys = {}
        for m in range(NHC):
            if m % 2 == 0:
                pi = m // 2
                wx = [P.now("t")] if pi < 2 else [fk_tmarks[m - 3]]
                fk_bufs[pi], fk_rdys[pi] = stream_w2("s_fk", fkw, pi, NCH * 128, wx, engine="gpsimd", kind="fk")
            pi = m // 2
            buf = fk_bufs[pi]
            col0 = (m % 2) * NCH * 128
            bank = pb[m % 2]
            waits = [fk_rdys[pi]]
            if m < 2:
                waits += [P.now("v"), P.now("a")]
            else:
                waits.append(kftt_marks[m - 2])
            for j in range(NCH):
                tm = P.emit("tensor", lambda e, j=j, buf=buf, bank=bank, col0=col0:
                            e.matmul(bank[:, 0:TW], buf[:, col0 + j * 128:col0 + (j + 1) * 128], xs(xkb, j),
                                     start=(j == 0), stop=(j == NCH - 1)),
                            wait=(waits if j == 0 else []), inc=("t" if j == NCH - 1 else None))
            fk_tmarks[m] = tm
            rslot = pool[2 + (m % 2)]
            P.emit("scalar", lambda e, bank=bank, rslot=rslot: e.activation(rslot[:, :], bank[:, 0:TW], AF.Relu, bias=0.0, scale=1.0),
                   wait=(["t"] if m < 2 else ["t", kftt_marks[m - 2]]), inc="a")
            kftt_marks[m] = P.emit("vector", lambda e, m=m, bank=bank, rslot=rslot:
                                   e.tensor_tensor(kf[:, m * TW:(m + 1) * TW], bank[:, 0:TW], rslot[:, :], ALU.mult),
                                   wait=["a"], inc="v")
        kv_marks = {}
        fv_submarks = {}
        for m in range(NCH):
            bank = pb[2 + (m % 2)]
            waits0 = [P.now("v")]
            if m >= 2:
                waits0.append(kv_marks[m - 2])
            hrdy = {}
            for h in range(2):
                wx = [P.now("t")] if m == 0 else [fv_submarks[(m - 1, h)]]

                def go(e, m=m, h=h):
                    src = bass.AP(scr["s_fv"], (m * 128) * H + h * 2048, [[H, 128], [1, 2048]])
                    return e.dma_start(fvw[h][:, 0:2048], src)
                hrdy[h] = P.emit("sync", go, wait=setup_done_waits() + wx, inc=f"fv{h}", inc_k=16)
            for o in range(NHC):
                h = o // 16
                if o == 0:
                    ww = waits0 + [hrdy[0]]
                elif o == 16:
                    ww = [hrdy[1]]
                else:
                    ww = []
                tm = P.emit("tensor", lambda e, o=o, bank=bank, h=h:
                            e.matmul(bank[:, 0:TW], fvw[h][:, (o % 16) * 128:(o % 16 + 1) * 128], kf[:, o * TW:(o + 1) * TW],
                                     start=(o == 0), stop=(o == NHC - 1)),
                            wait=ww, inc=("t" if o in (15, NHC - 1) else None))
                if o in (15, NHC - 1):
                    fv_submarks[(m, o // 16)] = tm
            tslot = pool[4 + (m % 2)]
            P.emit("vector", lambda e, m=m, bank=bank, tslot=tslot:
                   e.tensor_tensor(tslot[:, :], xs(sig2, m), bank[:, 0:TW], ALU.mult),
                   wait=["t"], inc="v")
            kv_marks[m] = P.emit("vector", lambda e, m=m, tslot=tslot:
                                 e.tensor_tensor(xs(xres, m), xs(xres, m), tslot[:, :], ALU.add),
                                 inc="v")

        for j in range(NCH):
            s1_mark[0] = P.emit("vector", lambda e, j=j: e.tensor_tensor(xs(PB, j), xs(x_in, j), xs(xres, j), ALU.add), inc="v")

        ln_stats(PB)
        ln_apply(PB, lambda j: xs(PA, j))

        tap("y1", t0, lambda j: xs(PA, j))
        for g in range(4):
            for j in range(NCH):
                P.emit("tensor", lambda e, g=g, j=j: e.transpose(pT[:, j * 128:(j + 1) * 128], PA[:, j * TW + g * 128:j * TW + (g + 1) * 128], idf[:, :]),
                       wait=([P.now("v")] if j == 0 else []), inc=("t" if j == NCH - 1 else None))
            P.emit("vector", lambda e, g=g: e.tensor_reduce(am4[:, g:g + 1], pT[:, 0:1024], AX.X, ALU.max, apply_absolute_value=True),
                   wait=["t"], inc="v")
        P.emit("scalar", lambda e: e.activation(amaxR[:, 0:4], am4[:, 0:4], AF.Ln, bias=0.0, scale=1.0),
               wait=["v"], inc="a")
        P.emit("scalar", lambda e: e.activation(am4[:, 4:8], amaxR[:, 0:4], AF.Exp, bias=cap(CI_LN127), scale=-1.0), inc="a")
        P.emit("scalar", lambda e: e.activation(am4[:, 0:4], amaxR[:, 0:4], AF.Exp, bias=cap(CI_NLN127), scale=1.0), inc="a")
        P.emit("vector", lambda e: None, wait=["a"])
        P.emit("tensor", lambda e: e.transpose(pT[0:8, 0:128], am4[:, 0:8], idf[:, :]), wait=["v", "a"], inc="t")
        P.emit("vector", lambda e: e.tensor_copy(prT[:, :], pT[0:8, 0:128]), wait=["t"], inc="v")
        for g in range(4):
            P.emit("tensor", lambda e, g=g: e.matmul(pT[:, g * 128:(g + 1) * 128], self8[:, (4 + g) * 128:(5 + g) * 128], prT[:, :], start=True, stop=True),
                   wait=(["v"] if g == 0 else []), inc=None)
            P.emit("tensor", lambda e, g=g: e.matmul(pT[:, TW + g * 128:TW + (g + 1) * 128], self8[:, g * 128:(g + 1) * 128], prT[:, :], start=True, stop=True),
                   inc=("t" if g == 3 else None))
        P.emit("vector", lambda e: e.tensor_copy(pool[0][:, :], pT[:, TW:2 * TW]), wait=["t"], inc="v")
        if dbg in ("rinvbc", "am4"):
            if dbg == "rinvbc":
                P.emit("vector", lambda e: e.tensor_copy(pool[2][:, :], pT[:, 0:TW]), wait=["t"], inc="v")
            else:
                P.emit("vector", lambda e: e.tensor_copy(pool[3][:, 0:8], am4[:, :]), inc="v")
                P.emit("vector", lambda e: e.tensor_copy(pool[3][:, 8:16], amaxR[:, 0:8]), inc="v")
                P.emit("vector", lambda e: e.tensor_copy(pool[3][:, 16:24], pool[2][:, 0:8]), inc="v")
            P.emit("sync", lambda e, t0=t0, _src=(2 if dbg == "rinvbc" else 3): e.dma_start(dbgF[0:128, t0:t0 + TW], pool[_src][:, :]),
                   wait=[P.now("v")], inc="so", inc_k=16)
            P.emit("vector", lambda e: None, wait=["so"])
        qcast_mark = None
        for j in range(NCH):
            ww = [P.now("t")] if j == 0 else [qcast_mark]
            P.emit("vector", lambda e, j=j: e.tensor_tensor(qi8[:, 0:TW], xs(PA, j), pT[:, 0:TW], ALU.mult),
                   wait=ww, inc="v")
            qcast_mark = P.emit("scalar", lambda e, j=j: e.copy(xs(xkb, j), qi8[:, 0:TW]), wait=["v"], inc="a")

        tap("qbf", t0, lambda j: xs(xkb, j), bf=True)
        def ffn_cons(m, bank):
            P.emit("vector", lambda e, m=m, bank=bank: e.scalar_tensor_tensor(pool[3][:, :], bank[:, 0:TW], srhs[:, m:m + 1], pool[0][:, :], ALU.mult, ALU.mult),
                   wait=["t"], inc="v")
            return P.emit("vector", lambda e, m=m: e.tensor_tensor(xs(xres, m), xs(PA, m), pool[3][:, :], ALU.add), inc="v")
        proj("s_q", xkb, (pb[0], pb[1]), ffn_cons, extra_start=[qcast_mark])

        tap("s2", t0, lambda j: xs(xres, j))
        ln_stats(xres)
        for j in range(NCH):
            P.emit("vector", lambda e, j=j: e.tensor_tensor(pool[4][:, :], xs(xres, j), pool[1][:, :], ALU.subtract),
                   wait=(["a"] if j == 0 else []), inc="v")
            P.emit("vector", lambda e, j=j: e.tensor_tensor(xs(PA, j), pool[4][:, :], pool[0][:, :], ALU.mult), inc="v")
            P.emit("sync", lambda e, j=j, t0=t0: e.dma_start(outT[j * 128:(j + 1) * 128, t0:t0 + TW], xs(PA, j)),
                   wait=["v"], inc="so", inc_k=16)

    P.emit("sync", lambda e: None, wait=["so"])

    with nc.Block() as block:
        @block.sync
        def _(e):
            P.run(e, "sync", sems)

        @block.gpsimd
        def _(e):
            P.run(e, "gpsimd", sems)

        @block.tensor
        def _(e):
            P.run(e, "tensor", sems)

        @block.vector
        def _(e):
            P.run(e, "vector", sems)

        @block.scalar
        def _(e):
            P.run(e, "scalar", sems)

    return nc, ctx


def kernel(**inputs):
    per_core = host_prep(inputs)
    nc, ctx = build_program()
    res = run_bass_kernel_spmd(nc, per_core, core_ids=list(range(8)))
    outs = [np.ascontiguousarray(r["outT"].T) for r in res.results]
    return np.stack(outs, axis=0).astype(np.float32)
